# version 14
# baseline (speedup 1.0000x reference)
import numpy as np
import concourse.bass as bass
import concourse.mybir as mybir
from concourse.bass_utils import run_bass_kernel_spmd

F32 = mybir.dt.float32
BF16 = mybir.dt.bfloat16
I32 = mybir.dt.int32
AF = mybir.ActivationFunctionType
ALU = mybir.AluOpType
AXX = mybir.AxisListType.X

D = 1024
NH = 16
HD = 64
HID = 3072
NP = 8
NC = 8
NHC = HID // 128
Q127 = float(1.0 / 127.0)

ENGS = ("pe", "dv", "ac", "sy", "gp")


class Plan:
    def __init__(self, sems):
        self.sems = sems
        self.cnt = {k: 0 for k in sems}
        self.cmds = {e: [] for e in ENGS}
        self.last_wait = {e: {} for e in ENGS}

    def emit(self, eng, fn, waits=(), inc=None, amt=None):
        merged = {}
        for w in waits:
            if w is None:
                continue
            s, v = w
            if v > merged.get(s, 0):
                merged[s] = v
        nw = []
        lw = self.last_wait[eng]
        for s, v in merged.items():
            if v <= 0 or lw.get(s, 0) >= v:
                continue
            lw[s] = v
            nw.append((s, v))
        a = None
        if inc is not None:
            a = amt if amt is not None else (16 if inc not in ("pe", "dv", "ac") else 1)
            self.cnt[inc] += a
        ev = (inc, self.cnt[inc]) if inc is not None else None
        sems = self.sems

        def closure(eng_obj, fn=fn, nw=tuple(nw), inc=inc, a=a):
            for s, v in nw:
                eng_obj.wait_ge(sems[s], v)
            r = fn(eng_obj)
            if inc is not None:
                r.then_inc(sems[inc], a)
        self.cmds[eng].append(closure)
        return ev

    def barrier(self):
        return [(s, self.cnt[s]) for s in self.sems]


def build_program(NTOT=8192, NOWN=4096, TT=512, QT=512):
    nc = bass.Bass("TRN2", target_bir_lowering=False)
    NTT = NTOT // TT
    NTTO = NOWN // TT
    NQT = NOWN // QT
    NCH = NTOT // 128
    NFT = NOWN // 512

    x_fm = nc.dram_tensor("x_fm", [D, NTOT], F32, kind="ExternalInput").ap()
    w_qkv = nc.dram_tensor("w_qkv", [D, 3 * D], F32, kind="ExternalInput").ap()
    w_proj = nc.dram_tensor("w_proj", [D, D], F32, kind="ExternalInput").ap()
    b_projt = nc.dram_tensor("b_projt", [128, NC], F32, kind="ExternalInput").ap()
    qs2x_in = nc.dram_tensor("qs2x", [128, 1], F32, kind="ExternalInput").ap()
    ks2x_in = nc.dram_tensor("ks2x", [128, 1], F32, kind="ExternalInput").ap()
    wgT = nc.dram_tensor("wgT", [D, HID], I32, kind="ExternalInput").ap()
    wuT = nc.dram_tensor("wuT", [D, HID], I32, kind="ExternalInput").ap()
    wdT = nc.dram_tensor("wdT", [HID, D], I32, kind="ExternalInput").ap()
    ropec_in = nc.dram_tensor("ropec", [128, NTOT], F32, kind="ExternalInput").ap()
    ropes_in = nc.dram_tensor("ropes", [128, NTOT], F32, kind="ExternalInput").ap()
    bc2_in = nc.dram_tensor("bc2c", [2, 128], BF16, kind="ExternalInput").ap()

    out_fm = nc.dram_tensor("out_fm", [D, NOWN], F32, kind="ExternalOutput").ap()

    wg_bf = nc.dram_tensor("wg_bf", [NHC, 128, NC * 128], BF16, kind="Internal").ap()
    wu_bf = nc.dram_tensor("wu_bf", [NHC, 128, NC * 128], BF16, kind="Internal").ap()
    wd_bf = nc.dram_tensor("wd_bf", [NC, 128, HID], BF16, kind="Internal").ap()
    xn_scr = nc.dram_tensor("xn_scr", [D, NTOT], BF16, kind="Internal").ap()
    wqkv_scr = nc.dram_tensor("wqkv_scr", [NC, 128, 3 * D], BF16, kind="Internal").ap()
    kt_scr = nc.dram_tensor("kt_scr", [NP, 128, NTOT], BF16, kind="Internal").ap()
    q_scr = nc.dram_tensor("q_scr", [NP, 128, NOWN], BF16, kind="Internal").ap()
    v_scr = nc.dram_tensor("v_scr", [NP, 128, NTOT], BF16, kind="Internal").ap()
    o_scr = nc.dram_tensor("o_scr", [NP, 128, NOWN], BF16, kind="Internal").ap()
    g_scr = nc.dram_tensor("g_scr", [NC, 128, NOWN], BF16, kind="Internal").ap()
    cflip = nc.dram_tensor("cflip", [NP, 2], F32, kind="Internal").ap()

    from contextlib import ExitStack
    with ExitStack() as es:
        E = es.enter_context
        zeros = E(nc.sbuf_tensor("zeros", [128, 512], BF16))
        onesb = E(nc.sbuf_tensor("onesb", [128, 4], BF16))
        onesf = E(nc.sbuf_tensor("onesf", [128, 128], F32))
        ones1b = E(nc.sbuf_tensor("ones1b", [128, 1], BF16))
        onesfh = E(nc.sbuf_tensor("onesfh", [128, 64], BF16))
        bones2 = E(nc.sbuf_tensor("bones2", [128, 2], BF16))
        bc2 = E(nc.sbuf_tensor("bc2", [2, 128], BF16))
        eps5 = E(nc.sbuf_tensor("eps5", [1, 1], F32))
        epsz = E(nc.sbuf_tensor("epsz", [1, 1], F32))
        bln64 = E(nc.sbuf_tensor("bln64", [1, 1], F32))
        eps6 = E(nc.sbuf_tensor("eps6", [2, 1], F32))
        onebias = E(nc.sbuf_tensor("onebias", [128, 1], F32))
        qs2x = E(nc.sbuf_tensor("qs2xt", [128, 1], F32))
        ks2x = E(nc.sbuf_tensor("ks2xt", [128, 1], F32))
        qse = E(nc.sbuf_tensor("qse", [128, 1], F32))
        bpt = E(nc.sbuf_tensor("bpt", [128, NC], F32))
        cbias = E(nc.sbuf_tensor("cbias", [128, NP], F32))
        kmax = E(nc.sbuf_tensor("kmax", [2, NP], F32))
        qmax = E(nc.sbuf_tensor("qmax", [2, NP], F32))
        crow = E(nc.sbuf_tensor("crow", [1, 2 * NP], F32))
        crow2 = E(nc.sbuf_tensor("crow2", [1, NP], F32))
        m_row = E(nc.sbuf_tensor("m_row", [1, TT], F32))
        msq = E(nc.sbuf_tensor("msq", [1, TT], F32))
        v_row = E(nc.sbuf_tensor("v_row", [1, TT], F32))
        lnv = E(nc.sbuf_tensor("lnv", [1, TT], F32))
        a_row = E(nc.sbuf_tensor("a_row", [1, TT], F32))
        b_row = E(nc.sbuf_tensor("b_row", [1, TT], F32))
        ln1 = E(nc.sbuf_tensor("ln1", [2, TT], F32))
        rstdf = E(nc.sbuf_tensor("rstdf", [2, TT], F32))
        rstdb = E(nc.sbuf_tensor("rstdb", [2, TT], BF16))
        redm = E(nc.sbuf_tensor("redm", [2, 1], F32))
        zt = E(nc.sbuf_tensor("zt", [128, QT], F32))
        zc1 = E(nc.sbuf_tensor("zc1", [1, QT], F32))
        zc2 = E(nc.sbuf_tensor("zc2", [1, QT], F32))
        zc3 = E(nc.sbuf_tensor("zc3", [1, QT], F32))
        zrow = E(nc.sbuf_tensor("zrow", [1, 2 * QT], F32))
        zln = E(nc.sbuf_tensor("zln", [1, 2 * QT], F32))
        zi = E(nc.sbuf_tensor("zi", [1, 2 * QT], F32))
        zib = E(nc.sbuf_tensor("zib", [1, 2 * QT], BF16))
        zbs = E(nc.sbuf_tensor("zbs", [128, QT], BF16))
        oraw = [E(nc.sbuf_tensor("oraw0", [128, QT], F32)),
                E(nc.sbuf_tensor("oraw1", [128, QT], F32))]

        ABIG = 82624
        big = E(nc.sbuf_tensor("big", [128, ABIG], BF16))

        class Arena:
            def __init__(self, start=0):
                self.off = start

            def a(self, cols, dt=BF16):
                n = cols if dt == BF16 else 2 * cols
                self.off = (self.off + 15) // 16 * 16
                o = self.off
                self.off += n
                assert self.off <= ABIG, f"arena overflow {self.off}"
                sl = big[:, o:o + n]
                return sl.bitcast(dt) if dt != BF16 else sl

        ar4 = Arena()
        kb = ar4.a(NTOT)
        vb = ar4.a(NTOT)
        qb = ar4.a(NOWN)
        pring = [ar4.a(2048) for _ in range(3)]
        ost = ar4.a(QT)
        assert ar4.off <= 27200, ar4.off
        ar3 = Arena(27200)
        xnf = [[ar3.a(512) for _ in range(NC)] for _ in range(2)]
        wgb = [ar3.a(NC * 128) for _ in range(2)]
        wub = [ar3.a(NC * 128) for _ in range(2)]
        hbuf = [ar3.a(512) for _ in range(NHC)]
        sgate = ar3.a(512, F32)
        sg1 = ar3.a(512, F32)
        gst = ar3.a(512)
        assert ar3.off <= 54400, ar3.off
        arp = Arena(54400)
        xnp = [[arp.a(TT) for _ in range(NC)] for _ in range(2)]
        wqp = [[arp.a(384) for _ in range(NC)] for _ in range(2)]
        ropec = [arp.a(TT, F32) for _ in range(2)]
        ropes = [arp.a(TT, F32) for _ in range(2)]
        kraw = arp.a(TT, F32)
        usc = arp.a(TT, F32)
        urot = arp.a(TT, F32)
        t1r = arp.a(TT, F32)
        tsr = arp.a(TT, F32)
        sqt = arp.a(TT)
        kqst = arp.a(TT)
        vst = arp.a(TT)
        assert arp.off <= ABIG, arp.off
        arl = Arena(0)
        xf = [[arl.a(TT, F32) for _ in range(NC)] for _ in range(2)]
        xb = arl.a(NC * TT)
        xsq = arl.a(NC * TT)
        xnt = arl.a(NC * TT)
        lnt = arl.a(TT, F32)
        assert arl.off <= 31000, arl.off
        stg = [big[:, 31008:31008 + 6144].bitcast(I32),
               big[:, 37152:37152 + 6144].bitcast(I32)]
        stgf = [big[:, 31008:31008 + 6144].bitcast(F32),
                big[:, 37152:37152 + 6144].bitcast(F32)]
        bst = [big[:, 43296:43296 + 3072], big[:, 46368:46368 + 3072]]
        ar5 = Arena(0)
        ob = [ar5.a(NOWN) for _ in range(NC)]
        wpb = [ar5.a(D) for _ in range(NC)]
        sfm = [ar5.a(NOWN) for _ in range(NC)]
        gcb = [ar5.a(512) for _ in range(2)]
        xcb = [ar5.a(512, F32) for _ in range(2)]
        sgc = ar5.a(512)
        otc = [ar5.a(512, F32) for _ in range(2)]
        assert ar5.off <= ABIG, ar5.off

        ps_s = E(nc.psum_tensor("ps_s", [128, 2048], F32))
        ps_pv = E(nc.psum_tensor("ps_pv", [128, 512], F32))
        ps_z = E(nc.psum_tensor("ps_z", [128, 512], F32))
        ps_a = E(nc.psum_tensor("ps_a", [128, 512], F32))
        ps_b = E(nc.psum_tensor("ps_b", [128, 512], F32))
        SBANK = {"s0": ps_s[:, 0:512], "s1": ps_s[:, 512:1024],
                 "s2": ps_s[:, 1024:1536], "s3": ps_s[:, 1536:2048]}

        DMA_SEMS = ("scst", "swq0", "swq1", "sxf0", "sxf1", "sxn0", "sxn1",
                    "swg0", "swg1", "swu0", "swu1", "sob", "swp0", "swp1",
                    "skv0", "skv1", "sge0", "sge1", "sxe0", "sxe1", "sxp0", "sxp1",
                    "sout0", "sout1", "gkq", "gv", "gxn", "gbst0", "gbst1",
                    "gg", "go", "gcf")
        sems = {s: E(nc.semaphore(s)) for s in ("pe", "dv", "ac") + DMA_SEMS}
        P = Plan(sems)

        def dma(q, sem, dst, src, waits=()):
            return P.emit(q, lambda e, d=dst, s=src: e.dma_start(out=d, in_=s),
                          waits=waits, inc=sem)

        def mm(out, lhsT, rhs, start, stop, tp=None, waits=(), inc=None):
            def f(e, o=out, l=lhsT, r=rhs, st=start, sp=stop, tp=tp):
                return e.matmul(o, l, r, start=st, stop=sp, tile_position=tp)
            return P.emit("pe", f, waits=waits, inc=inc)

        def dv(fn, waits=()):
            return P.emit("dv", fn, waits=waits, inc="dv")

        def ac(fn, waits=()):
            return P.emit("ac", fn, waits=waits, inc="ac")

        bank = {k: {"w": None, "r": None} for k in
                ("s0", "s1", "s2", "s3", "pv", "z", "a", "b")}

        dv(lambda e: e.memset(zeros[:, :], 0.0))
        dv(lambda e: e.memset(onesb[:, :], 1.0))
        dv(lambda e: e.memset(onesf[:, :], 1.0))
        dv(lambda e: e.memset(ones1b[:, :], 1.0))
        dv(lambda e: e.memset(onesfh[:, :], 1.0))
        dv(lambda e: e.memset(bones2[0:64, 0:1], 1.0))
        dv(lambda e: e.memset(bones2[64:128, 0:1], 0.0))
        dv(lambda e: e.memset(bones2[0:64, 1:2], 0.0))
        dv(lambda e: e.memset(bones2[64:128, 1:2], 1.0))
        pass
        dv(lambda e: e.memset(eps5[:, :], 1e-5))
        dv(lambda e: e.memset(epsz[:, :], 1e-38))
        dv(lambda e: e.memset(bln64[:, :], -44.3614195558365))
        dv(lambda e: e.memset(eps6[:, :], 1e-6))
        dv(lambda e: e.memset(onebias[:, :], 1.0))
        dv(lambda e: e.memset(kmax[:, :], 0.0))
        ev_con = dv(lambda e: e.memset(qmax[:, :], 0.0))
        dma("sy", "scst", qs2x[:, :], qs2x_in)
        dma("sy", "scst", ks2x[:, :], ks2x_in)
        dma("sy", "scst", bpt[:, :], b_projt)
        e_cst = dma("sy", "scst", bc2[:, :], bc2_in)
        e1 = e2 = e3 = e_cst
        ev_qse = dv(lambda e: e.tensor_scalar(out=qse[:, :], in0=qs2x[:, :],
                                              scalar1=0.125, scalar2=None,
                                              op0=ALU.mult), waits=[e1])

        ev_stg = [None, None]
        ev_bst0 = [None, None]
        for c in range(NC):
            sl = c % 2
            el = dma("sy", "swq%d" % sl, stgf[sl][:, 0:3 * D],
                     w_qkv[c * 128:(c + 1) * 128, :], waits=[ev_stg[sl]])
            ec = dv(lambda e, sl=sl: e.tensor_copy(
                out=bst[sl][:, 0:3072], in_=stgf[sl][:, 0:3 * D]),
                waits=[el, ev_bst0[sl]])
            ev_stg[sl] = ec
            ev_bst0[sl] = dma("gp", "gbst%d" % sl, wqkv_scr[c], bst[sl][:, 0:3072],
                              waits=[ec])
        ev_wq_all = ev_bst0[(NC - 1) % 2]

        ev_bst = list(ev_bst0)
        wlist = []
        for hh in range(NHC):
            wlist.append((D, wgT[:, hh * 128:(hh + 1) * 128], wg_bf[hh]))
            wlist.append((D, wuT[:, hh * 128:(hh + 1) * 128], wu_bf[hh]))
        for dd in range(NC):
            wlist.append((HID, wdT[:, dd * 128:(dd + 1) * 128], wd_bf[dd]))
        slot = 0
        ev_deq = {}
        ev_stg_free = [ev_stg[0], ev_stg[1]]
        for i, (rows, src, dst) in enumerate(wlist):
            ncols = rows
            it = stg[slot][:, 0:ncols]
            bt = bst[slot][:, 0:ncols]
            s3 = src.rearrange("(c p) h -> p c h", p=128)
            i3 = it.rearrange("p (c h) -> p c h", h=128)
            el = dma("sy", "swq%d" % slot, i3, s3,
                     waits=[ev_stg_free[slot], ev_wq_all])
            ec = dv(lambda e, it=it, bt=bt: e.tensor_copy(out=bt, in_=it),
                    waits=[el, ev_bst[slot]])
            ev_stg_free[slot] = ec
            ev_bst[slot] = dma("gp", "gbst%d" % slot, dst, bt, waits=[ec])
            ev_deq[i] = ev_bst[slot]
            slot ^= 1
        ev_deq_all = P.barrier()

        ev_pair = [None] * NP
        ev_xnspill = [None] * NTT
        bar_mid = None

        def prep_pairs():
            ev_xnp = [None, None]
            ev_xnp_free = [None, None]
            ev_wqp_free = [None, None]
            ev_wqp = [None, None]
            ev_kraw_free = None
            ev_sq_free = None
            ev_usc_free = None
            ev_urot_free = None
            ev_t1_free = None
            ev_tsr_free = None
            ev_kq_free = None
            ev_v_free = None
            ev_rope_free = [None, None]
            jc = 0
            for p in range(NP):
                ps_ = p % 2
                wq_w = [ev_wqp_free[ps_], ev_wq_all]
                evw = None
                for c in range(NC):
                    for j, coff in enumerate((1024, 0, 2048)):
                        evw = dma("sy", "swq%d" % ps_,
                                  wqp[ps_][c][:, j * 128:(j + 1) * 128],
                                  wqkv_scr[c][:, coff + p * 128:coff + (p + 1) * 128],
                                  waits=wq_w if (c == 0 and j == 0) else ())
                ev_wqp[ps_] = evw
                yield
                ev_maxk = None
                ev_maxq = None
                for tt in range(NTT):
                    sl = tt % 2
                    own = tt < NTTO
                    t0 = tt * TT
                    while ev_xnspill[tt] is None:
                        yield
                    wxx = [ev_xnp_free[sl], ev_xnspill[tt]]
                    evx = None
                    for c in range(NC):
                        evx = dma("sy", "sxp%d" % sl, xnp[sl][c][:, :],
                                  xn_scr[c * 128:(c + 1) * 128, t0:t0 + TT],
                                  waits=wxx if c == 0 else ())
                    er1 = dma("sy", "sxp%d" % sl, ropec[sl][:, :],
                              ropec_in[:, t0:t0 + TT],
                              waits=[ev_rope_free[sl]])
                    er2 = dma("sy", "sxp%d" % sl, ropes[sl][:, :],
                              ropes_in[:, t0:t0 + TT])
                    evx = er2
                    er1 = er2
                    ev_rope_rd = None
                    yield
                    jobs = [("k", ks2x, 0, kt_scr[p])]
                    if own:
                        jobs.append(("q", qse, 1, q_scr[p]))
                    for kind, scv, joff, dstt in jobs:
                        psk = ps_a
                        bname = "a"
                        jc += 1
                        evk = None
                        for c in range(NC):
                            w0 = ([ev_wqp[ps_], evx, bank[bname]["r"]]
                                  if c == 0 else ())
                            evk = mm(psk[:, 0:TT],
                                     wqp[ps_][c][:, joff * 128:(joff + 1) * 128],
                                     xnp[sl][c][:, :], c == 0, c == NC - 1,
                                     waits=w0, inc="pe" if c == NC - 1 else None)
                        bank[bname]["w"] = evk
                        ekraw = dv(lambda e, b=psk: e.tensor_copy(
                            out=kraw[:, :], in_=b[:, 0:TT]),
                            waits=[evk, ev_kraw_free])
                        esq = dv(lambda e: e.tensor_tensor(out=sqt[:, :],
                                                           in0=kraw[:, :],
                                                           in1=kraw[:, :],
                                                           op=ALU.mult),
                                 waits=[ekraw, ev_sq_free])
                        yield
                        evn = mm(psk[0:2, 0:TT], bones2[:, 0:2], sqt[:, :], True,
                                 True, waits=[esq, ekraw], inc="pe")
                        bank[bname]["w"] = evn
                        elr = ac(lambda e, b=psk: e.activation(
                            out=ln1[:, :], in_=b[0:2, 0:TT], func=AF.Ln,
                            bias=eps6[:, 0:1], scale=1.0 / HD), waits=[evn])
                        err = ac(lambda e: e.activation(out=rstdf[:, :],
                                                        in_=ln1[:, :],
                                                        func=AF.Exp, scale=-0.5),
                                 waits=[elr])
                        erb = dv(lambda e: e.tensor_copy(out=rstdb[:, :],
                                                         in_=rstdf[:, :]),
                                 waits=[err])
                        yield
                        evrb = mm(psk[:, 0:TT], bc2[:, :], rstdb[:, :], True, True,
                                  waits=[erb, elr], inc="pe")
                        bank[bname]["w"] = evrb
                        eu = dv(lambda e, s=scv: e.tensor_scalar(
                            out=usc[:, :], in0=kraw[:, :], scalar1=s[:, 0:1],
                            scalar2=None, op0=ALU.mult),
                            waits=[ekraw, ev_usc_free, ev_qse])
                        ev_kraw_free = eu
                        ec1 = dv(lambda e: e.tensor_copy(out=urot[0:32, :],
                                                         in_=usc[32:64, :]),
                                 waits=[eu, ev_urot_free])
                        ec2 = dv(lambda e: e.tensor_copy(out=urot[32:64, :],
                                                         in_=usc[0:32, :]),
                                 waits=[eu])
                        ec3 = dv(lambda e: e.tensor_copy(out=urot[64:96, :],
                                                         in_=usc[96:128, :]),
                                 waits=[eu])
                        ec4 = dv(lambda e: e.tensor_copy(out=urot[96:128, :],
                                                         in_=usc[64:96, :]),
                                 waits=[eu])
                        et1 = dv(lambda e, sl=sl: e.tensor_tensor(
                            out=t1r[:, :], in0=usc[:, :], in1=ropec[sl][:, :],
                            op=ALU.mult), waits=[eu, er1, ev_t1_free])
                        et2 = dv(lambda e, sl=sl: e.tensor_tensor(
                            out=urot[:, :], in0=urot[:, :], in1=ropes[sl][:, :],
                            op=ALU.mult), waits=[ec1, ec2, ec3, ec4, er2])
                        ets = dv(lambda e: e.tensor_tensor(out=tsr[:, :],
                                                           in0=t1r[:, :],
                                                           in1=urot[:, :],
                                                           op=ALU.add),
                                 waits=[et1, et2, ev_tsr_free])
                        ev_rope_rd = ets
                        ev_usc_free = ets
                        ev_urot_free = ets
                        ev_t1_free = ets
                        ekf = dv(lambda e, b=psk: e.tensor_tensor(
                            out=kqst[:, :], in0=tsr[:, :], in1=b[:, 0:TT],
                            op=ALU.mult), waits=[ets, evrb, ev_kq_free])
                        ev_tsr_free = ekf
                        esq2 = dv(lambda e: e.tensor_tensor(out=sqt[:, :],
                                                            in0=kqst[:, :],
                                                            in1=kqst[:, :],
                                                            op=ALU.mult),
                                  waits=[ekf, evn])
                        ev_sq_free = esq2
                        yield
                        evn2 = mm(psk[0:2, 0:TT], bones2[:, 0:2], sqt[:, :], True,
                                  True, waits=[esq2, ekf], inc="pe")
                        bank[bname]["w"] = evn2
                        evre = dv(lambda e, b=psk: e.reduce_max(
                            out=redm[:, 0:1], in_=b[0:2, 0:TT], axis=AXX),
                            waits=[evn2])
                        bank[bname]["r"] = evre
                        mx = kmax if kind == "k" else qmax
                        evmx = dv(lambda e, m=mx, p=p: e.tensor_tensor(
                            out=m[:, p:p + 1], in0=m[:, p:p + 1],
                            in1=redm[:, 0:1], op=ALU.max), waits=[evre])
                        if kind == "k":
                            ev_maxk = evmx
                        else:
                            ev_maxq = evmx
                        ev_kq_free = dma("gp", "gkq", dstt[:, t0:t0 + TT],
                                         kqst[:, :], waits=[ekf])
                        yield
                    psk = ps_a
                    bname = "a"
                    jc += 1
                    mm(psk[:, 0:512], zeros[:, 0:128], zeros[:, 0:512], True, False,
                       waits=[evx, ev_wqp[ps_], bank[bname]["r"]])
                    for tc in range(TT // 128):
                        for c in range(NC):
                            mm(psk[:, tc * 128:(tc + 1) * 128],
                               xnp[sl][c][:, tc * 128:(tc + 1) * 128],
                               wqp[ps_][c][:, 256:384],
                               False, False)
                    evvc = mm(psk[:, 0:1], zeros[:, 0:128], zeros[:, 0:1], False,
                              True, inc="pe")
                    bank[bname]["w"] = evvc
                    evve = dv(lambda e, b=psk: e.tensor_copy(
                        out=vst[:, :], in_=b[:, 0:TT]),
                        waits=[evvc, ev_v_free])
                    bank[bname]["r"] = evve
                    ev_v_free = dma("gp", "gv", v_scr[p][:, t0:t0 + TT],
                                    vst[:, :], waits=[evve])
                    ev_xnp_free[sl] = evvc
                    ev_rope_free[sl] = ev_rope_rd
                    yield
                ev_wqp_free[ps_] = evvc
                ebp = dv(lambda e, p=p: e.tensor_tensor(
                    out=kmax[:, p:p + 1], in0=kmax[:, p:p + 1],
                    in1=qmax[:, p:p + 1], op=ALU.mult), waits=[ev_maxk, ev_maxq])
                ef1 = dma("gp", "gcf", cflip[p:p + 1, :], kmax[0:2, p:p + 1],
                          waits=[ebp])
                ef2 = dma("gp", "gcf", crow[:, 2 * p:2 * p + 2], cflip[p:p + 1, :],
                          waits=[ef1])
                em1 = dv(lambda e, p=p: e.reduce_max(
                    out=crow2[0:1, p:p + 1], in_=crow[0:1, 2 * p:2 * p + 2],
                    axis=AXX), waits=[ef2])
                ci = crow2.bitcast(I32)
                em2a = dv(lambda e, p=p: e.tensor_scalar(
                    out=ci[0:1, p:p + 1], in0=ci[0:1, p:p + 1], scalar1=1,
                    scalar2=None, op0=ALU.arith_shift_right), waits=[em1])
                em2_ = dv(lambda e, p=p: e.tensor_scalar(
                    out=ci[0:1, p:p + 1], in0=ci[0:1, p:p + 1],
                    scalar1=0x1FC00000, scalar2=None, op0=ALU.add),
                    waits=[em2a])
                em4 = dv(lambda e, p=p: e.tensor_scalar(
                    out=crow2[0:1, p:p + 1], in0=crow2[0:1, p:p + 1],
                    scalar1=-1.04, scalar2=72.0, op0=ALU.mult, op1=ALU.add),
                    waits=[em2_])
                psk = ps_a
                bname = "a"
                jc += 1
                ecb = mm(psk[:, 0:1], onesf[0:1, 0:128], crow2[0:1, p:p + 1],
                         True, True, waits=[em4, bank[bname]["r"]], inc="pe")
                bank[bname]["w"] = ecb
                ev_cbp = dv(lambda e, b=psk, p=p: e.tensor_copy(
                    out=cbias[:, p:p + 1], in_=b[:, 0:1]), waits=[ecb])
                bank[bname]["r"] = ev_cbp
                ev_pair[p] = [ev_kq_free, ev_v_free, ev_cbp]
                yield

        gen_prep = prep_pairs()
        ev_xf = [None, None]
        ev_xf_free = [list(ev_deq_all), list(ev_deq_all)]
        ev_xn_rd = None
        ev_t1r_free = None
        ev_con2 = ev_con
        for tt in range(NTT):
            sl = tt % 2
            t0 = tt * TT
            wx = [e for e in ev_xf_free[sl] if e]
            for c in range(NC):
                ev_xf[sl] = dma("sy", "sxf%d" % sl, xf[sl][c][:, :],
                                x_fm[c * 128:(c + 1) * 128, t0:t0 + TT],
                                waits=wx if c == 0 else ())
            ev_xf_free[sl] = []
            evxb = None
            for c in range(NC):
                wts = [ev_xf[sl]] + ([ev_xn_rd] if (c == 0 and ev_xn_rd) else [])
                evxb = dv(lambda e, c=c, sl=sl: e.tensor_copy(
                    out=xb[:, c * TT:(c + 1) * TT], in_=xf[sl][c][:, :]), waits=wts)
            evsq = dv(lambda e: e.tensor_tensor(out=xsq[:, :], in0=xb[:, :],
                                                in1=xb[:, :], op=ALU.mult),
                      waits=[evxb])
            evA = evB = None
            for c in range(NC):
                wa = ([evxb, bank["a"]["r"]]) if c == 0 else ()
                wb = ([evsq, bank["b"]["r"]]) if c == 0 else ()
                evA = mm(ps_a[0:1, 0:TT], ones1b[:, 0:1], xb[:, c * TT:(c + 1) * TT],
                         c == 0, c == NC - 1, waits=wa,
                         inc="pe" if c == NC - 1 else None)
                evB = mm(ps_b[0:1, 0:TT], ones1b[:, 0:1], xsq[:, c * TT:(c + 1) * TT],
                         c == 0, c == NC - 1, waits=wb,
                         inc="pe" if c == NC - 1 else None)
            bank["a"]["w"] = evA
            bank["b"]["w"] = evB
            em = dv(lambda e: e.tensor_scalar(out=m_row[:, :], in0=ps_a[0:1, 0:TT],
                                              scalar1=1.0 / D, scalar2=None,
                                              op0=ALU.mult), waits=[evA])
            em2 = dv(lambda e: e.tensor_tensor(out=msq[:, :], in0=m_row[:, :],
                                               in1=m_row[:, :], op=ALU.mult),
                     waits=[em])
            evr = dv(lambda e: e.scalar_tensor_tensor(out=v_row[:, :],
                                                      in0=ps_b[0:1, 0:TT],
                                                      scalar=1.0 / D, in1=msq[:, :],
                                                      op0=ALU.mult,
                                                      op1=ALU.subtract),
                     waits=[evB, em2])
            bank["a"]["r"] = em
            bank["b"]["r"] = evr
            el1 = ac(lambda e: e.activation(out=lnv[:, :], in_=v_row[:, :],
                                            func=AF.Ln, bias=eps5[:, 0:1],
                                            scale=1.0), waits=[evr, ev_con2])
            ea = ac(lambda e: e.activation(out=a_row[:, :], in_=lnv[:, :],
                                           func=AF.Exp, scale=-0.5), waits=[el1])
            eb = dv(lambda e: e.scalar_tensor_tensor(out=b_row[:, :], in0=m_row[:, :],
                                                     scalar=-1.0, in1=a_row[:, :],
                                                     op0=ALU.mult, op1=ALU.mult),
                    waits=[ea])
            eba = mm(ps_a[:, 0:TT], onesf[0:1, 0:128], a_row[:, :], True, True,
                     waits=[ea, bank["a"]["r"]], inc="pe")
            ebb = mm(ps_b[:, 0:TT], onesf[0:1, 0:128], b_row[:, :], True, True,
                     waits=[eb, bank["b"]["r"]], inc="pe")
            bank["a"]["w"] = eba
            bank["b"]["w"] = ebb
            exn = None
            ev_sp = None
            for c in range(NC):
                wts = [eba, ev_t1r_free] if c == 0 else [eba, exn]
                et = dv(lambda e, c=c, sl=sl: e.tensor_tensor(
                    out=lnt[:, :], in0=xf[sl][c][:, :], in1=ps_a[:, 0:TT],
                    op=ALU.mult),
                    waits=wts + ([ev_xn_rd] if (c == 0 and ev_xn_rd) else []))
                exn = dv(lambda e, c=c: e.tensor_tensor(
                    out=xnt[:, c * TT:(c + 1) * TT], in0=lnt[:, :],
                    in1=ps_b[:, 0:TT], op=ALU.add), waits=[ebb, et])
                ev_sp = dma("gp", "gxn", xn_scr[c * 128:(c + 1) * 128, t0:t0 + TT],
                            xnt[:, c * TT:(c + 1) * TT], waits=[exn])
            ev_t1r_free = exn
            bank["a"]["r"] = exn
            bank["b"]["r"] = exn
            ev_xn_rd = ev_sp
            ev_xf_free[sl].append(exn)
            ev_xnspill[tt] = ev_sp
            if tt >= 2:
                for _ in range(26):
                    next(gen_prep)
        bar_mid = P.barrier()

        TQ = float(Q127 / 2.0)

        def ffn_units():
            ev_xnf = [None, None]
            ev_xnf_free = [list(bar_mid), list(bar_mid)]
            ev_wg_free = [None, None]
            ev_wu_free = [None, None]
            ev_h = [None] * NHC
            ev_sg_free = None
            ev_sg1_free = None
            ev_gst_free = None
            for ft in range(NFT):
                f0 = ft * 512
                sl = ft % 2
                evs = None
                for c in range(NC):
                    w0 = ()
                    if c == 0 and ev_xnf_free[sl]:
                        w0 = (ev_xnf_free[sl] if isinstance(ev_xnf_free[sl], list)
                              else [ev_xnf_free[sl]])
                    evs = dma("sy", "sxn%d" % sl, xnf[sl][c][:, :],
                              xn_scr[c * 128:(c + 1) * 128, f0:f0 + 512], waits=w0)
                ev_xnf[sl] = evs
                yield
                for hh in range(NHC):
                    ws = hh % 2
                    eg = dma("sy", "swg%d" % ws, wgb[ws][:, :], wg_bf[hh],
                             waits=[ev_wg_free[ws], ev_deq[2 * hh]])
                    eu_ = dma("sy", "swu%d" % ws, wub[ws][:, :], wu_bf[hh],
                              waits=[ev_wu_free[ws], ev_deq[2 * hh + 1]])
                    yield
                    evg = evu = None
                    for c in range(NC):
                        wg_ = [eg, ev_xnf[sl], bank["b"]["r"]] if c == 0 else ()
                        evg = mm(ps_b[:, :], wgb[ws][:, c * 128:(c + 1) * 128],
                                 xnf[sl][c][:, :], c == 0, c == NC - 1, waits=wg_,
                                 inc="pe" if c == NC - 1 else None)
                    bank["b"]["w"] = evg
                    ev_wg_free[ws] = evg
                    ee0 = dv(lambda e: e.tensor_scalar(
                        out=sg1[:, :], in0=ps_b[:, :], scalar1=-Q127, scalar2=43.0,
                        op0=ALU.mult, op1=ALU.min), waits=[evg, ev_sg1_free])
                    ee1 = ac(lambda e: e.activation(out=sgate[:, :], in_=sg1[:, :],
                                                    func=AF.Exp),
                             waits=[ee0, ev_sg_free])
                    ee2 = ac(lambda e: e.activation(out=sg1[:, :], in_=sgate[:, :],
                                                    func=AF.Ln, bias=onebias[:, 0:1],
                                                    scale=1.0),
                             waits=[ee1])
                    ee3 = ac(lambda e: e.activation(out=sgate[:, :], in_=sg1[:, :],
                                                    func=AF.Exp, scale=-1.0),
                             waits=[ee2])
                    et0 = dv(lambda e: e.scalar_tensor_tensor(
                        out=sg1[:, :], in0=ps_b[:, :], scalar=Q127 * Q127,
                        in1=sgate[:, :], op0=ALU.mult, op1=ALU.mult),
                        waits=[ee3, evg])
                    ev_sg1_free = et0
                    bank["b"]["r"] = et0
                    yield
                    for c in range(NC):
                        wu2 = [eu_, ev_xnf[sl], et0] if c == 0 else ()
                        evu = mm(ps_b[:, :], wub[ws][:, c * 128:(c + 1) * 128],
                                 xnf[sl][c][:, :], c == 0, c == NC - 1, waits=wu2,
                                 inc="pe" if c == NC - 1 else None)
                    bank["b"]["w"] = evu
                    ev_wu_free[ws] = evu
                    evh = dv(lambda e, hh=hh: e.tensor_tensor(
                        out=hbuf[hh][:, :], in0=ps_b[:, :], in1=sg1[:, :],
                        op=ALU.mult), waits=[et0, evu, ev_h[hh]])
                    ev_sg_free = evh
                    ev_sg1_free = evh
                    ev_h[hh] = evh
                    bank["b"]["r"] = evh
                    yield
                ev_wd_free = [ev_wg_free[0], ev_wg_free[1], ev_wu_free[0]]
                evd_last = None
                for dd in range(NC):
                    psd = ps_b
                    bname = "b"
                    wdl = []
                    for seg in range(3):
                        buf = (wgb[0], wgb[1], wub[0])[seg]
                        wsem = ("swg0", "swg1", "swu0")[seg]
                        wdl.append((buf, dma("sy", wsem, buf[:, :],
                                             wd_bf[dd][:, seg * 1024:(seg + 1) * 1024],
                                             waits=[ev_wd_free[seg],
                                                    ev_deq[2 * NHC + dd]])))
                    yield
                    evd = None
                    for hh in range(NHC):
                        seg, off = divmod(hh, 8)
                        w0 = ([wdl[0][1], wdl[1][1], wdl[2][1], ev_h[NHC - 1],
                               bank[bname]["r"]] if hh == 0 else ())
                        evd = mm(psd[:, :], wdl[seg][0][:, off * 128:(off + 1) * 128],
                                 hbuf[hh][:, :], hh == 0, hh == NHC - 1, waits=w0,
                                 inc="pe" if hh == NHC - 1 else None)
                    ev_wd_free = [evd, evd, evd]
                    bank[bname]["w"] = evd
                    evd_last = evd
                    ef0 = dv(lambda e, psd=psd: e.tensor_scalar(
                        out=sg1[:, :], in0=psd[:, :], scalar1=-Q127, scalar2=43.0,
                        op0=ALU.mult, op1=ALU.min), waits=[evd, ev_sg1_free])
                    ef1 = ac(lambda e: e.activation(
                        out=sgate[:, :], in_=sg1[:, :], func=AF.Exp),
                        waits=[ef0, ev_sg_free])
                    ef2 = ac(lambda e: e.activation(out=sg1[:, :], in_=sgate[:, :],
                                                    func=AF.Ln, bias=onebias[:, 0:1],
                                                    scale=1.0),
                             waits=[ef1])
                    es2 = ac(lambda e: e.activation(out=sgate[:, :], in_=sg1[:, :],
                                                    func=AF.Exp, scale=-1.0),
                             waits=[ef2])
                    ev_sg1_free = es2
                    evgb = dv(lambda e, psd=psd: e.scalar_tensor_tensor(
                        out=gst[:, :], in0=psd[:, :], scalar=Q127,
                        in1=sgate[:, :], op0=ALU.mult, op1=ALU.mult),
                        waits=[es2, ev_gst_free])
                    ev_sg_free = evgb
                    ev_sg1_free = evgb
                    bank[bname]["r"] = evgb
                    ev_gst_free = dma("gp", "gg", g_scr[dd][:, f0:f0 + 512],
                                      gst[:, :], waits=[evgb])
                    yield

                for hh in range(NHC):
                    ev_h[hh] = evd_last
                ev_xnf_free[sl] = evd_last
                ev_wg_free = [evd_last, evd_last]
                ev_wu_free = [evd_last, evd_last]

        gen_ffn = ffn_units()
        while ev_pair[0] is None:
            next(gen_prep)
        npump = [0]
        gens = [gen_prep, gen_ffn]

        def pump():
            npump[0] += 1
            g = gens[npump[0] % 2]
            try:
                next(g)
            except StopIteration:
                try:
                    next(gens[(npump[0] + 1) % 2])
                except StopIteration:
                    pass
        ev_kvqf = None
        ev_pfree = [None, None, None]
        ev_ost_free = None
        ev_zt_free = None
        ev_zbs_free = None
        ev_oraw_free = [None, None]
        st = {"zbs_free": None, "ost_free": None, "oraw_free": [None, None],
              "zt_free": None}

        def pr6(i):
            j = i % 6
            return pring[j // 2][:, (j % 2) * 1024:((j % 2) + 1) * 1024]

        for p in range(NP):
            sl = p % 2
            while ev_pair[p] is None:
                next(gen_prep)
            wld = ev_pair[p] + ([ev_kvqf] if ev_kvqf else [])
            ek = dma("sy", "skv%d" % sl, kb[:, 0:NTOT], kt_scr[p], waits=wld)
            eq = dma("sy", "skv%d" % sl, qb[:, 0:NOWN], q_scr[p])
            evv = dma("sy", "skv%d" % sl, vb[:, 0:NTOT], v_scr[p])
            ek = eq = evv
            ev_sexp = [None, None]
            pfree = [[] for _ in range(6)]
            pend_flush = [None]
            for qt in range(NQT):
                q0 = qt * QT
                mm(ps_pv[:, 0:QT], zeros[:, 0:128], zeros[:, 0:QT], True, False,
                   waits=[ek, bank["pv"]["r"]])
                ev_exp = [None] * NCH

                def emit_pv(c):
                    slot = pr6(c)
                    mm(ps_pv[0:64, 0:QT], vb[:, c * 128:c * 128 + 64],
                       slot[:, 0:QT], False, False, tp=(0, 0), waits=[ev_exp[c]])
                    ev = mm(ps_pv[64:128, 0:QT], vb[:, c * 128 + 64:(c + 1) * 128],
                            slot[:, 512:512 + QT], False, False, tp=(0, 64),
                            inc="pe")
                    pfree[c % 6].append(ev)

                def emit_z(c0):
                    sa, sb_ = pr6(c0), pr6(c0 + 1)
                    mm(ps_z[0:1, 0:QT], onesb[:, 0:1], sa[:, 0:QT], False, False,
                       tp=(0, 0), waits=[ev_exp[c0 + 1]])
                    mm(ps_z[32:33, 0:QT], onesb[:, 1:2], sa[:, 512:512 + QT],
                       False, False, tp=(0, 32))
                    mm(ps_z[64:65, 0:QT], onesb[:, 2:3], sb_[:, 0:QT],
                       False, False, tp=(0, 64))
                    ev = mm(ps_z[96:97, 0:QT], onesb[:, 3:4], sb_[:, 512:512 + QT],
                            False, False, tp=(0, 96), inc="pe")
                    pfree[c0 % 6].append(ev)
                    pfree[(c0 + 1) % 6].append(ev)

                for c in range(NCH):
                    hf = c % 2
                    so = hf * 1024
                    mm(ps_s[:, so:so + QT], kb[0:64, c * 128:(c + 1) * 128],
                       qb[0:64, q0:q0 + QT], True, True, tp=(0, 0),
                       waits=[ek, ev_sexp[hf]])
                    ev_s = mm(ps_s[:, so + 512:so + 512 + QT],
                              kb[64:128, c * 128:(c + 1) * 128],
                              qb[64:128, q0:q0 + QT], True, True, tp=(64, 0),
                              inc="pe")
                    ev_exp[c] = ac(lambda e, c=c, so=so, p=p: e.activation(
                        out=pr6(c), in_=ps_s[:, so:so + 1024], func=AF.Exp,
                        bias=cbias[:, p:p + 1], scale=1.0),
                        waits=[ev_s] + pfree[c % 6])
                    pfree[c % 6] = []
                    ev_sexp[hf] = ev_exp[c]
                    if c == 2:
                        if pend_flush[0] is not None:
                            pend_flush[0]()
                            pend_flush[0] = None
                        mm(ps_z[:, 0:QT], zeros[:, 0:128], zeros[:, 0:QT], True,
                           False, waits=[bank["z"]["r"]])
                    if c >= 2:
                        emit_pv(c - 2)
                    if c >= 3 and c % 2 == 1:
                        emit_z(c - 3)
                    pump()
                emit_pv(NCH - 2)
                emit_pv(NCH - 1)
                emit_z(NCH - 2)
                evc1 = mm(ps_pv[:, 0:1], zeros[:, 0:128], zeros[:, 0:1], False,
                          True, inc="pe")
                evc2 = mm(ps_z[:, 0:1], zeros[:, 0:128], zeros[:, 0:1], False,
                          True, inc="pe")
                bank["pv"]["w"] = evc1
                bank["z"]["w"] = evc2
                osl = qt % 2
                eor = dv(lambda e, osl=osl: e.tensor_copy(out=oraw[osl][:, :],
                                                          in_=ps_pv[:, 0:QT]),
                         waits=[evc1, st["oraw_free"][osl]])
                bank["pv"]["r"] = eor
                ez0 = dv(lambda e: e.tensor_copy(out=zt[:, :], in_=ps_z[:, 0:QT]),
                         waits=[evc2, st["zt_free"]])
                bank["z"]["r"] = ez0
                ec1 = dv(lambda e: e.tensor_copy(out=zc1[:, :], in_=zt[64:65, :]),
                         waits=[ez0])
                ec2 = dv(lambda e: e.tensor_copy(out=zc2[:, :], in_=zt[96:97, :]),
                         waits=[ez0])
                ec3 = dv(lambda e: e.tensor_copy(out=zc3[:, :], in_=zt[32:33, :]),
                         waits=[ez0])
                ea1 = dv(lambda e: e.tensor_tensor(out=zrow[:, 0:QT],
                                                   in0=zt[0:1, :], in1=zc1[:, :],
                                                   op=ALU.add), waits=[ec1])
                ea2 = dv(lambda e: e.tensor_tensor(out=zrow[:, QT:2 * QT],
                                                   in0=zc3[:, :], in1=zc2[:, :],
                                                   op=ALU.add), waits=[ec2, ec3])
                st["zt_free"] = ea2
                eg_ = dv(lambda e: e.tensor_scalar(out=zln[:, :], in0=zrow[:, :],
                                                   scalar1=1e-30, scalar2=None,
                                                   op0=ALU.add), waits=[ea1, ea2])
                ei = dv(lambda e: e.reciprocal(out=zi[:, :], in_=zln[:, :]),
                        waits=[eg_])
                eib = dv(lambda e: e.tensor_copy(out=zib[:, :], in_=zi[:, :]),
                         waits=[ei])

                def flush(p=p, qt=qt, q0=q0, eib=eib, osl=osl, eor=eor):
                    mm(ps_z[:, 0:QT], zeros[:, 0:128], zeros[:, 0:QT], True, False,
                       waits=[eib, bank["z"]["r"]])
                    mm(ps_z[0:64, 0:QT], onesfh[0:1, 0:64], zib[:, 0:QT],
                       False, False, tp=(0, 0))
                    mm(ps_z[64:128, 0:QT], onesfh[0:1, 0:64], zib[:, QT:2 * QT],
                       False, False, tp=(0, 64))
                    ezb = mm(ps_z[:, 0:1], zeros[:, 0:128], zeros[:, 0:1], False,
                             True, inc="pe")
                    bank["z"]["w"] = ezb
                    ezbs = dv(lambda e: e.tensor_copy(out=zbs[:, :],
                                                      in_=ps_z[:, 0:QT]),
                              waits=[ezb, st["zbs_free"]])
                    bank["z"]["r"] = ezbs
                    eo = dv(lambda e, osl=osl: e.tensor_tensor(
                        out=ost[:, :], in0=oraw[osl][:, :], in1=zbs[:, :],
                        op=ALU.mult), waits=[eor, ezbs, st["ost_free"]])
                    st["zbs_free"] = eo
                    st["oraw_free"][osl] = eo
                    st["ost_free"] = dma("gp", "go", o_scr[p][:, q0:q0 + QT],
                                         ost[:, :], waits=[eo])

                pend_flush[0] = flush
            pend_flush[0]()
            pend_flush[0] = None
            ev_kvqf = evc1

        for g in gens:
            for _ in g:
                pass

        bar_pr = P.barrier()
        ev_ob = []
        for dd in range(NC):
            ev_ob.append(dma("sy", "sob", ob[dd][:, :], o_scr[dd],
                             waits=bar_pr if dd == 0 else ()))
        ev_c1 = ev_c2 = None
        for dd in range(NC):
            e1_ = dma("sy", "swp0", xcb[0][:, :],
                      w_proj[dd * 128:(dd + 1) * 128, 0:512],
                      waits=[ev_c1] if ev_c1 else bar_pr)
            e2_ = dma("sy", "swp1", xcb[1][:, :],
                      w_proj[dd * 128:(dd + 1) * 128, 512:1024],
                      waits=[ev_c2] if ev_c2 else bar_pr)
            ev_c1 = dv(lambda e, dd=dd: e.tensor_copy(out=wpb[dd][:, 0:512],
                                                      in_=xcb[0][:, :]), waits=[e1_])
            ev_c2 = dv(lambda e, dd=dd: e.tensor_copy(out=wpb[dd][:, 512:1024],
                                                      in_=xcb[1][:, :]), waits=[e2_])
        ev_wp = ev_c2
        prb = ["s0", "s1", "s2", "s3"]
        bi = 0
        ev_sfm = None
        for oo in range(NC):
            for qt in range(NQT):
                bnk = prb[bi % 4]
                bi += 1
                evm = None
                for dd in range(NC):
                    w0 = ([ev_wp, ev_c1, ev_ob[-1], bank[bnk]["r"]]
                          if dd == 0 else ())
                    evm = mm(SBANK[bnk][:, 0:QT], wpb[dd][:, oo * 128:(oo + 1) * 128],
                             ob[dd][:, qt * QT:(qt + 1) * QT], dd == 0, dd == NC - 1,
                             waits=w0, inc="pe" if dd == NC - 1 else None)
                bank[bnk]["w"] = evm
                ev_sfm = dv(lambda e, b=bnk, oo=oo, qt=qt: e.tensor_scalar(
                    out=sfm[oo][:, qt * QT:(qt + 1) * QT], in0=SBANK[b][:, 0:QT],
                    scalar1=bpt[:, oo:oo + 1], scalar2=None, op0=ALU.add),
                    waits=[evm, e3])
                bank[bnk]["r"] = ev_sfm

        ev_gc = [ev_sfm, ev_sfm]
        ev_xc = [ev_c1, ev_c2]
        ev_otf = [None, None]
        ev_sg = None
        for ft in range(NFT):
            f0 = ft * 512
            for oo in range(NC):
                slot = (ft * NC + oo) % 2
                eg = dma("sy", "sge%d" % slot, gcb[slot][:, :],
                         g_scr[oo][:, f0:f0 + 512], waits=[ev_gc[slot]])
                ex = dma("sy", "sxe%d" % slot, xcb[slot][:, :],
                         x_fm[oo * 128:(oo + 1) * 128, f0:f0 + 512],
                         waits=[ev_xc[slot]])
                esg2 = dv(lambda e, oo=oo, f0=f0, slot=slot: e.tensor_tensor(
                    out=sgc[:, :], in0=sfm[oo][:, f0:f0 + 512], in1=gcb[slot][:, :],
                    op=ALU.mult), waits=[eg, ev_sfm, ev_sg])
                ev_gc[slot] = esg2
                eot = dv(lambda e, slot=slot: e.tensor_tensor(
                    out=otc[slot][:, :], in0=sgc[:, :], in1=xcb[slot][:, :],
                    op=ALU.add), waits=[esg2, ex, ev_otf[slot]])
                ev_sg = eot
                ev_xc[slot] = eot
                ev_otf[slot] = dma("sy", "sout%d" % slot,
                                   out_fm[oo * 128:(oo + 1) * 128, f0:f0 + 512],
                                   otc[slot][:, :], waits=[eot])

        P.emit("gp", lambda e: None,
               waits=[("sout0", P.cnt["sout0"]), ("sout1", P.cnt["sout1"])])

        block = E(nc.Block())

        @block.sync
        def _(sy):
            for f in P.cmds["sy"]:
                f(sy)

        @block.gpsimd
        def _(gp):
            for f in P.cmds["gp"]:
                f(gp)

        @block.tensor
        def _(te):
            for f in P.cmds["pe"]:
                f(te)

        @block.vector
        def _(ve):
            for f in P.cmds["dv"]:
                f(ve)

        @block.scalar
        def _(sc):
            for f in P.cmds["ac"]:
                f(sc)

    return nc


def _bc2_const():
    import ml_dtypes
    m = np.zeros((2, 128), dtype=ml_dtypes.bfloat16)
    m[0, 0:64] = 1
    m[1, 64:128] = 1
    return m


def _rope_tables(npos):
    half = 32
    inv = (np.float32(1.0) /
           np.power(np.float32(10000.0),
                    np.arange(half, dtype=np.float32) / np.float32(half)))
    t = np.arange(npos, dtype=np.float32)
    f = np.outer(t, inv).astype(np.float32)
    cos = np.cos(f).T.astype(np.float32)
    sin = np.sin(f).T.astype(np.float32)
    cos2x = np.ascontiguousarray(np.tile(cos, (4, 1)))
    sinS = np.ascontiguousarray(np.concatenate([-sin, sin, -sin, sin], axis=0))
    return cos2x, sinS


def kernel(L0_attn_x, L0_attn_w_qkv, L0_attn_q_scale, L0_attn_k_scale,
           L0_attn_w_proj, L0_attn_b_proj, L0_ffn_w_gate, L0_ffn_w_up,
           L0_ffn_w_down):
    x = np.asarray(L0_attn_x)
    B, N, C = x.shape
    NOWN = N // 2
    cos2x, sinS = _rope_tables(N)
    w_qkv = np.ascontiguousarray(L0_attn_w_qkv, dtype=np.float32)
    w_proj = np.ascontiguousarray(L0_attn_w_proj, dtype=np.float32)
    b_projt = np.ascontiguousarray(
        np.asarray(L0_attn_b_proj, dtype=np.float32).reshape(NC, 128).T)
    qs2x = np.ascontiguousarray(
        np.concatenate([L0_attn_q_scale, L0_attn_q_scale]).astype(np.float32)
        .reshape(128, 1))
    ks2x = np.ascontiguousarray(
        np.concatenate([L0_attn_k_scale, L0_attn_k_scale]).astype(np.float32)
        .reshape(128, 1))
    wgT = np.ascontiguousarray(np.asarray(L0_ffn_w_gate, dtype=np.int32).T)
    wuT = np.ascontiguousarray(np.asarray(L0_ffn_w_up, dtype=np.int32).T)
    wdT = np.ascontiguousarray(np.asarray(L0_ffn_w_down, dtype=np.int32).T)

    nc = build_program(NTOT=N, NOWN=NOWN)

    in_maps = []
    for core in range(8):
        b, h = core // 2, core % 2
        xt = np.ascontiguousarray(x[b].T.astype(np.float32))
        if h == 0:
            x_c, rc, rs = xt, cos2x, sinS
        else:
            x_c = np.ascontiguousarray(
                np.concatenate([xt[:, NOWN:], xt[:, :NOWN]], axis=1))
            rc = np.ascontiguousarray(
                np.concatenate([cos2x[:, NOWN:], cos2x[:, :NOWN]], axis=1))
            rs = np.ascontiguousarray(
                np.concatenate([sinS[:, NOWN:], sinS[:, :NOWN]], axis=1))
        in_maps.append(dict(x_fm=x_c, w_qkv=w_qkv, w_proj=w_proj,
                            b_projt=b_projt, qs2x=qs2x, ks2x=ks2x,
                            wgT=wgT, wuT=wuT, wdT=wdT, ropec=rc, ropes=rs,
                            bc2c=_bc2_const()))

    res = run_bass_kernel_spmd(nc, in_maps, core_ids=list(range(8)))
    out = np.empty((B, N, C), dtype=np.float32)
    for core in range(8):
        b, h = core // 2, core % 2
        out[b, h * NOWN:(h + 1) * NOWN, :] = np.asarray(
            res.results[core]["out_fm"]).T
    return out


# version 15
# speedup vs baseline: 1.0008x; 1.0008x over previous
import numpy as np
import concourse.bass as bass
import concourse.mybir as mybir
from concourse.bass_utils import run_bass_kernel_spmd

F32 = mybir.dt.float32
BF16 = mybir.dt.bfloat16
I32 = mybir.dt.int32
AF = mybir.ActivationFunctionType
ALU = mybir.AluOpType
AXX = mybir.AxisListType.X

D = 1024
NH = 16
HD = 64
HID = 3072
NP = 8
NC = 8
NHC = HID // 128
Q127 = float(1.0 / 127.0)

ENGS = ("pe", "dv", "ac", "sy", "gp")


class Plan:
    def __init__(self, sems):
        self.sems = sems
        self.cnt = {k: 0 for k in sems}
        self.cmds = {e: [] for e in ENGS}
        self.last_wait = {e: {} for e in ENGS}

    def emit(self, eng, fn, waits=(), inc=None, amt=None):
        merged = {}
        for w in waits:
            if w is None:
                continue
            s, v = w
            if v > merged.get(s, 0):
                merged[s] = v
        nw = []
        lw = self.last_wait[eng]
        for s, v in merged.items():
            if v <= 0 or lw.get(s, 0) >= v:
                continue
            lw[s] = v
            nw.append((s, v))
        a = None
        if inc is not None:
            a = amt if amt is not None else (16 if inc not in ("pe", "dv", "ac") else 1)
            self.cnt[inc] += a
        ev = (inc, self.cnt[inc]) if inc is not None else None
        sems = self.sems

        def closure(eng_obj, fn=fn, nw=tuple(nw), inc=inc, a=a):
            for s, v in nw:
                eng_obj.wait_ge(sems[s], v)
            r = fn(eng_obj)
            if inc is not None:
                r.then_inc(sems[inc], a)
        self.cmds[eng].append(closure)
        return ev

    def barrier(self):
        return [(s, self.cnt[s]) for s in self.sems]


def build_program(NTOT=8192, NOWN=4096, TT=512, QT=512):
    nc = bass.Bass("TRN2", target_bir_lowering=False)
    NTT = NTOT // TT
    NTTO = NOWN // TT
    NQT = NOWN // QT
    NCH = NTOT // 128
    NFT = NOWN // 512

    x_fm = nc.dram_tensor("x_fm", [D, NTOT], F32, kind="ExternalInput").ap()
    w_qkv = nc.dram_tensor("w_qkv", [D, 3 * D], F32, kind="ExternalInput").ap()
    w_proj = nc.dram_tensor("w_proj", [D, D], F32, kind="ExternalInput").ap()
    b_projt = nc.dram_tensor("b_projt", [128, NC], F32, kind="ExternalInput").ap()
    qs2x_in = nc.dram_tensor("qs2x", [128, 1], F32, kind="ExternalInput").ap()
    ks2x_in = nc.dram_tensor("ks2x", [128, 1], F32, kind="ExternalInput").ap()
    wgT = nc.dram_tensor("wgT", [D, HID], I32, kind="ExternalInput").ap()
    wuT = nc.dram_tensor("wuT", [D, HID], I32, kind="ExternalInput").ap()
    wdT = nc.dram_tensor("wdT", [HID, D], I32, kind="ExternalInput").ap()
    ropec_in = nc.dram_tensor("ropec", [128, NTOT], F32, kind="ExternalInput").ap()
    ropes_in = nc.dram_tensor("ropes", [128, NTOT], F32, kind="ExternalInput").ap()
    bc2_in = nc.dram_tensor("bc2c", [2, 128], BF16, kind="ExternalInput").ap()

    out_fm = nc.dram_tensor("out_fm", [D, NOWN], F32, kind="ExternalOutput").ap()

    wg_bf = nc.dram_tensor("wg_bf", [NHC, 128, NC * 128], BF16, kind="Internal").ap()
    wu_bf = nc.dram_tensor("wu_bf", [NHC, 128, NC * 128], BF16, kind="Internal").ap()
    wd_bf = nc.dram_tensor("wd_bf", [NC, 128, HID], BF16, kind="Internal").ap()
    xn_scr = nc.dram_tensor("xn_scr", [D, NTOT], BF16, kind="Internal").ap()
    wqkv_scr = nc.dram_tensor("wqkv_scr", [NC, 128, 3 * D], BF16, kind="Internal").ap()
    kt_scr = nc.dram_tensor("kt_scr", [NP, 128, NTOT], BF16, kind="Internal").ap()
    q_scr = nc.dram_tensor("q_scr", [NP, 128, NOWN], BF16, kind="Internal").ap()
    v_scr = nc.dram_tensor("v_scr", [NP, 128, NTOT], BF16, kind="Internal").ap()
    o_scr = nc.dram_tensor("o_scr", [NP, 128, NOWN], BF16, kind="Internal").ap()
    g_scr = nc.dram_tensor("g_scr", [NC, 128, NOWN], BF16, kind="Internal").ap()
    cflip = nc.dram_tensor("cflip", [NP, 2], F32, kind="Internal").ap()

    from contextlib import ExitStack
    with ExitStack() as es:
        E = es.enter_context
        zeros = E(nc.sbuf_tensor("zeros", [128, 512], BF16))
        onesb = E(nc.sbuf_tensor("onesb", [128, 4], BF16))
        onesf = E(nc.sbuf_tensor("onesf", [128, 128], F32))
        ones1b = E(nc.sbuf_tensor("ones1b", [128, 1], BF16))
        onesfh = E(nc.sbuf_tensor("onesfh", [128, 64], BF16))
        bones2 = E(nc.sbuf_tensor("bones2", [128, 2], BF16))
        bc2 = E(nc.sbuf_tensor("bc2", [2, 128], BF16))
        eps5 = E(nc.sbuf_tensor("eps5", [1, 1], F32))
        epsz = E(nc.sbuf_tensor("epsz", [1, 1], F32))
        bln64 = E(nc.sbuf_tensor("bln64", [1, 1], F32))
        eps6 = E(nc.sbuf_tensor("eps6", [2, 1], F32))
        onebias = E(nc.sbuf_tensor("onebias", [128, 1], F32))
        qs2x = E(nc.sbuf_tensor("qs2xt", [128, 1], F32))
        ks2x = E(nc.sbuf_tensor("ks2xt", [128, 1], F32))
        qse = E(nc.sbuf_tensor("qse", [128, 1], F32))
        bpt = E(nc.sbuf_tensor("bpt", [128, NC], F32))
        cbias = E(nc.sbuf_tensor("cbias", [128, NP], F32))
        kmax = E(nc.sbuf_tensor("kmax", [2, NP], F32))
        qmax = E(nc.sbuf_tensor("qmax", [2, NP], F32))
        crow = E(nc.sbuf_tensor("crow", [1, 2 * NP], F32))
        crow2 = E(nc.sbuf_tensor("crow2", [1, NP], F32))
        m_row = E(nc.sbuf_tensor("m_row", [1, TT], F32))
        msq = E(nc.sbuf_tensor("msq", [1, TT], F32))
        v_row = E(nc.sbuf_tensor("v_row", [1, TT], F32))
        lnv = E(nc.sbuf_tensor("lnv", [1, TT], F32))
        a_row = E(nc.sbuf_tensor("a_row", [1, TT], F32))
        b_row = E(nc.sbuf_tensor("b_row", [1, TT], F32))
        ln1 = E(nc.sbuf_tensor("ln1", [2, TT], F32))
        rstdf = E(nc.sbuf_tensor("rstdf", [2, TT], F32))
        rstdb = E(nc.sbuf_tensor("rstdb", [2, TT], BF16))
        redm = E(nc.sbuf_tensor("redm", [2, 1], F32))
        zt = E(nc.sbuf_tensor("zt", [128, QT], F32))
        zc1 = E(nc.sbuf_tensor("zc1", [1, QT], F32))
        zc2 = E(nc.sbuf_tensor("zc2", [1, QT], F32))
        zc3 = E(nc.sbuf_tensor("zc3", [1, QT], F32))
        zrow = E(nc.sbuf_tensor("zrow", [1, 2 * QT], F32))
        zln = E(nc.sbuf_tensor("zln", [1, 2 * QT], F32))
        zi = E(nc.sbuf_tensor("zi", [1, 2 * QT], F32))
        zib = E(nc.sbuf_tensor("zib", [1, 2 * QT], BF16))
        zbs = E(nc.sbuf_tensor("zbs", [128, QT], BF16))
        oraw = [E(nc.sbuf_tensor("oraw0", [128, QT], F32)),
                E(nc.sbuf_tensor("oraw1", [128, QT], F32))]

        ABIG = 82624
        big = E(nc.sbuf_tensor("big", [128, ABIG], BF16))

        class Arena:
            def __init__(self, start=0):
                self.off = start

            def a(self, cols, dt=BF16):
                n = cols if dt == BF16 else 2 * cols
                self.off = (self.off + 15) // 16 * 16
                o = self.off
                self.off += n
                assert self.off <= ABIG, f"arena overflow {self.off}"
                sl = big[:, o:o + n]
                return sl.bitcast(dt) if dt != BF16 else sl

        ar4 = Arena()
        kb = ar4.a(NTOT)
        vb = ar4.a(NTOT)
        qb = ar4.a(NOWN)
        pring = [ar4.a(2048) for _ in range(3)]
        ost = ar4.a(QT)
        assert ar4.off <= 27200, ar4.off
        ar3 = Arena(27200)
        xnf = [[ar3.a(512) for _ in range(NC)] for _ in range(2)]
        wgb = [ar3.a(NC * 128) for _ in range(2)]
        wub = [ar3.a(NC * 128) for _ in range(2)]
        hbuf = [ar3.a(512) for _ in range(NHC)]
        sgate = ar3.a(512, F32)
        sg1 = ar3.a(512, F32)
        gst = ar3.a(512)
        assert ar3.off <= 54400, ar3.off
        arp = Arena(54400)
        xnp = [[arp.a(TT) for _ in range(NC)] for _ in range(2)]
        wqp = [[arp.a(384) for _ in range(NC)] for _ in range(2)]
        ropec = [arp.a(TT, F32) for _ in range(2)]
        ropes = [arp.a(TT, F32) for _ in range(2)]
        kraw = arp.a(TT, F32)
        usc = arp.a(TT, F32)
        urot = arp.a(TT, F32)
        t1r = arp.a(TT, F32)
        tsr = arp.a(TT, F32)
        sqt = arp.a(TT)
        kqst = arp.a(TT)
        vst = arp.a(TT)
        assert arp.off <= ABIG, arp.off
        arl = Arena(0)
        xf = [[arl.a(TT, F32) for _ in range(NC)] for _ in range(2)]
        xb = arl.a(NC * TT)
        xsq = arl.a(NC * TT)
        xnt = arl.a(NC * TT)
        lnt = arl.a(TT, F32)
        assert arl.off <= 31000, arl.off
        stg = [big[:, 31008:31008 + 6144].bitcast(I32),
               big[:, 37152:37152 + 6144].bitcast(I32)]
        stgf = [big[:, 31008:31008 + 6144].bitcast(F32),
                big[:, 37152:37152 + 6144].bitcast(F32)]
        bst = [big[:, 43296:43296 + 3072], big[:, 46368:46368 + 3072]]
        ar5 = Arena(0)
        ob = [ar5.a(NOWN) for _ in range(NC)]
        wpb = [ar5.a(D) for _ in range(NC)]
        sfm = [ar5.a(NOWN) for _ in range(NC)]
        gcb = [ar5.a(512) for _ in range(2)]
        xcb = [ar5.a(512, F32) for _ in range(2)]
        sgc = ar5.a(512)
        otc = [ar5.a(512, F32) for _ in range(2)]
        assert ar5.off <= ABIG, ar5.off

        ps_s = E(nc.psum_tensor("ps_s", [128, 2048], F32))
        ps_pv = E(nc.psum_tensor("ps_pv", [128, 512], F32))
        ps_z = E(nc.psum_tensor("ps_z", [128, 512], F32))
        ps_a = E(nc.psum_tensor("ps_a", [128, 512], F32))
        ps_b = E(nc.psum_tensor("ps_b", [128, 512], F32))
        SBANK = {"s0": ps_s[:, 0:512], "s1": ps_s[:, 512:1024],
                 "s2": ps_s[:, 1024:1536], "s3": ps_s[:, 1536:2048]}

        DMA_SEMS = ("scst", "swq0", "swq1", "sxf0", "sxf1", "sxn0", "sxn1",
                    "swg0", "swg1", "swu0", "swu1", "sob", "swp0", "swp1",
                    "skv0", "skv1", "sge0", "sge1", "sxe0", "sxe1", "sxp0", "sxp1",
                    "sout0", "sout1", "gkq", "gv", "gxn", "gbst0", "gbst1",
                    "gg", "go", "gcf")
        sems = {s: E(nc.semaphore(s)) for s in ("pe", "dv", "ac") + DMA_SEMS}
        P = Plan(sems)

        def dma(q, sem, dst, src, waits=()):
            return P.emit(q, lambda e, d=dst, s=src: e.dma_start(out=d, in_=s),
                          waits=waits, inc=sem)

        def mm(out, lhsT, rhs, start, stop, tp=None, waits=(), inc=None):
            def f(e, o=out, l=lhsT, r=rhs, st=start, sp=stop, tp=tp):
                return e.matmul(o, l, r, start=st, stop=sp, tile_position=tp)
            return P.emit("pe", f, waits=waits, inc=inc)

        def dv(fn, waits=()):
            return P.emit("dv", fn, waits=waits, inc="dv")

        def ac(fn, waits=()):
            return P.emit("ac", fn, waits=waits, inc="ac")

        bank = {k: {"w": None, "r": None} for k in
                ("s0", "s1", "s2", "s3", "pv", "z", "a", "b")}

        dv(lambda e: e.memset(zeros[:, :], 0.0))
        dv(lambda e: e.memset(onesb[:, :], 1.0))
        dv(lambda e: e.memset(onesf[:, :], 1.0))
        dv(lambda e: e.memset(ones1b[:, :], 1.0))
        dv(lambda e: e.memset(onesfh[:, :], 1.0))
        dv(lambda e: e.memset(bones2[0:64, 0:1], 1.0))
        dv(lambda e: e.memset(bones2[64:128, 0:1], 0.0))
        dv(lambda e: e.memset(bones2[0:64, 1:2], 0.0))
        dv(lambda e: e.memset(bones2[64:128, 1:2], 1.0))
        pass
        dv(lambda e: e.memset(eps5[:, :], 1e-5))
        dv(lambda e: e.memset(epsz[:, :], 1e-38))
        dv(lambda e: e.memset(bln64[:, :], -44.3614195558365))
        dv(lambda e: e.memset(eps6[:, :], 1e-6))
        dv(lambda e: e.memset(onebias[:, :], 1.0))
        dv(lambda e: e.memset(kmax[:, :], 0.0))
        ev_con = dv(lambda e: e.memset(qmax[:, :], 0.0))
        dma("sy", "scst", qs2x[:, :], qs2x_in)
        dma("sy", "scst", ks2x[:, :], ks2x_in)
        dma("sy", "scst", bpt[:, :], b_projt)
        e_cst = dma("sy", "scst", bc2[:, :], bc2_in)
        e1 = e2 = e3 = e_cst
        ev_qse = dv(lambda e: e.tensor_scalar(out=qse[:, :], in0=qs2x[:, :],
                                              scalar1=0.125, scalar2=None,
                                              op0=ALU.mult), waits=[e1])

        ev_stg = [None, None]
        ev_bst0 = [None, None]
        for c in range(NC):
            sl = c % 2
            el = dma("sy", "swq%d" % sl, stgf[sl][:, 0:3 * D],
                     w_qkv[c * 128:(c + 1) * 128, :], waits=[ev_stg[sl]])
            ec = dv(lambda e, sl=sl: e.tensor_copy(
                out=bst[sl][:, 0:3072], in_=stgf[sl][:, 0:3 * D]),
                waits=[el, ev_bst0[sl]])
            ev_stg[sl] = ec
            ev_bst0[sl] = dma("gp", "gbst%d" % sl, wqkv_scr[c], bst[sl][:, 0:3072],
                              waits=[ec])
        ev_wq_all = ev_bst0[(NC - 1) % 2]

        ev_bst = list(ev_bst0)
        wlist = []
        for hh in range(NHC):
            wlist.append((D, wgT[:, hh * 128:(hh + 1) * 128], wg_bf[hh]))
            wlist.append((D, wuT[:, hh * 128:(hh + 1) * 128], wu_bf[hh]))
        for dd in range(NC):
            wlist.append((HID, wdT[:, dd * 128:(dd + 1) * 128], wd_bf[dd]))
        slot = 0
        ev_deq = {}
        ev_stg_free = [ev_stg[0], ev_stg[1]]
        for i, (rows, src, dst) in enumerate(wlist):
            ncols = rows
            it = stg[slot][:, 0:ncols]
            bt = bst[slot][:, 0:ncols]
            s3 = src.rearrange("(c p) h -> p c h", p=128)
            i3 = it.rearrange("p (c h) -> p c h", h=128)
            el = dma("sy", "swq%d" % slot, i3, s3,
                     waits=[ev_stg_free[slot], ev_wq_all])
            ec = dv(lambda e, it=it, bt=bt: e.tensor_copy(out=bt, in_=it),
                    waits=[el, ev_bst[slot]])
            ev_stg_free[slot] = ec
            ev_bst[slot] = dma("gp", "gbst%d" % slot, dst, bt, waits=[ec])
            ev_deq[i] = ev_bst[slot]
            slot ^= 1
        ev_deq_all = P.barrier()

        ev_pair = [None] * NP
        ev_xnspill = [None] * NTT
        bar_mid = None

        def prep_pairs():
            ev_xnp = [None, None]
            ev_xnp_free = [None, None]
            ev_wqp_free = [None, None]
            ev_wqp = [None, None]
            ev_kraw_free = None
            ev_sq_free = None
            ev_usc_free = None
            ev_urot_free = None
            ev_t1_free = None
            ev_tsr_free = None
            ev_kq_free = None
            ev_v_free = None
            ev_rope_free = [None, None]
            jc = 0
            for p in range(NP):
                ps_ = p % 2
                wq_w = [ev_wqp_free[ps_], ev_wq_all]
                evw = None
                for c in range(NC):
                    for j, coff in enumerate((1024, 0, 2048)):
                        evw = dma("sy", "swq%d" % ps_,
                                  wqp[ps_][c][:, j * 128:(j + 1) * 128],
                                  wqkv_scr[c][:, coff + p * 128:coff + (p + 1) * 128],
                                  waits=wq_w if (c == 0 and j == 0) else ())
                ev_wqp[ps_] = evw
                yield
                ev_maxk = None
                ev_maxq = None
                for tt in range(NTT):
                    sl = tt % 2
                    own = tt < NTTO
                    t0 = tt * TT
                    while ev_xnspill[tt] is None:
                        yield
                    wxx = [ev_xnp_free[sl], ev_xnspill[tt]]
                    evx = None
                    for c in range(NC):
                        evx = dma("sy", "sxp%d" % sl, xnp[sl][c][:, :],
                                  xn_scr[c * 128:(c + 1) * 128, t0:t0 + TT],
                                  waits=wxx if c == 0 else ())
                    er1 = dma("sy", "sxp%d" % sl, ropec[sl][:, :],
                              ropec_in[:, t0:t0 + TT],
                              waits=[ev_rope_free[sl]])
                    er2 = dma("sy", "sxp%d" % sl, ropes[sl][:, :],
                              ropes_in[:, t0:t0 + TT])
                    evx = er2
                    er1 = er2
                    ev_rope_rd = None
                    yield
                    jobs = [("k", ks2x, 0, kt_scr[p])]
                    if own:
                        jobs.append(("q", qse, 1, q_scr[p]))
                    for kind, scv, joff, dstt in jobs:
                        psk = ps_a
                        bname = "a"
                        jc += 1
                        evk = None
                        for c in range(NC):
                            w0 = ([ev_wqp[ps_], evx, bank[bname]["r"]]
                                  if c == 0 else ())
                            evk = mm(psk[:, 0:TT],
                                     wqp[ps_][c][:, joff * 128:(joff + 1) * 128],
                                     xnp[sl][c][:, :], c == 0, c == NC - 1,
                                     waits=w0, inc="pe" if c == NC - 1 else None)
                        bank[bname]["w"] = evk
                        ekraw = dv(lambda e, b=psk: e.tensor_copy(
                            out=kraw[:, :], in_=b[:, 0:TT]),
                            waits=[evk, ev_kraw_free])
                        esq = dv(lambda e: e.tensor_tensor(out=sqt[:, :],
                                                           in0=kraw[:, :],
                                                           in1=kraw[:, :],
                                                           op=ALU.mult),
                                 waits=[ekraw, ev_sq_free])
                        yield
                        evn = mm(psk[0:2, 0:TT], bones2[:, 0:2], sqt[:, :], True,
                                 True, waits=[esq, ekraw], inc="pe")
                        bank[bname]["w"] = evn
                        elr = ac(lambda e, b=psk: e.activation(
                            out=ln1[:, :], in_=b[0:2, 0:TT], func=AF.Ln,
                            bias=eps6[:, 0:1], scale=1.0 / HD), waits=[evn])
                        err = ac(lambda e: e.activation(out=rstdf[:, :],
                                                        in_=ln1[:, :],
                                                        func=AF.Exp, scale=-0.5),
                                 waits=[elr])
                        erb = dv(lambda e: e.tensor_copy(out=rstdb[:, :],
                                                         in_=rstdf[:, :]),
                                 waits=[err])
                        yield
                        evrb = mm(psk[:, 0:TT], bc2[:, :], rstdb[:, :], True, True,
                                  waits=[erb, elr], inc="pe")
                        bank[bname]["w"] = evrb
                        eu = dv(lambda e, s=scv: e.tensor_scalar(
                            out=usc[:, :], in0=kraw[:, :], scalar1=s[:, 0:1],
                            scalar2=None, op0=ALU.mult),
                            waits=[ekraw, ev_usc_free, ev_qse])
                        ev_kraw_free = eu
                        ec1 = dv(lambda e: e.tensor_copy(out=urot[0:32, :],
                                                         in_=usc[32:64, :]),
                                 waits=[eu, ev_urot_free])
                        ec2 = dv(lambda e: e.tensor_copy(out=urot[32:64, :],
                                                         in_=usc[0:32, :]),
                                 waits=[eu])
                        ec3 = dv(lambda e: e.tensor_copy(out=urot[64:96, :],
                                                         in_=usc[96:128, :]),
                                 waits=[eu])
                        ec4 = dv(lambda e: e.tensor_copy(out=urot[96:128, :],
                                                         in_=usc[64:96, :]),
                                 waits=[eu])
                        et1 = dv(lambda e, sl=sl: e.tensor_tensor(
                            out=t1r[:, :], in0=usc[:, :], in1=ropec[sl][:, :],
                            op=ALU.mult), waits=[eu, er1, ev_t1_free])
                        et2 = dv(lambda e, sl=sl: e.tensor_tensor(
                            out=urot[:, :], in0=urot[:, :], in1=ropes[sl][:, :],
                            op=ALU.mult), waits=[ec1, ec2, ec3, ec4, er2])
                        ets = dv(lambda e: e.tensor_tensor(out=tsr[:, :],
                                                           in0=t1r[:, :],
                                                           in1=urot[:, :],
                                                           op=ALU.add),
                                 waits=[et1, et2, ev_tsr_free])
                        ev_rope_rd = ets
                        ev_usc_free = ets
                        ev_urot_free = ets
                        ev_t1_free = ets
                        ekf = dv(lambda e, b=psk: e.tensor_tensor(
                            out=kqst[:, :], in0=tsr[:, :], in1=b[:, 0:TT],
                            op=ALU.mult), waits=[ets, evrb, ev_kq_free])
                        ev_tsr_free = ekf
                        esq2 = dv(lambda e: e.tensor_tensor(out=sqt[:, :],
                                                            in0=kqst[:, :],
                                                            in1=kqst[:, :],
                                                            op=ALU.mult),
                                  waits=[ekf, evn])
                        ev_sq_free = esq2
                        yield
                        evn2 = mm(psk[0:2, 0:TT], bones2[:, 0:2], sqt[:, :], True,
                                  True, waits=[esq2, ekf], inc="pe")
                        bank[bname]["w"] = evn2
                        evre = dv(lambda e, b=psk: e.reduce_max(
                            out=redm[:, 0:1], in_=b[0:2, 0:TT], axis=AXX),
                            waits=[evn2])
                        bank[bname]["r"] = evre
                        mx = kmax if kind == "k" else qmax
                        evmx = dv(lambda e, m=mx, p=p: e.tensor_tensor(
                            out=m[:, p:p + 1], in0=m[:, p:p + 1],
                            in1=redm[:, 0:1], op=ALU.max), waits=[evre])
                        if kind == "k":
                            ev_maxk = evmx
                        else:
                            ev_maxq = evmx
                        ev_kq_free = dma("gp", "gkq", dstt[:, t0:t0 + TT],
                                         kqst[:, :], waits=[ekf])
                        yield
                    psk = ps_a
                    bname = "a"
                    jc += 1
                    mm(psk[:, 0:512], zeros[:, 0:128], zeros[:, 0:512], True, False,
                       waits=[evx, ev_wqp[ps_], bank[bname]["r"]])
                    for tc in range(TT // 128):
                        for c in range(NC):
                            mm(psk[:, tc * 128:(tc + 1) * 128],
                               xnp[sl][c][:, tc * 128:(tc + 1) * 128],
                               wqp[ps_][c][:, 256:384],
                               False, False)
                    evvc = mm(psk[:, 0:1], zeros[:, 0:128], zeros[:, 0:1], False,
                              True, inc="pe")
                    bank[bname]["w"] = evvc
                    evve = dv(lambda e, b=psk: e.tensor_copy(
                        out=vst[:, :], in_=b[:, 0:TT]),
                        waits=[evvc, ev_v_free])
                    bank[bname]["r"] = evve
                    ev_v_free = dma("gp", "gv", v_scr[p][:, t0:t0 + TT],
                                    vst[:, :], waits=[evve])
                    ev_xnp_free[sl] = evvc
                    ev_rope_free[sl] = ev_rope_rd
                    yield
                ev_wqp_free[ps_] = evvc
                ebp = dv(lambda e, p=p: e.tensor_tensor(
                    out=kmax[:, p:p + 1], in0=kmax[:, p:p + 1],
                    in1=qmax[:, p:p + 1], op=ALU.mult), waits=[ev_maxk, ev_maxq])
                ef1 = dma("gp", "gcf", cflip[p:p + 1, :], kmax[0:2, p:p + 1],
                          waits=[ebp])
                ef2 = dma("gp", "gcf", crow[:, 2 * p:2 * p + 2], cflip[p:p + 1, :],
                          waits=[ef1])
                em1 = dv(lambda e, p=p: e.reduce_max(
                    out=crow2[0:1, p:p + 1], in_=crow[0:1, 2 * p:2 * p + 2],
                    axis=AXX), waits=[ef2])
                ci = crow2.bitcast(I32)
                em2a = dv(lambda e, p=p: e.tensor_scalar(
                    out=ci[0:1, p:p + 1], in0=ci[0:1, p:p + 1], scalar1=1,
                    scalar2=None, op0=ALU.arith_shift_right), waits=[em1])
                em2_ = dv(lambda e, p=p: e.tensor_scalar(
                    out=ci[0:1, p:p + 1], in0=ci[0:1, p:p + 1],
                    scalar1=0x1FC00000, scalar2=None, op0=ALU.add),
                    waits=[em2a])
                em4 = dv(lambda e, p=p: e.tensor_scalar(
                    out=crow2[0:1, p:p + 1], in0=crow2[0:1, p:p + 1],
                    scalar1=-1.04, scalar2=72.0, op0=ALU.mult, op1=ALU.add),
                    waits=[em2_])
                psk = ps_a
                bname = "a"
                jc += 1
                ecb = mm(psk[:, 0:1], onesf[0:1, 0:128], crow2[0:1, p:p + 1],
                         True, True, waits=[em4, bank[bname]["r"]], inc="pe")
                bank[bname]["w"] = ecb
                ev_cbp = dv(lambda e, b=psk, p=p: e.tensor_copy(
                    out=cbias[:, p:p + 1], in_=b[:, 0:1]), waits=[ecb])
                bank[bname]["r"] = ev_cbp
                ev_pair[p] = [ev_kq_free, ev_v_free, ev_cbp]
                yield

        gen_prep = prep_pairs()
        ev_xf = [None, None]
        ev_xf_free = [list(ev_deq_all), list(ev_deq_all)]
        ev_xn_rd = None
        ev_t1r_free = None
        ev_con2 = ev_con
        for tt in range(NTT):
            sl = tt % 2
            t0 = tt * TT
            wx = [e for e in ev_xf_free[sl] if e]
            for c in range(NC):
                ev_xf[sl] = dma("sy", "sxf%d" % sl, xf[sl][c][:, :],
                                x_fm[c * 128:(c + 1) * 128, t0:t0 + TT],
                                waits=wx if c == 0 else ())
            ev_xf_free[sl] = []
            evxb = None
            for c in range(NC):
                wts = [ev_xf[sl]] + ([ev_xn_rd] if (c == 0 and ev_xn_rd) else [])
                evxb = dv(lambda e, c=c, sl=sl: e.tensor_copy(
                    out=xb[:, c * TT:(c + 1) * TT], in_=xf[sl][c][:, :]), waits=wts)
            evsq = dv(lambda e: e.tensor_tensor(out=xsq[:, :], in0=xb[:, :],
                                                in1=xb[:, :], op=ALU.mult),
                      waits=[evxb])
            evA = evB = None
            for c in range(NC):
                wa = ([evxb, bank["a"]["r"]]) if c == 0 else ()
                wb = ([evsq, bank["b"]["r"]]) if c == 0 else ()
                evA = mm(ps_a[0:1, 0:TT], ones1b[:, 0:1], xb[:, c * TT:(c + 1) * TT],
                         c == 0, c == NC - 1, waits=wa,
                         inc="pe" if c == NC - 1 else None)
                evB = mm(ps_b[0:1, 0:TT], ones1b[:, 0:1], xsq[:, c * TT:(c + 1) * TT],
                         c == 0, c == NC - 1, waits=wb,
                         inc="pe" if c == NC - 1 else None)
            bank["a"]["w"] = evA
            bank["b"]["w"] = evB
            em = dv(lambda e: e.tensor_scalar(out=m_row[:, :], in0=ps_a[0:1, 0:TT],
                                              scalar1=1.0 / D, scalar2=None,
                                              op0=ALU.mult), waits=[evA])
            em2 = dv(lambda e: e.tensor_tensor(out=msq[:, :], in0=m_row[:, :],
                                               in1=m_row[:, :], op=ALU.mult),
                     waits=[em])
            evr = dv(lambda e: e.scalar_tensor_tensor(out=v_row[:, :],
                                                      in0=ps_b[0:1, 0:TT],
                                                      scalar=1.0 / D, in1=msq[:, :],
                                                      op0=ALU.mult,
                                                      op1=ALU.subtract),
                     waits=[evB, em2])
            bank["a"]["r"] = em
            bank["b"]["r"] = evr
            el1 = ac(lambda e: e.activation(out=lnv[:, :], in_=v_row[:, :],
                                            func=AF.Ln, bias=eps5[:, 0:1],
                                            scale=1.0), waits=[evr, ev_con2])
            ea = ac(lambda e: e.activation(out=a_row[:, :], in_=lnv[:, :],
                                           func=AF.Exp, scale=-0.5), waits=[el1])
            eb = dv(lambda e: e.scalar_tensor_tensor(out=b_row[:, :], in0=m_row[:, :],
                                                     scalar=-1.0, in1=a_row[:, :],
                                                     op0=ALU.mult, op1=ALU.mult),
                    waits=[ea])
            eba = mm(ps_a[:, 0:TT], onesf[0:1, 0:128], a_row[:, :], True, True,
                     waits=[ea, bank["a"]["r"]], inc="pe")
            ebb = mm(ps_b[:, 0:TT], onesf[0:1, 0:128], b_row[:, :], True, True,
                     waits=[eb, bank["b"]["r"]], inc="pe")
            bank["a"]["w"] = eba
            bank["b"]["w"] = ebb
            exn = None
            ev_sp = None
            for c in range(NC):
                wts = [eba, ev_t1r_free] if c == 0 else [eba, exn]
                et = dv(lambda e, c=c, sl=sl: e.tensor_tensor(
                    out=lnt[:, :], in0=xf[sl][c][:, :], in1=ps_a[:, 0:TT],
                    op=ALU.mult),
                    waits=wts + ([ev_xn_rd] if (c == 0 and ev_xn_rd) else []))
                exn = dv(lambda e, c=c: e.tensor_tensor(
                    out=xnt[:, c * TT:(c + 1) * TT], in0=lnt[:, :],
                    in1=ps_b[:, 0:TT], op=ALU.add), waits=[ebb, et])
                ev_sp = dma("gp", "gxn", xn_scr[c * 128:(c + 1) * 128, t0:t0 + TT],
                            xnt[:, c * TT:(c + 1) * TT], waits=[exn])
            ev_t1r_free = exn
            bank["a"]["r"] = exn
            bank["b"]["r"] = exn
            ev_xn_rd = ev_sp
            ev_xf_free[sl].append(exn)
            ev_xnspill[tt] = ev_sp
            if tt >= 2:
                for _ in range(26):
                    next(gen_prep)
        bar_mid = P.barrier()

        TQ = float(Q127 / 2.0)

        def ffn_units():
            ev_xnf = [None, None]
            ev_xnf_free = [list(bar_mid), list(bar_mid)]
            ev_wg_free = [None, None]
            ev_wu_free = [None, None]
            ev_h = [None] * NHC
            ev_sg_free = None
            ev_sg1_free = None
            ev_gst_free = None
            for ft in range(NFT):
                f0 = ft * 512
                sl = ft % 2
                evs = None
                for c in range(NC):
                    w0 = ()
                    if c == 0 and ev_xnf_free[sl]:
                        w0 = (ev_xnf_free[sl] if isinstance(ev_xnf_free[sl], list)
                              else [ev_xnf_free[sl]])
                    evs = dma("sy", "sxn%d" % sl, xnf[sl][c][:, :],
                              xn_scr[c * 128:(c + 1) * 128, f0:f0 + 512], waits=w0)
                ev_xnf[sl] = evs
                yield
                for hh in range(NHC):
                    ws = hh % 2
                    eg = dma("sy", "swg%d" % ws, wgb[ws][:, :], wg_bf[hh],
                             waits=[ev_wg_free[ws], ev_deq[2 * hh]])
                    eu_ = dma("sy", "swu%d" % ws, wub[ws][:, :], wu_bf[hh],
                              waits=[ev_wu_free[ws], ev_deq[2 * hh + 1]])
                    yield
                    evg = evu = None
                    for c in range(NC):
                        wg_ = [eg, ev_xnf[sl], bank["b"]["r"]] if c == 0 else ()
                        evg = mm(ps_b[:, :], wgb[ws][:, c * 128:(c + 1) * 128],
                                 xnf[sl][c][:, :], c == 0, c == NC - 1, waits=wg_,
                                 inc="pe" if c == NC - 1 else None)
                    bank["b"]["w"] = evg
                    ev_wg_free[ws] = evg
                    ee0 = dv(lambda e: e.tensor_scalar(
                        out=sg1[:, :], in0=ps_b[:, :], scalar1=-Q127, scalar2=43.0,
                        op0=ALU.mult, op1=ALU.min), waits=[evg, ev_sg1_free])
                    ee1 = ac(lambda e: e.activation(out=sgate[:, :], in_=sg1[:, :],
                                                    func=AF.Exp),
                             waits=[ee0, ev_sg_free])
                    ee2 = ac(lambda e: e.activation(out=sg1[:, :], in_=sgate[:, :],
                                                    func=AF.Ln, bias=onebias[:, 0:1],
                                                    scale=1.0),
                             waits=[ee1])
                    ee3 = ac(lambda e: e.activation(out=sgate[:, :], in_=sg1[:, :],
                                                    func=AF.Exp, scale=-1.0),
                             waits=[ee2])
                    et0 = dv(lambda e: e.scalar_tensor_tensor(
                        out=sg1[:, :], in0=ps_b[:, :], scalar=Q127 * Q127,
                        in1=sgate[:, :], op0=ALU.mult, op1=ALU.mult),
                        waits=[ee3, evg])
                    ev_sg1_free = et0
                    bank["b"]["r"] = et0
                    yield
                    for c in range(NC):
                        wu2 = [eu_, ev_xnf[sl], et0] if c == 0 else ()
                        evu = mm(ps_b[:, :], wub[ws][:, c * 128:(c + 1) * 128],
                                 xnf[sl][c][:, :], c == 0, c == NC - 1, waits=wu2,
                                 inc="pe" if c == NC - 1 else None)
                    bank["b"]["w"] = evu
                    ev_wu_free[ws] = evu
                    evh = dv(lambda e, hh=hh: e.tensor_tensor(
                        out=hbuf[hh][:, :], in0=ps_b[:, :], in1=sg1[:, :],
                        op=ALU.mult), waits=[et0, evu, ev_h[hh]])
                    ev_sg_free = evh
                    ev_sg1_free = evh
                    ev_h[hh] = evh
                    bank["b"]["r"] = evh
                    yield
                ev_wd_free = [ev_wg_free[0], ev_wg_free[1], ev_wu_free[0]]
                evd_last = None
                for dd in range(NC):
                    psd = ps_b
                    bname = "b"
                    wdl = []
                    for seg in range(3):
                        buf = (wgb[0], wgb[1], wub[0])[seg]
                        wsem = ("swg0", "swg1", "swu0")[seg]
                        wdl.append((buf, dma("sy", wsem, buf[:, :],
                                             wd_bf[dd][:, seg * 1024:(seg + 1) * 1024],
                                             waits=[ev_wd_free[seg],
                                                    ev_deq[2 * NHC + dd]])))
                    yield
                    evd = None
                    for hh in range(NHC):
                        seg, off = divmod(hh, 8)
                        w0 = ([wdl[0][1], wdl[1][1], wdl[2][1], ev_h[NHC - 1],
                               bank[bname]["r"]] if hh == 0 else ())
                        evd = mm(psd[:, :], wdl[seg][0][:, off * 128:(off + 1) * 128],
                                 hbuf[hh][:, :], hh == 0, hh == NHC - 1, waits=w0,
                                 inc="pe" if hh == NHC - 1 else None)
                    ev_wd_free = [evd, evd, evd]
                    bank[bname]["w"] = evd
                    evd_last = evd
                    ef0 = dv(lambda e, psd=psd: e.tensor_scalar(
                        out=sg1[:, :], in0=psd[:, :], scalar1=-Q127, scalar2=43.0,
                        op0=ALU.mult, op1=ALU.min), waits=[evd, ev_sg1_free])
                    ef1 = ac(lambda e: e.activation(
                        out=sgate[:, :], in_=sg1[:, :], func=AF.Exp),
                        waits=[ef0, ev_sg_free])
                    ef2 = ac(lambda e: e.activation(out=sg1[:, :], in_=sgate[:, :],
                                                    func=AF.Ln, bias=onebias[:, 0:1],
                                                    scale=1.0),
                             waits=[ef1])
                    es2 = ac(lambda e: e.activation(out=sgate[:, :], in_=sg1[:, :],
                                                    func=AF.Exp, scale=-1.0),
                             waits=[ef2])
                    ev_sg1_free = es2
                    evgb = dv(lambda e, psd=psd: e.scalar_tensor_tensor(
                        out=gst[:, :], in0=psd[:, :], scalar=Q127,
                        in1=sgate[:, :], op0=ALU.mult, op1=ALU.mult),
                        waits=[es2, ev_gst_free])
                    ev_sg_free = evgb
                    ev_sg1_free = evgb
                    bank[bname]["r"] = evgb
                    ev_gst_free = dma("gp", "gg", g_scr[dd][:, f0:f0 + 512],
                                      gst[:, :], waits=[evgb])
                    yield

                for hh in range(NHC):
                    ev_h[hh] = evd_last
                ev_xnf_free[sl] = evd_last
                ev_wg_free = [evd_last, evd_last]
                ev_wu_free = [evd_last, evd_last]

        gen_ffn = ffn_units()
        while ev_pair[0] is None:
            next(gen_prep)
        npump = [0]
        gens = [gen_prep, gen_ffn]

        def pump():
            npump[0] += 1
            g = gens[npump[0] % 2]
            try:
                next(g)
            except StopIteration:
                try:
                    next(gens[(npump[0] + 1) % 2])
                except StopIteration:
                    pass
        ev_kvqf = None
        ev_pfree = [None, None, None]
        ev_ost_free = None
        ev_zt_free = None
        ev_zbs_free = None
        ev_oraw_free = [None, None]
        st = {"zbs_free": None, "ost_free": None, "oraw_free": [None, None],
              "zt_free": None}

        def pr6(i):
            j = i % 6
            return pring[j // 2][:, (j % 2) * 1024:((j % 2) + 1) * 1024]

        for p in range(NP):
            sl = p % 2
            while ev_pair[p] is None:
                next(gen_prep)
            wld = ev_pair[p] + ([ev_kvqf] if ev_kvqf else [])
            ek = dma("sy", "skv%d" % sl, kb[:, 0:NTOT], kt_scr[p], waits=wld)
            eq = dma("sy", "skv%d" % sl, qb[:, 0:NOWN], q_scr[p])
            evv = dma("sy", "skv%d" % sl, vb[:, 0:NTOT], v_scr[p])
            ek = eq = evv
            ev_sexp = [None, None]
            pfree = [[] for _ in range(6)]
            pend_flush = [None]
            for qt in range(NQT):
                q0 = qt * QT
                mm(ps_pv[:, 0:QT], zeros[:, 0:128], zeros[:, 0:QT], True, False,
                   waits=[ek, bank["pv"]["r"]])
                ev_exp = [None] * NCH

                def emit_pv(c):
                    slot = pr6(c)
                    mm(ps_pv[0:64, 0:QT], vb[:, c * 128:c * 128 + 64],
                       slot[:, 0:QT], False, False, tp=(0, 0), waits=[ev_exp[c]])
                    ev = mm(ps_pv[64:128, 0:QT], vb[:, c * 128 + 64:(c + 1) * 128],
                            slot[:, 512:512 + QT], False, False, tp=(0, 64),
                            inc="pe")
                    pfree[c % 6].append(ev)

                def emit_z(c0):
                    sa, sb_ = pr6(c0), pr6(c0 + 1)
                    mm(ps_z[0:1, 0:QT], onesb[:, 0:1], sa[:, 0:QT], False, False,
                       tp=(0, 0), waits=[ev_exp[c0 + 1]])
                    mm(ps_z[32:33, 0:QT], onesb[:, 1:2], sa[:, 512:512 + QT],
                       False, False, tp=(0, 32))
                    mm(ps_z[64:65, 0:QT], onesb[:, 2:3], sb_[:, 0:QT],
                       False, False, tp=(0, 64))
                    ev = mm(ps_z[96:97, 0:QT], onesb[:, 3:4], sb_[:, 512:512 + QT],
                            False, False, tp=(0, 96), inc="pe")
                    pfree[c0 % 6].append(ev)
                    pfree[(c0 + 1) % 6].append(ev)

                for c in range(NCH):
                    hf = c % 2
                    so = hf * 1024
                    mm(ps_s[:, so:so + QT], kb[0:64, c * 128:(c + 1) * 128],
                       qb[0:64, q0:q0 + QT], True, True, tp=(0, 0),
                       waits=[ek, ev_sexp[hf]])
                    ev_s = mm(ps_s[:, so + 512:so + 512 + QT],
                              kb[64:128, c * 128:(c + 1) * 128],
                              qb[64:128, q0:q0 + QT], True, True, tp=(64, 0),
                              inc="pe")
                    ev_exp[c] = ac(lambda e, c=c, so=so, p=p: e.activation(
                        out=pr6(c), in_=ps_s[:, so:so + 1024], func=AF.Exp,
                        bias=cbias[:, p:p + 1], scale=1.0),
                        waits=[ev_s] + pfree[c % 6])
                    pfree[c % 6] = []
                    ev_sexp[hf] = ev_exp[c]
                    if c == 2:
                        if pend_flush[0] is not None:
                            pend_flush[0]()
                            pend_flush[0] = None
                        mm(ps_z[:, 0:QT], zeros[:, 0:128], zeros[:, 0:QT], True,
                           False, waits=[bank["z"]["r"]])
                    if c >= 2:
                        emit_pv(c - 2)
                    if c >= 3 and c % 2 == 1:
                        emit_z(c - 3)
                    pump()
                emit_pv(NCH - 2)
                emit_pv(NCH - 1)
                emit_z(NCH - 2)
                evc1 = mm(ps_pv[:, 0:1], zeros[:, 0:128], zeros[:, 0:1], False,
                          True, inc="pe")
                evc2 = mm(ps_z[:, 0:1], zeros[:, 0:128], zeros[:, 0:1], False,
                          True, inc="pe")
                bank["pv"]["w"] = evc1
                bank["z"]["w"] = evc2
                osl = qt % 2
                eor = dv(lambda e, osl=osl: e.tensor_copy(out=oraw[osl][:, :],
                                                          in_=ps_pv[:, 0:QT]),
                         waits=[evc1, st["oraw_free"][osl]])
                bank["pv"]["r"] = eor
                ez0 = dv(lambda e: e.tensor_copy(out=zt[:, :], in_=ps_z[:, 0:QT]),
                         waits=[evc2, st["zt_free"]])
                bank["z"]["r"] = ez0
                ec1 = dv(lambda e: e.tensor_copy(out=zc1[:, :], in_=zt[64:65, :]),
                         waits=[ez0])
                ec2 = dv(lambda e: e.tensor_copy(out=zc2[:, :], in_=zt[96:97, :]),
                         waits=[ez0])
                ec3 = dv(lambda e: e.tensor_copy(out=zc3[:, :], in_=zt[32:33, :]),
                         waits=[ez0])
                ea1 = dv(lambda e: e.tensor_tensor(out=zrow[:, 0:QT],
                                                   in0=zt[0:1, :], in1=zc1[:, :],
                                                   op=ALU.add), waits=[ec1])
                ea2 = dv(lambda e: e.tensor_tensor(out=zrow[:, QT:2 * QT],
                                                   in0=zc3[:, :], in1=zc2[:, :],
                                                   op=ALU.add), waits=[ec2, ec3])
                st["zt_free"] = ea2
                eg_ = dv(lambda e: e.tensor_scalar(out=zln[:, :], in0=zrow[:, :],
                                                   scalar1=1e-30, scalar2=None,
                                                   op0=ALU.add), waits=[ea1, ea2])
                ei = dv(lambda e: e.reciprocal(out=zi[:, :], in_=zln[:, :]),
                        waits=[eg_])
                eib = dv(lambda e: e.tensor_copy(out=zib[:, :], in_=zi[:, :]),
                         waits=[ei])

                def flush(p=p, qt=qt, q0=q0, eib=eib, osl=osl, eor=eor):
                    mm(ps_z[:, 0:QT], zeros[:, 0:128], zeros[:, 0:QT], True, False,
                       waits=[eib, bank["z"]["r"]])
                    mm(ps_z[0:64, 0:QT], onesfh[0:1, 0:64], zib[:, 0:QT],
                       False, False, tp=(0, 0))
                    mm(ps_z[64:128, 0:QT], onesfh[0:1, 0:64], zib[:, QT:2 * QT],
                       False, False, tp=(0, 64))
                    ezb = mm(ps_z[:, 0:1], zeros[:, 0:128], zeros[:, 0:1], False,
                             True, inc="pe")
                    bank["z"]["w"] = ezb
                    ezbs = dv(lambda e: e.tensor_copy(out=zbs[:, :],
                                                      in_=ps_z[:, 0:QT]),
                              waits=[ezb, st["zbs_free"]])
                    bank["z"]["r"] = ezbs
                    eo = dv(lambda e, osl=osl: e.tensor_tensor(
                        out=ost[:, :], in0=oraw[osl][:, :], in1=zbs[:, :],
                        op=ALU.mult), waits=[eor, ezbs, st["ost_free"]])
                    st["zbs_free"] = eo
                    st["oraw_free"][osl] = eo
                    st["ost_free"] = dma("gp", "go", o_scr[p][:, q0:q0 + QT],
                                         ost[:, :], waits=[eo])

                pend_flush[0] = flush
            pend_flush[0]()
            pend_flush[0] = None
            ev_kvqf = evc1

        for g in gens:
            for _ in g:
                pass

        bar_pr = P.barrier()
        ev_ob = []
        for dd in range(NC):
            ev_ob.append(dma("sy", "sob", ob[dd][:, :], o_scr[dd],
                             waits=bar_pr if dd == 0 else ()))
        ev_c1 = ev_c2 = None
        for dd in range(NC):
            e1_ = dma("sy", "swp0", xcb[0][:, :],
                      w_proj[dd * 128:(dd + 1) * 128, 0:512],
                      waits=[ev_c1] if ev_c1 else bar_pr)
            e2_ = dma("sy", "swp1", xcb[1][:, :],
                      w_proj[dd * 128:(dd + 1) * 128, 512:1024],
                      waits=[ev_c2] if ev_c2 else bar_pr)
            ev_c1 = dv(lambda e, dd=dd: e.tensor_copy(out=wpb[dd][:, 0:512],
                                                      in_=xcb[0][:, :]), waits=[e1_])
            ev_c2 = dv(lambda e, dd=dd: e.tensor_copy(out=wpb[dd][:, 512:1024],
                                                      in_=xcb[1][:, :]), waits=[e2_])
        ev_wp = ev_c2
        prb = ["s0", "s1", "s2", "s3"]
        bi = 0
        ev_gc = [None, None]
        ev_xc = [ev_c1, ev_c2]
        ev_otf = [None, None]
        ev_sg = None
        for oo in range(NC):
            ev_sfm = None
            for qt in range(NQT):
                bnk = prb[bi % 4]
                bi += 1
                evm = None
                for dd in range(NC):
                    w0 = ([ev_wp, ev_c1, ev_ob[-1], bank[bnk]["r"]]
                          if dd == 0 else ())
                    evm = mm(SBANK[bnk][:, 0:QT], wpb[dd][:, oo * 128:(oo + 1) * 128],
                             ob[dd][:, qt * QT:(qt + 1) * QT], dd == 0, dd == NC - 1,
                             waits=w0, inc="pe" if dd == NC - 1 else None)
                bank[bnk]["w"] = evm
                ev_sfm = dv(lambda e, b=bnk, oo=oo, qt=qt: e.tensor_scalar(
                    out=sfm[oo][:, qt * QT:(qt + 1) * QT], in0=SBANK[b][:, 0:QT],
                    scalar1=bpt[:, oo:oo + 1], scalar2=None, op0=ALU.add),
                    waits=[evm, e3])
                bank[bnk]["r"] = ev_sfm
            for ft in range(NFT):
                f0 = ft * 512
                slot = (oo * NFT + ft) % 2
                eg = dma("sy", "sge%d" % slot, gcb[slot][:, :],
                         g_scr[oo][:, f0:f0 + 512], waits=[ev_gc[slot]])
                ex = dma("sy", "sxe%d" % slot, xcb[slot][:, :],
                         x_fm[oo * 128:(oo + 1) * 128, f0:f0 + 512],
                         waits=[ev_xc[slot]])
                esg2 = dv(lambda e, oo=oo, f0=f0, slot=slot: e.tensor_tensor(
                    out=sgc[:, :], in0=sfm[oo][:, f0:f0 + 512], in1=gcb[slot][:, :],
                    op=ALU.mult), waits=[eg, ev_sfm, ev_sg])
                ev_gc[slot] = esg2
                eot = dv(lambda e, slot=slot: e.tensor_tensor(
                    out=otc[slot][:, :], in0=sgc[:, :], in1=xcb[slot][:, :],
                    op=ALU.add), waits=[esg2, ex, ev_otf[slot]])
                ev_sg = eot
                ev_xc[slot] = eot
                ev_otf[slot] = dma("sy", "sout%d" % slot,
                                   out_fm[oo * 128:(oo + 1) * 128, f0:f0 + 512],
                                   otc[slot][:, :], waits=[eot])

        P.emit("gp", lambda e: None,
               waits=[("sout0", P.cnt["sout0"]), ("sout1", P.cnt["sout1"])])

        block = E(nc.Block())

        @block.sync
        def _(sy):
            for f in P.cmds["sy"]:
                f(sy)

        @block.gpsimd
        def _(gp):
            for f in P.cmds["gp"]:
                f(gp)

        @block.tensor
        def _(te):
            for f in P.cmds["pe"]:
                f(te)

        @block.vector
        def _(ve):
            for f in P.cmds["dv"]:
                f(ve)

        @block.scalar
        def _(sc):
            for f in P.cmds["ac"]:
                f(sc)

    return nc


def _bc2_const():
    import ml_dtypes
    m = np.zeros((2, 128), dtype=ml_dtypes.bfloat16)
    m[0, 0:64] = 1
    m[1, 64:128] = 1
    return m


def _rope_tables(npos):
    half = 32
    inv = (np.float32(1.0) /
           np.power(np.float32(10000.0),
                    np.arange(half, dtype=np.float32) / np.float32(half)))
    t = np.arange(npos, dtype=np.float32)
    f = np.outer(t, inv).astype(np.float32)
    cos = np.cos(f).T.astype(np.float32)
    sin = np.sin(f).T.astype(np.float32)
    cos2x = np.ascontiguousarray(np.tile(cos, (4, 1)))
    sinS = np.ascontiguousarray(np.concatenate([-sin, sin, -sin, sin], axis=0))
    return cos2x, sinS


def kernel(L0_attn_x, L0_attn_w_qkv, L0_attn_q_scale, L0_attn_k_scale,
           L0_attn_w_proj, L0_attn_b_proj, L0_ffn_w_gate, L0_ffn_w_up,
           L0_ffn_w_down):
    x = np.asarray(L0_attn_x)
    B, N, C = x.shape
    NOWN = N // 2
    cos2x, sinS = _rope_tables(N)
    w_qkv = np.ascontiguousarray(L0_attn_w_qkv, dtype=np.float32)
    w_proj = np.ascontiguousarray(L0_attn_w_proj, dtype=np.float32)
    b_projt = np.ascontiguousarray(
        np.asarray(L0_attn_b_proj, dtype=np.float32).reshape(NC, 128).T)
    qs2x = np.ascontiguousarray(
        np.concatenate([L0_attn_q_scale, L0_attn_q_scale]).astype(np.float32)
        .reshape(128, 1))
    ks2x = np.ascontiguousarray(
        np.concatenate([L0_attn_k_scale, L0_attn_k_scale]).astype(np.float32)
        .reshape(128, 1))
    wgT = np.ascontiguousarray(np.asarray(L0_ffn_w_gate, dtype=np.int32).T)
    wuT = np.ascontiguousarray(np.asarray(L0_ffn_w_up, dtype=np.int32).T)
    wdT = np.ascontiguousarray(np.asarray(L0_ffn_w_down, dtype=np.int32).T)

    nc = build_program(NTOT=N, NOWN=NOWN)

    in_maps = []
    for core in range(8):
        b, h = core // 2, core % 2
        xt = np.ascontiguousarray(x[b].T.astype(np.float32))
        if h == 0:
            x_c, rc, rs = xt, cos2x, sinS
        else:
            x_c = np.ascontiguousarray(
                np.concatenate([xt[:, NOWN:], xt[:, :NOWN]], axis=1))
            rc = np.ascontiguousarray(
                np.concatenate([cos2x[:, NOWN:], cos2x[:, :NOWN]], axis=1))
            rs = np.ascontiguousarray(
                np.concatenate([sinS[:, NOWN:], sinS[:, :NOWN]], axis=1))
        in_maps.append(dict(x_fm=x_c, w_qkv=w_qkv, w_proj=w_proj,
                            b_projt=b_projt, qs2x=qs2x, ks2x=ks2x,
                            wgT=wgT, wuT=wuT, wdT=wdT, ropec=rc, ropes=rs,
                            bc2c=_bc2_const()))

    res = run_bass_kernel_spmd(nc, in_maps, core_ids=list(range(8)))
    out = np.empty((B, N, C), dtype=np.float32)
    for core in range(8):
        b, h = core // 2, core % 2
        out[b, h * NOWN:(h + 1) * NOWN, :] = np.asarray(
            res.results[core]["out_fm"]).T
    return out


# version 16
# speedup vs baseline: 1.0465x; 1.0457x over previous
import numpy as np
import concourse.bass as bass
import concourse.mybir as mybir
from concourse.bass_utils import run_bass_kernel_spmd

F32 = mybir.dt.float32
BF16 = mybir.dt.bfloat16
I32 = mybir.dt.int32
AF = mybir.ActivationFunctionType
ALU = mybir.AluOpType
AXX = mybir.AxisListType.X

D = 1024
NH = 16
HD = 64
HID = 3072
NP = 8
NC = 8
NHC = HID // 128
Q127 = float(1.0 / 127.0)

ENGS = ("pe", "dv", "ac", "sy", "gp")


class Plan:
    def __init__(self, sems):
        self.sems = sems
        self.cnt = {k: 0 for k in sems}
        self.cmds = {e: [] for e in ENGS}
        self.last_wait = {e: {} for e in ENGS}

    def emit(self, eng, fn, waits=(), inc=None, amt=None):
        merged = {}
        for w in waits:
            if w is None:
                continue
            s, v = w
            if v > merged.get(s, 0):
                merged[s] = v
        nw = []
        lw = self.last_wait[eng]
        for s, v in merged.items():
            if v <= 0 or lw.get(s, 0) >= v:
                continue
            lw[s] = v
            nw.append((s, v))
        a = None
        if inc is not None:
            a = amt if amt is not None else (16 if inc not in ("pe", "dv", "ac") else 1)
            self.cnt[inc] += a
        ev = (inc, self.cnt[inc]) if inc is not None else None
        sems = self.sems

        def closure(eng_obj, fn=fn, nw=tuple(nw), inc=inc, a=a):
            for s, v in nw:
                eng_obj.wait_ge(sems[s], v)
            r = fn(eng_obj)
            if inc is not None:
                r.then_inc(sems[inc], a)
        self.cmds[eng].append(closure)
        return ev

    def barrier(self):
        return [(s, self.cnt[s]) for s in self.sems]


def build_program(NTOT=8192, NOWN=4096, TT=512, QT=512):
    nc = bass.Bass("TRN2", target_bir_lowering=False)
    NTT = NTOT // TT
    NTTO = NOWN // TT
    NQT = NOWN // QT
    NCH = NTOT // 128
    NFT = NOWN // 512

    x_fm = nc.dram_tensor("x_fm", [D, NTOT], F32, kind="ExternalInput").ap()
    w_qkv = nc.dram_tensor("w_qkv", [D, 3 * D], F32, kind="ExternalInput").ap()
    w_proj = nc.dram_tensor("w_proj", [D, D], F32, kind="ExternalInput").ap()
    b_projt = nc.dram_tensor("b_projt", [128, NC], F32, kind="ExternalInput").ap()
    qs2x_in = nc.dram_tensor("qs2x", [128, 1], F32, kind="ExternalInput").ap()
    ks2x_in = nc.dram_tensor("ks2x", [128, 1], F32, kind="ExternalInput").ap()
    wgT = nc.dram_tensor("wgT", [D, HID], I32, kind="ExternalInput").ap()
    wuT = nc.dram_tensor("wuT", [D, HID], I32, kind="ExternalInput").ap()
    wdT = nc.dram_tensor("wdT", [HID, D], I32, kind="ExternalInput").ap()
    ropec_in = nc.dram_tensor("ropec", [128, NTOT], F32, kind="ExternalInput").ap()
    ropes_in = nc.dram_tensor("ropes", [128, NTOT], F32, kind="ExternalInput").ap()
    bc2_in = nc.dram_tensor("bc2c", [2, 128], BF16, kind="ExternalInput").ap()

    out_fm = nc.dram_tensor("out_fm", [D, NOWN], F32, kind="ExternalOutput").ap()

    wg_bf = nc.dram_tensor("wg_bf", [NHC, 128, NC * 128], BF16, kind="Internal").ap()
    wu_bf = nc.dram_tensor("wu_bf", [NHC, 128, NC * 128], BF16, kind="Internal").ap()
    wd_bf = nc.dram_tensor("wd_bf", [NC, 128, HID], BF16, kind="Internal").ap()
    xn_scr = nc.dram_tensor("xn_scr", [D, NTOT], BF16, kind="Internal").ap()
    wqkv_scr = nc.dram_tensor("wqkv_scr", [NC, 128, 3 * D], BF16, kind="Internal").ap()
    kt_scr = nc.dram_tensor("kt_scr", [NP, 128, NTOT], BF16, kind="Internal").ap()
    q_scr = nc.dram_tensor("q_scr", [NP, 128, NOWN], BF16, kind="Internal").ap()
    v_scr = nc.dram_tensor("v_scr", [NP, 128, NTOT], BF16, kind="Internal").ap()
    o_scr = nc.dram_tensor("o_scr", [NP, 128, NOWN], BF16, kind="Internal").ap()
    g_scr = nc.dram_tensor("g_scr", [NC, 128, NOWN], BF16, kind="Internal").ap()
    cflip = nc.dram_tensor("cflip", [NP, 2], F32, kind="Internal").ap()

    from contextlib import ExitStack
    with ExitStack() as es:
        E = es.enter_context
        zeros = E(nc.sbuf_tensor("zeros", [128, 512], BF16))
        onesb = E(nc.sbuf_tensor("onesb", [128, 4], BF16))
        onesf = E(nc.sbuf_tensor("onesf", [128, 128], F32))
        ones1b = E(nc.sbuf_tensor("ones1b", [128, 1], BF16))
        onesfh = E(nc.sbuf_tensor("onesfh", [128, 64], BF16))
        bones2 = E(nc.sbuf_tensor("bones2", [128, 2], BF16))
        bc2 = E(nc.sbuf_tensor("bc2", [2, 128], BF16))
        eps5 = E(nc.sbuf_tensor("eps5", [1, 1], F32))
        epsz = E(nc.sbuf_tensor("epsz", [1, 1], F32))
        bln64 = E(nc.sbuf_tensor("bln64", [1, 1], F32))
        eps6 = E(nc.sbuf_tensor("eps6", [2, 1], F32))
        onebias = E(nc.sbuf_tensor("onebias", [128, 1], F32))
        qs2x = E(nc.sbuf_tensor("qs2xt", [128, 1], F32))
        ks2x = E(nc.sbuf_tensor("ks2xt", [128, 1], F32))
        qse = E(nc.sbuf_tensor("qse", [128, 1], F32))
        bpt = E(nc.sbuf_tensor("bpt", [128, NC], F32))
        cbias = E(nc.sbuf_tensor("cbias", [128, NP], F32))
        kmax = E(nc.sbuf_tensor("kmax", [2, NP], F32))
        qmax = E(nc.sbuf_tensor("qmax", [2, NP], F32))
        crow = E(nc.sbuf_tensor("crow", [1, 2 * NP], F32))
        crow2 = E(nc.sbuf_tensor("crow2", [1, NP], F32))
        m_row = E(nc.sbuf_tensor("m_row", [1, TT], F32))
        msq = E(nc.sbuf_tensor("msq", [1, TT], F32))
        v_row = E(nc.sbuf_tensor("v_row", [1, TT], F32))
        lnv = E(nc.sbuf_tensor("lnv", [1, TT], F32))
        a_row = E(nc.sbuf_tensor("a_row", [1, TT], F32))
        b_row = E(nc.sbuf_tensor("b_row", [1, TT], F32))
        ln1 = E(nc.sbuf_tensor("ln1", [2, TT], F32))
        rstdf = E(nc.sbuf_tensor("rstdf", [2, TT], F32))
        rstdb = E(nc.sbuf_tensor("rstdb", [2, TT], BF16))
        redm = E(nc.sbuf_tensor("redm", [2, 1], F32))
        zt = E(nc.sbuf_tensor("zt", [128, QT], F32))
        zc1 = E(nc.sbuf_tensor("zc1", [1, QT], F32))
        zc2 = E(nc.sbuf_tensor("zc2", [1, QT], F32))
        zc3 = E(nc.sbuf_tensor("zc3", [1, QT], F32))
        zrow = E(nc.sbuf_tensor("zrow", [1, 2 * QT], F32))
        zln = E(nc.sbuf_tensor("zln", [1, 2 * QT], F32))
        zi = E(nc.sbuf_tensor("zi", [1, 2 * QT], F32))
        zib = E(nc.sbuf_tensor("zib", [1, 2 * QT], BF16))
        zbs = E(nc.sbuf_tensor("zbs", [128, QT], BF16))
        oraw = [E(nc.sbuf_tensor("oraw0", [128, QT], F32)),
                E(nc.sbuf_tensor("oraw1", [128, QT], F32))]

        ABIG = 82624
        big = E(nc.sbuf_tensor("big", [128, ABIG], BF16))

        class Arena:
            def __init__(self, start=0):
                self.off = start

            def a(self, cols, dt=BF16):
                n = cols if dt == BF16 else 2 * cols
                self.off = (self.off + 15) // 16 * 16
                o = self.off
                self.off += n
                assert self.off <= ABIG, f"arena overflow {self.off}"
                sl = big[:, o:o + n]
                return sl.bitcast(dt) if dt != BF16 else sl

        ar4 = Arena()
        kb = ar4.a(NTOT)
        vb = ar4.a(NTOT)
        qb = ar4.a(NOWN)
        pring = [ar4.a(2048) for _ in range(3)]
        ost = ar4.a(QT)
        assert ar4.off <= 27200, ar4.off
        ar3 = Arena(27200)
        xnf = [[ar3.a(512) for _ in range(NC)] for _ in range(2)]
        wgb = [ar3.a(NC * 128) for _ in range(2)]
        wub = [ar3.a(NC * 128) for _ in range(2)]
        hbuf = [ar3.a(512) for _ in range(NHC)]
        sgate = ar3.a(512, F32)
        sg1 = ar3.a(512, F32)
        gst = ar3.a(512)
        assert ar3.off <= 54400, ar3.off
        arp = Arena(54400)
        xnp = [[arp.a(TT) for _ in range(NC)] for _ in range(2)]
        wqp = [[arp.a(384) for _ in range(NC)] for _ in range(2)]
        ropec = [arp.a(TT, F32) for _ in range(2)]
        ropes = [arp.a(TT, F32) for _ in range(2)]
        kraw = arp.a(TT, F32)
        usc = arp.a(TT, F32)
        urot = arp.a(TT, F32)
        t1r = arp.a(TT, F32)
        tsr = arp.a(TT, F32)
        sqt = arp.a(TT)
        kqst = arp.a(TT)
        vst = arp.a(TT)
        assert arp.off <= ABIG, arp.off
        arl = Arena(0)
        xf = [[arl.a(TT, F32) for _ in range(NC)] for _ in range(2)]
        xb = arl.a(NC * TT)
        xsq = arl.a(NC * TT)
        xnt = arl.a(NC * TT)
        lnt = arl.a(TT, F32)
        assert arl.off <= 31000, arl.off
        stg = [big[:, 31008:31008 + 6144].bitcast(I32),
               big[:, 37152:37152 + 6144].bitcast(I32)]
        stgf = [big[:, 31008:31008 + 6144].bitcast(F32),
                big[:, 37152:37152 + 6144].bitcast(F32)]
        bst = [big[:, 43296:43296 + 3072], big[:, 46368:46368 + 3072]]
        ar5 = Arena(0)
        ob = [ar5.a(NOWN) for _ in range(NC)]
        wpb = [ar5.a(D) for _ in range(NC)]
        sfm = [ar5.a(NOWN) for _ in range(NC)]
        gcb = [ar5.a(512) for _ in range(2)]
        xcb = [ar5.a(512, F32) for _ in range(2)]
        sgc = ar5.a(512)
        otc = [ar5.a(512, F32) for _ in range(2)]
        assert ar5.off <= ABIG, ar5.off

        ps_s = E(nc.psum_tensor("ps_s", [128, 2048], F32))
        ps_pv = E(nc.psum_tensor("ps_pv", [128, 512], F32))
        ps_z = E(nc.psum_tensor("ps_z", [128, 512], F32))
        ps_a = E(nc.psum_tensor("ps_a", [128, 512], F32))
        ps_b = E(nc.psum_tensor("ps_b", [128, 512], F32))
        SBANK = {"s0": ps_s[:, 0:512], "s1": ps_s[:, 512:1024],
                 "s2": ps_s[:, 1024:1536], "s3": ps_s[:, 1536:2048]}

        DMA_SEMS = ("scst", "swq0", "swq1", "sxf0", "sxf1", "sxn0", "sxn1",
                    "swg0", "swg1", "swu0", "swu1", "sob", "swp0", "swp1",
                    "skv0", "skv1", "sge0", "sge1", "sxe0", "sxe1", "sxp0", "sxp1",
                    "sout0", "sout1", "gkq", "gv", "gxn", "gbst0", "gbst1",
                    "gg", "go", "gcf")
        sems = {s: E(nc.semaphore(s)) for s in ("pe", "dv", "ac") + DMA_SEMS}
        P = Plan(sems)

        def dma(q, sem, dst, src, waits=()):
            return P.emit(q, lambda e, d=dst, s=src: e.dma_start(out=d, in_=s),
                          waits=waits, inc=sem)

        def mm(out, lhsT, rhs, start, stop, tp=None, waits=(), inc=None):
            def f(e, o=out, l=lhsT, r=rhs, st=start, sp=stop, tp=tp):
                return e.matmul(o, l, r, start=st, stop=sp, tile_position=tp)
            return P.emit("pe", f, waits=waits, inc=inc)

        def dv(fn, waits=()):
            return P.emit("dv", fn, waits=waits, inc="dv")

        def ac(fn, waits=()):
            return P.emit("ac", fn, waits=waits, inc="ac")

        bank = {k: {"w": None, "r": None} for k in
                ("s0", "s1", "s2", "s3", "pv", "z", "a", "b")}

        dv(lambda e: e.memset(zeros[:, :], 0.0))
        dv(lambda e: e.memset(onesb[:, :], 1.0))
        dv(lambda e: e.memset(onesf[:, :], 1.0))
        dv(lambda e: e.memset(ones1b[:, :], 1.0))
        dv(lambda e: e.memset(onesfh[:, :], 1.0))
        dv(lambda e: e.memset(bones2[0:64, 0:1], 1.0))
        dv(lambda e: e.memset(bones2[64:128, 0:1], 0.0))
        dv(lambda e: e.memset(bones2[0:64, 1:2], 0.0))
        dv(lambda e: e.memset(bones2[64:128, 1:2], 1.0))
        pass
        dv(lambda e: e.memset(eps5[:, :], 1e-5))
        dv(lambda e: e.memset(epsz[:, :], 1e-38))
        dv(lambda e: e.memset(bln64[:, :], -44.3614195558365))
        dv(lambda e: e.memset(eps6[:, :], 1e-6))
        dv(lambda e: e.memset(onebias[:, :], 1.0))
        dv(lambda e: e.memset(kmax[:, :], 0.0))
        ev_con = dv(lambda e: e.memset(qmax[:, :], 0.0))
        dma("sy", "scst", qs2x[:, :], qs2x_in)
        dma("sy", "scst", ks2x[:, :], ks2x_in)
        dma("sy", "scst", bpt[:, :], b_projt)
        e_cst = dma("sy", "scst", bc2[:, :], bc2_in)
        e1 = e2 = e3 = e_cst
        ev_qse = dv(lambda e: e.tensor_scalar(out=qse[:, :], in0=qs2x[:, :],
                                              scalar1=0.125, scalar2=None,
                                              op0=ALU.mult), waits=[e1])

        ev_stg = [None, None]
        ev_bst0 = [None, None]
        for c in range(NC):
            sl = c % 2
            el = dma("sy", "swq%d" % sl, stgf[sl][:, 0:3 * D],
                     w_qkv[c * 128:(c + 1) * 128, :], waits=[ev_stg[sl]])
            ec = dv(lambda e, sl=sl: e.tensor_copy(
                out=bst[sl][:, 0:3072], in_=stgf[sl][:, 0:3 * D]),
                waits=[el, ev_bst0[sl]])
            ev_stg[sl] = ec
            ev_bst0[sl] = dma("gp", "gbst%d" % sl, wqkv_scr[c], bst[sl][:, 0:3072],
                              waits=[ec])
        ev_wq_all = ev_bst0[(NC - 1) % 2]

        ev_bst = list(ev_bst0)
        wlist = []
        for hh in range(NHC):
            wlist.append((D, wgT[:, hh * 128:(hh + 1) * 128], wg_bf[hh]))
            wlist.append((D, wuT[:, hh * 128:(hh + 1) * 128], wu_bf[hh]))
        for dd in range(NC):
            wlist.append((HID, wdT[:, dd * 128:(dd + 1) * 128], wd_bf[dd]))
        slot = 0
        ev_deq = {}
        ev_stg_free = [ev_stg[0], ev_stg[1]]
        for i, (rows, src, dst) in enumerate(wlist):
            ncols = rows
            it = stg[slot][:, 0:ncols]
            bt = bst[slot][:, 0:ncols]
            s3 = src.rearrange("(c p) h -> p c h", p=128)
            i3 = it.rearrange("p (c h) -> p c h", h=128)
            el = dma("sy", "swq%d" % slot, i3, s3,
                     waits=[ev_stg_free[slot], ev_wq_all])
            ec = dv(lambda e, it=it, bt=bt: e.tensor_copy(out=bt, in_=it),
                    waits=[el, ev_bst[slot]])
            ev_stg_free[slot] = ec
            ev_bst[slot] = dma("gp", "gbst%d" % slot, dst, bt, waits=[ec])
            ev_deq[i] = ev_bst[slot]
            slot ^= 1
        ev_deq_all = P.barrier()

        ev_pair = [None] * NP
        ev_xnspill = [None] * NTT
        bar_mid = None

        def prep_pairs():
            ev_xnp = [None, None]
            ev_xnp_free = [None, None]
            ev_wqp_free = [None, None]
            ev_wqp = [None, None]
            ev_kraw_free = None
            ev_sq_free = None
            ev_usc_free = None
            ev_urot_free = None
            ev_t1_free = None
            ev_tsr_free = None
            ev_kq_free = None
            ev_v_free = None
            ev_rope_free = [None, None]
            jc = 0
            for p in range(NP):
                ps_ = p % 2
                wq_w = [ev_wqp_free[ps_], ev_wq_all]
                evw = None
                for c in range(NC):
                    for j, coff in enumerate((1024, 0, 2048)):
                        evw = dma("sy", "swq%d" % ps_,
                                  wqp[ps_][c][:, j * 128:(j + 1) * 128],
                                  wqkv_scr[c][:, coff + p * 128:coff + (p + 1) * 128],
                                  waits=wq_w if (c == 0 and j == 0) else ())
                ev_wqp[ps_] = evw
                yield
                ev_maxk = None
                ev_maxq = None
                for tt in range(NTT):
                    sl = tt % 2
                    own = tt < NTTO
                    t0 = tt * TT
                    while ev_xnspill[tt] is None:
                        yield
                    wxx = [ev_xnp_free[sl], ev_xnspill[tt]]
                    evx = None
                    for c in range(NC):
                        evx = dma("sy", "sxp%d" % sl, xnp[sl][c][:, :],
                                  xn_scr[c * 128:(c + 1) * 128, t0:t0 + TT],
                                  waits=wxx if c == 0 else ())
                    er1 = dma("sy", "sxp%d" % sl, ropec[sl][:, :],
                              ropec_in[:, t0:t0 + TT],
                              waits=[ev_rope_free[sl]])
                    er2 = dma("sy", "sxp%d" % sl, ropes[sl][:, :],
                              ropes_in[:, t0:t0 + TT])
                    evx = er2
                    er1 = er2
                    ev_rope_rd = None
                    yield
                    jobs = [("k", ks2x, 0, kt_scr[p])]
                    if own:
                        jobs.append(("q", qse, 1, q_scr[p]))
                    for kind, scv, joff, dstt in jobs:
                        psk = ps_a
                        bname = "a"
                        jc += 1
                        evk = None
                        for c in range(NC):
                            w0 = ([ev_wqp[ps_], evx, bank[bname]["r"]]
                                  if c == 0 else ())
                            evk = mm(psk[:, 0:TT],
                                     wqp[ps_][c][:, joff * 128:(joff + 1) * 128],
                                     xnp[sl][c][:, :], c == 0, c == NC - 1,
                                     waits=w0, inc="pe" if c == NC - 1 else None)
                        bank[bname]["w"] = evk
                        ekraw = dv(lambda e, b=psk: e.tensor_copy(
                            out=kraw[:, :], in_=b[:, 0:TT]),
                            waits=[evk, ev_kraw_free])
                        esq = dv(lambda e: e.tensor_tensor(out=sqt[:, :],
                                                           in0=kraw[:, :],
                                                           in1=kraw[:, :],
                                                           op=ALU.mult),
                                 waits=[ekraw, ev_sq_free])
                        yield
                        evn = mm(psk[0:2, 0:TT], bones2[:, 0:2], sqt[:, :], True,
                                 True, waits=[esq, ekraw], inc="pe")
                        bank[bname]["w"] = evn
                        elr = ac(lambda e, b=psk: e.activation(
                            out=ln1[:, :], in_=b[0:2, 0:TT], func=AF.Ln,
                            bias=eps6[:, 0:1], scale=1.0 / HD), waits=[evn])
                        err = ac(lambda e: e.activation(out=rstdf[:, :],
                                                        in_=ln1[:, :],
                                                        func=AF.Exp, scale=-0.5),
                                 waits=[elr])
                        erb = dv(lambda e: e.tensor_copy(out=rstdb[:, :],
                                                         in_=rstdf[:, :]),
                                 waits=[err])
                        yield
                        evrb = mm(psk[:, 0:TT], bc2[:, :], rstdb[:, :], True, True,
                                  waits=[erb, elr], inc="pe")
                        bank[bname]["w"] = evrb
                        eu = dv(lambda e, s=scv: e.tensor_scalar(
                            out=usc[:, :], in0=kraw[:, :], scalar1=s[:, 0:1],
                            scalar2=None, op0=ALU.mult),
                            waits=[ekraw, ev_usc_free, ev_qse])
                        ev_kraw_free = eu
                        ec1 = dv(lambda e: e.tensor_copy(out=urot[0:32, :],
                                                         in_=usc[32:64, :]),
                                 waits=[eu, ev_urot_free])
                        ec2 = dv(lambda e: e.tensor_copy(out=urot[32:64, :],
                                                         in_=usc[0:32, :]),
                                 waits=[eu])
                        ec3 = dv(lambda e: e.tensor_copy(out=urot[64:96, :],
                                                         in_=usc[96:128, :]),
                                 waits=[eu])
                        ec4 = dv(lambda e: e.tensor_copy(out=urot[96:128, :],
                                                         in_=usc[64:96, :]),
                                 waits=[eu])
                        et1 = dv(lambda e, sl=sl: e.tensor_tensor(
                            out=t1r[:, :], in0=usc[:, :], in1=ropec[sl][:, :],
                            op=ALU.mult), waits=[eu, er1, ev_t1_free])
                        et2 = dv(lambda e, sl=sl: e.tensor_tensor(
                            out=urot[:, :], in0=urot[:, :], in1=ropes[sl][:, :],
                            op=ALU.mult), waits=[ec1, ec2, ec3, ec4, er2])
                        ets = dv(lambda e: e.tensor_tensor(out=tsr[:, :],
                                                           in0=t1r[:, :],
                                                           in1=urot[:, :],
                                                           op=ALU.add),
                                 waits=[et1, et2, ev_tsr_free])
                        ev_rope_rd = ets
                        ev_usc_free = ets
                        ev_urot_free = ets
                        ev_t1_free = ets
                        ekf = dv(lambda e, b=psk: e.tensor_tensor(
                            out=kqst[:, :], in0=tsr[:, :], in1=b[:, 0:TT],
                            op=ALU.mult), waits=[ets, evrb, ev_kq_free])
                        ev_tsr_free = ekf
                        esq2 = dv(lambda e: e.tensor_tensor(out=sqt[:, :],
                                                            in0=kqst[:, :],
                                                            in1=kqst[:, :],
                                                            op=ALU.mult),
                                  waits=[ekf, evn])
                        ev_sq_free = esq2
                        yield
                        evn2 = mm(psk[0:2, 0:TT], bones2[:, 0:2], sqt[:, :], True,
                                  True, waits=[esq2, ekf], inc="pe")
                        bank[bname]["w"] = evn2
                        evre = dv(lambda e, b=psk: e.reduce_max(
                            out=redm[:, 0:1], in_=b[0:2, 0:TT], axis=AXX),
                            waits=[evn2])
                        bank[bname]["r"] = evre
                        mx = kmax if kind == "k" else qmax
                        evmx = dv(lambda e, m=mx, p=p: e.tensor_tensor(
                            out=m[:, p:p + 1], in0=m[:, p:p + 1],
                            in1=redm[:, 0:1], op=ALU.max), waits=[evre])
                        if kind == "k":
                            ev_maxk = evmx
                        else:
                            ev_maxq = evmx
                        ev_kq_free = dma("gp", "gkq", dstt[:, t0:t0 + TT],
                                         kqst[:, :], waits=[ekf])
                        yield
                    psk = ps_a
                    bname = "a"
                    jc += 1
                    mm(psk[:, 0:512], zeros[:, 0:128], zeros[:, 0:512], True, False,
                       waits=[evx, ev_wqp[ps_], bank[bname]["r"]])
                    for tc in range(TT // 128):
                        for c in range(NC):
                            mm(psk[:, tc * 128:(tc + 1) * 128],
                               xnp[sl][c][:, tc * 128:(tc + 1) * 128],
                               wqp[ps_][c][:, 256:384],
                               False, False)
                    evvc = mm(psk[:, 0:1], zeros[:, 0:128], zeros[:, 0:1], False,
                              True, inc="pe")
                    bank[bname]["w"] = evvc
                    evve = dv(lambda e, b=psk: e.tensor_copy(
                        out=vst[:, :], in_=b[:, 0:TT]),
                        waits=[evvc, ev_v_free])
                    bank[bname]["r"] = evve
                    ev_v_free = dma("gp", "gv", v_scr[p][:, t0:t0 + TT],
                                    vst[:, :], waits=[evve])
                    ev_xnp_free[sl] = evvc
                    ev_rope_free[sl] = ev_rope_rd
                    yield
                ev_wqp_free[ps_] = evvc
                ebp = dv(lambda e, p=p: e.tensor_tensor(
                    out=kmax[:, p:p + 1], in0=kmax[:, p:p + 1],
                    in1=qmax[:, p:p + 1], op=ALU.mult), waits=[ev_maxk, ev_maxq])
                ef1 = dma("gp", "gcf", cflip[p:p + 1, :], kmax[0:2, p:p + 1],
                          waits=[ebp])
                ef2 = dma("gp", "gcf", crow[:, 2 * p:2 * p + 2], cflip[p:p + 1, :],
                          waits=[ef1])
                em1 = dv(lambda e, p=p: e.reduce_max(
                    out=crow2[0:1, p:p + 1], in_=crow[0:1, 2 * p:2 * p + 2],
                    axis=AXX), waits=[ef2])
                ci = crow2.bitcast(I32)
                em2a = dv(lambda e, p=p: e.tensor_scalar(
                    out=ci[0:1, p:p + 1], in0=ci[0:1, p:p + 1], scalar1=1,
                    scalar2=None, op0=ALU.arith_shift_right), waits=[em1])
                em2_ = dv(lambda e, p=p: e.tensor_scalar(
                    out=ci[0:1, p:p + 1], in0=ci[0:1, p:p + 1],
                    scalar1=0x1FC00000, scalar2=None, op0=ALU.add),
                    waits=[em2a])
                em4 = dv(lambda e, p=p: e.tensor_scalar(
                    out=crow2[0:1, p:p + 1], in0=crow2[0:1, p:p + 1],
                    scalar1=-1.04, scalar2=72.0, op0=ALU.mult, op1=ALU.add),
                    waits=[em2_])
                psk = ps_a
                bname = "a"
                jc += 1
                ecb = mm(psk[:, 0:1], onesf[0:1, 0:128], crow2[0:1, p:p + 1],
                         True, True, waits=[em4, bank[bname]["r"]], inc="pe")
                bank[bname]["w"] = ecb
                ev_cbp = dv(lambda e, b=psk, p=p: e.tensor_copy(
                    out=cbias[:, p:p + 1], in_=b[:, 0:1]), waits=[ecb])
                bank[bname]["r"] = ev_cbp
                ev_pair[p] = [ev_kq_free, ev_v_free, ev_cbp]
                yield

        gen_prep = prep_pairs()
        ev_xf = [None, None]
        ev_xf_free = [[], []]
        ev_xn_rd = None
        ev_t1r_free = None
        ev_con2 = ev_con
        for tt in range(NTT):
            sl = tt % 2
            t0 = tt * TT
            wx = [e for e in ev_xf_free[sl] if e]
            for c in range(NC):
                ev_xf[sl] = dma("sy", "sxf%d" % sl, xf[sl][c][:, :],
                                x_fm[c * 128:(c + 1) * 128, t0:t0 + TT],
                                waits=wx if c == 0 else ())
            ev_xf_free[sl] = []
            evxb = None
            for c in range(NC):
                wts = [ev_xf[sl]] + ([ev_xn_rd] if (c == 0 and ev_xn_rd) else [])
                evxb = dv(lambda e, c=c, sl=sl: e.tensor_copy(
                    out=xb[:, c * TT:(c + 1) * TT], in_=xf[sl][c][:, :]), waits=wts)
            evsq = dv(lambda e: e.tensor_tensor(out=xsq[:, :], in0=xb[:, :],
                                                in1=xb[:, :], op=ALU.mult),
                      waits=[evxb])
            evA = evB = None
            for c in range(NC):
                wa = ([evxb, bank["a"]["r"]]) if c == 0 else ()
                wb = ([evsq, bank["b"]["r"]]) if c == 0 else ()
                evA = mm(ps_a[0:1, 0:TT], ones1b[:, 0:1], xb[:, c * TT:(c + 1) * TT],
                         c == 0, c == NC - 1, waits=wa,
                         inc="pe" if c == NC - 1 else None)
                evB = mm(ps_b[0:1, 0:TT], ones1b[:, 0:1], xsq[:, c * TT:(c + 1) * TT],
                         c == 0, c == NC - 1, waits=wb,
                         inc="pe" if c == NC - 1 else None)
            bank["a"]["w"] = evA
            bank["b"]["w"] = evB
            em = dv(lambda e: e.tensor_scalar(out=m_row[:, :], in0=ps_a[0:1, 0:TT],
                                              scalar1=1.0 / D, scalar2=None,
                                              op0=ALU.mult), waits=[evA])
            em2 = dv(lambda e: e.tensor_tensor(out=msq[:, :], in0=m_row[:, :],
                                               in1=m_row[:, :], op=ALU.mult),
                     waits=[em])
            evr = dv(lambda e: e.scalar_tensor_tensor(out=v_row[:, :],
                                                      in0=ps_b[0:1, 0:TT],
                                                      scalar=1.0 / D, in1=msq[:, :],
                                                      op0=ALU.mult,
                                                      op1=ALU.subtract),
                     waits=[evB, em2])
            bank["a"]["r"] = em
            bank["b"]["r"] = evr
            el1 = ac(lambda e: e.activation(out=lnv[:, :], in_=v_row[:, :],
                                            func=AF.Ln, bias=eps5[:, 0:1],
                                            scale=1.0), waits=[evr, ev_con2])
            ea = ac(lambda e: e.activation(out=a_row[:, :], in_=lnv[:, :],
                                           func=AF.Exp, scale=-0.5), waits=[el1])
            eb = dv(lambda e: e.scalar_tensor_tensor(out=b_row[:, :], in0=m_row[:, :],
                                                     scalar=-1.0, in1=a_row[:, :],
                                                     op0=ALU.mult, op1=ALU.mult),
                    waits=[ea])
            eba = mm(ps_a[:, 0:TT], onesf[0:1, 0:128], a_row[:, :], True, True,
                     waits=[ea, bank["a"]["r"]], inc="pe")
            ebb = mm(ps_b[:, 0:TT], onesf[0:1, 0:128], b_row[:, :], True, True,
                     waits=[eb, bank["b"]["r"]], inc="pe")
            bank["a"]["w"] = eba
            bank["b"]["w"] = ebb
            exn = None
            ev_sp = None
            for c in range(NC):
                wts = [eba, ev_t1r_free] if c == 0 else [eba, exn]
                et = dv(lambda e, c=c, sl=sl: e.tensor_tensor(
                    out=lnt[:, :], in0=xf[sl][c][:, :], in1=ps_a[:, 0:TT],
                    op=ALU.mult),
                    waits=wts + ([ev_xn_rd] if (c == 0 and ev_xn_rd) else []))
                exn = dv(lambda e, c=c: e.tensor_tensor(
                    out=xnt[:, c * TT:(c + 1) * TT], in0=lnt[:, :],
                    in1=ps_b[:, 0:TT], op=ALU.add), waits=[ebb, et])
                ev_sp = dma("gp", "gxn", xn_scr[c * 128:(c + 1) * 128, t0:t0 + TT],
                            xnt[:, c * TT:(c + 1) * TT], waits=[exn])
            ev_t1r_free = exn
            bank["a"]["r"] = exn
            bank["b"]["r"] = exn
            ev_xn_rd = ev_sp
            ev_xf_free[sl].append(exn)
            ev_xnspill[tt] = ev_sp
            if tt >= 2:
                for _ in range(26):
                    next(gen_prep)
        bar_mid = P.barrier()

        TQ = float(Q127 / 2.0)

        def ffn_units():
            ev_xnf = [None, None]
            ev_xnf_free = [list(bar_mid), list(bar_mid)]
            ev_wg_free = [None, None]
            ev_wu_free = [None, None]
            ev_h = [None] * NHC
            ev_sg_free = None
            ev_sg1_free = None
            ev_gst_free = None
            for ft in range(NFT):
                f0 = ft * 512
                sl = ft % 2
                evs = None
                for c in range(NC):
                    w0 = ()
                    if c == 0 and ev_xnf_free[sl]:
                        w0 = (ev_xnf_free[sl] if isinstance(ev_xnf_free[sl], list)
                              else [ev_xnf_free[sl]])
                    evs = dma("sy", "sxn%d" % sl, xnf[sl][c][:, :],
                              xn_scr[c * 128:(c + 1) * 128, f0:f0 + 512], waits=w0)
                ev_xnf[sl] = evs
                yield
                for hh in range(NHC):
                    ws = hh % 2
                    eg = dma("sy", "swg%d" % ws, wgb[ws][:, :], wg_bf[hh],
                             waits=[ev_wg_free[ws], ev_deq[2 * hh]])
                    eu_ = dma("sy", "swu%d" % ws, wub[ws][:, :], wu_bf[hh],
                              waits=[ev_wu_free[ws], ev_deq[2 * hh + 1]])
                    yield
                    evg = evu = None
                    for c in range(NC):
                        wg_ = [eg, ev_xnf[sl], bank["b"]["r"]] if c == 0 else ()
                        evg = mm(ps_b[:, :], wgb[ws][:, c * 128:(c + 1) * 128],
                                 xnf[sl][c][:, :], c == 0, c == NC - 1, waits=wg_,
                                 inc="pe" if c == NC - 1 else None)
                    bank["b"]["w"] = evg
                    ev_wg_free[ws] = evg
                    ee0 = dv(lambda e: e.tensor_scalar(
                        out=sg1[:, :], in0=ps_b[:, :], scalar1=-Q127, scalar2=43.0,
                        op0=ALU.mult, op1=ALU.min), waits=[evg, ev_sg1_free])
                    ee1 = ac(lambda e: e.activation(out=sgate[:, :], in_=sg1[:, :],
                                                    func=AF.Exp),
                             waits=[ee0, ev_sg_free])
                    ee2 = ac(lambda e: e.activation(out=sg1[:, :], in_=sgate[:, :],
                                                    func=AF.Ln, bias=onebias[:, 0:1],
                                                    scale=1.0),
                             waits=[ee1])
                    ee3 = ac(lambda e: e.activation(out=sgate[:, :], in_=sg1[:, :],
                                                    func=AF.Exp, scale=-1.0),
                             waits=[ee2])
                    et0 = dv(lambda e: e.scalar_tensor_tensor(
                        out=sg1[:, :], in0=ps_b[:, :], scalar=Q127 * Q127,
                        in1=sgate[:, :], op0=ALU.mult, op1=ALU.mult),
                        waits=[ee3, evg])
                    ev_sg1_free = et0
                    bank["b"]["r"] = et0
                    yield
                    for c in range(NC):
                        wu2 = [eu_, ev_xnf[sl], et0] if c == 0 else ()
                        evu = mm(ps_b[:, :], wub[ws][:, c * 128:(c + 1) * 128],
                                 xnf[sl][c][:, :], c == 0, c == NC - 1, waits=wu2,
                                 inc="pe" if c == NC - 1 else None)
                    bank["b"]["w"] = evu
                    ev_wu_free[ws] = evu
                    evh = dv(lambda e, hh=hh: e.tensor_tensor(
                        out=hbuf[hh][:, :], in0=ps_b[:, :], in1=sg1[:, :],
                        op=ALU.mult), waits=[et0, evu, ev_h[hh]])
                    ev_sg_free = evh
                    ev_sg1_free = evh
                    ev_h[hh] = evh
                    bank["b"]["r"] = evh
                    yield
                ev_wd_free = [ev_wg_free[0], ev_wg_free[1], ev_wu_free[0]]
                evd_last = None
                for dd in range(NC):
                    psd = ps_b
                    bname = "b"
                    wdl = []
                    for seg in range(3):
                        buf = (wgb[0], wgb[1], wub[0])[seg]
                        wsem = ("swg0", "swg1", "swu0")[seg]
                        wdl.append((buf, dma("sy", wsem, buf[:, :],
                                             wd_bf[dd][:, seg * 1024:(seg + 1) * 1024],
                                             waits=[ev_wd_free[seg],
                                                    ev_deq[2 * NHC + dd]])))
                    yield
                    evd = None
                    for hh in range(NHC):
                        seg, off = divmod(hh, 8)
                        w0 = ([wdl[0][1], wdl[1][1], wdl[2][1], ev_h[NHC - 1],
                               bank[bname]["r"]] if hh == 0 else ())
                        evd = mm(psd[:, :], wdl[seg][0][:, off * 128:(off + 1) * 128],
                                 hbuf[hh][:, :], hh == 0, hh == NHC - 1, waits=w0,
                                 inc="pe" if hh == NHC - 1 else None)
                    ev_wd_free = [evd, evd, evd]
                    bank[bname]["w"] = evd
                    evd_last = evd
                    ef0 = dv(lambda e, psd=psd: e.tensor_scalar(
                        out=sg1[:, :], in0=psd[:, :], scalar1=-Q127, scalar2=43.0,
                        op0=ALU.mult, op1=ALU.min), waits=[evd, ev_sg1_free])
                    ef1 = ac(lambda e: e.activation(
                        out=sgate[:, :], in_=sg1[:, :], func=AF.Exp),
                        waits=[ef0, ev_sg_free])
                    ef2 = ac(lambda e: e.activation(out=sg1[:, :], in_=sgate[:, :],
                                                    func=AF.Ln, bias=onebias[:, 0:1],
                                                    scale=1.0),
                             waits=[ef1])
                    es2 = ac(lambda e: e.activation(out=sgate[:, :], in_=sg1[:, :],
                                                    func=AF.Exp, scale=-1.0),
                             waits=[ef2])
                    ev_sg1_free = es2
                    evgb = dv(lambda e, psd=psd: e.scalar_tensor_tensor(
                        out=gst[:, :], in0=psd[:, :], scalar=Q127,
                        in1=sgate[:, :], op0=ALU.mult, op1=ALU.mult),
                        waits=[es2, ev_gst_free])
                    ev_sg_free = evgb
                    ev_sg1_free = evgb
                    bank[bname]["r"] = evgb
                    ev_gst_free = dma("gp", "gg", g_scr[dd][:, f0:f0 + 512],
                                      gst[:, :], waits=[evgb])
                    yield

                for hh in range(NHC):
                    ev_h[hh] = evd_last
                ev_xnf_free[sl] = evd_last
                ev_wg_free = [evd_last, evd_last]
                ev_wu_free = [evd_last, evd_last]

        gen_ffn = ffn_units()
        while ev_pair[0] is None:
            next(gen_prep)
        npump = [0]
        gens = [gen_prep, gen_ffn]

        def pump():
            npump[0] += 1
            r = npump[0] % 4
            if r == 1:
                g, h = gens[0], gens[1]
            elif r == 3:
                g, h = gens[1], gens[0]
            else:
                return
            try:
                next(g)
            except StopIteration:
                try:
                    next(h)
                except StopIteration:
                    pass
        ev_kvqf = None
        ev_pfree = [None, None, None]
        ev_ost_free = None
        ev_zt_free = None
        ev_zbs_free = None
        ev_oraw_free = [None, None]
        st = {"zbs_free": None, "ost_free": None, "oraw_free": [None, None],
              "zt_free": None}

        def pr6(i):
            j = i % 6
            return pring[j // 2][:, (j % 2) * 1024:((j % 2) + 1) * 1024]

        for p in range(NP):
            sl = p % 2
            while ev_pair[p] is None:
                next(gen_prep)
            wld = ev_pair[p] + ([ev_kvqf] if ev_kvqf else [])
            ek = dma("sy", "skv%d" % sl, kb[:, 0:NTOT], kt_scr[p], waits=wld)
            eq = dma("sy", "skv%d" % sl, qb[:, 0:NOWN], q_scr[p])
            evv = dma("sy", "skv%d" % sl, vb[:, 0:NTOT], v_scr[p])
            ek = eq = evv
            ev_sexp = [None, None]
            pfree = [[] for _ in range(6)]
            pend_flush = [None]
            for qt in range(NQT):
                q0 = qt * QT
                mm(ps_pv[:, 0:QT], zeros[:, 0:128], zeros[:, 0:QT], True, False,
                   waits=[ek, bank["pv"]["r"]])
                ev_exp = [None] * NCH

                def emit_pv(c):
                    slot = pr6(c)
                    mm(ps_pv[0:64, 0:QT], vb[:, c * 128:c * 128 + 64],
                       slot[:, 0:QT], False, False, tp=(0, 0), waits=[ev_exp[c]])
                    ev = mm(ps_pv[64:128, 0:QT], vb[:, c * 128 + 64:(c + 1) * 128],
                            slot[:, 512:512 + QT], False, False, tp=(0, 64),
                            inc="pe")
                    pfree[c % 6].append(ev)

                def emit_z(c0):
                    sa, sb_ = pr6(c0), pr6(c0 + 1)
                    mm(ps_z[0:1, 0:QT], onesb[:, 0:1], sa[:, 0:QT], False, False,
                       tp=(0, 0), waits=[ev_exp[c0 + 1]])
                    mm(ps_z[32:33, 0:QT], onesb[:, 1:2], sa[:, 512:512 + QT],
                       False, False, tp=(0, 32))
                    mm(ps_z[64:65, 0:QT], onesb[:, 2:3], sb_[:, 0:QT],
                       False, False, tp=(0, 64))
                    ev = mm(ps_z[96:97, 0:QT], onesb[:, 3:4], sb_[:, 512:512 + QT],
                            False, False, tp=(0, 96), inc="pe")
                    pfree[c0 % 6].append(ev)
                    pfree[(c0 + 1) % 6].append(ev)

                for c in range(NCH):
                    hf = c % 2
                    so = hf * 1024
                    mm(ps_s[:, so:so + QT], kb[0:64, c * 128:(c + 1) * 128],
                       qb[0:64, q0:q0 + QT], True, True, tp=(0, 0),
                       waits=[ek, ev_sexp[hf]])
                    ev_s = mm(ps_s[:, so + 512:so + 512 + QT],
                              kb[64:128, c * 128:(c + 1) * 128],
                              qb[64:128, q0:q0 + QT], True, True, tp=(64, 0),
                              inc="pe")
                    ev_exp[c] = ac(lambda e, c=c, so=so, p=p: e.activation(
                        out=pr6(c), in_=ps_s[:, so:so + 1024], func=AF.Exp,
                        bias=cbias[:, p:p + 1], scale=1.0),
                        waits=[ev_s] + pfree[c % 6])
                    pfree[c % 6] = []
                    ev_sexp[hf] = ev_exp[c]
                    if c == 2:
                        if pend_flush[0] is not None:
                            pend_flush[0]()
                            pend_flush[0] = None
                        mm(ps_z[:, 0:QT], zeros[:, 0:128], zeros[:, 0:QT], True,
                           False, waits=[bank["z"]["r"]])
                    if c >= 2:
                        emit_pv(c - 2)
                    if c >= 3 and c % 2 == 1:
                        emit_z(c - 3)
                    pump()
                emit_pv(NCH - 2)
                emit_pv(NCH - 1)
                emit_z(NCH - 2)
                evc1 = mm(ps_pv[:, 0:1], zeros[:, 0:128], zeros[:, 0:1], False,
                          True, inc="pe")
                evc2 = mm(ps_z[:, 0:1], zeros[:, 0:128], zeros[:, 0:1], False,
                          True, inc="pe")
                bank["pv"]["w"] = evc1
                bank["z"]["w"] = evc2
                osl = qt % 2
                eor = dv(lambda e, osl=osl: e.tensor_copy(out=oraw[osl][:, :],
                                                          in_=ps_pv[:, 0:QT]),
                         waits=[evc1, st["oraw_free"][osl]])
                bank["pv"]["r"] = eor
                ez0 = dv(lambda e: e.tensor_copy(out=zt[:, :], in_=ps_z[:, 0:QT]),
                         waits=[evc2, st["zt_free"]])
                bank["z"]["r"] = ez0
                ec1 = dv(lambda e: e.tensor_copy(out=zc1[:, :], in_=zt[64:65, :]),
                         waits=[ez0])
                ec2 = dv(lambda e: e.tensor_copy(out=zc2[:, :], in_=zt[96:97, :]),
                         waits=[ez0])
                ec3 = dv(lambda e: e.tensor_copy(out=zc3[:, :], in_=zt[32:33, :]),
                         waits=[ez0])
                ea1 = dv(lambda e: e.tensor_tensor(out=zrow[:, 0:QT],
                                                   in0=zt[0:1, :], in1=zc1[:, :],
                                                   op=ALU.add), waits=[ec1])
                ea2 = dv(lambda e: e.tensor_tensor(out=zrow[:, QT:2 * QT],
                                                   in0=zc3[:, :], in1=zc2[:, :],
                                                   op=ALU.add), waits=[ec2, ec3])
                st["zt_free"] = ea2
                eg_ = dv(lambda e: e.tensor_scalar(out=zln[:, :], in0=zrow[:, :],
                                                   scalar1=1e-30, scalar2=None,
                                                   op0=ALU.add), waits=[ea1, ea2])
                ei = dv(lambda e: e.reciprocal(out=zi[:, :], in_=zln[:, :]),
                        waits=[eg_])
                eib = dv(lambda e: e.tensor_copy(out=zib[:, :], in_=zi[:, :]),
                         waits=[ei])

                def flush(p=p, qt=qt, q0=q0, eib=eib, osl=osl, eor=eor):
                    mm(ps_z[:, 0:QT], zeros[:, 0:128], zeros[:, 0:QT], True, False,
                       waits=[eib, bank["z"]["r"]])
                    mm(ps_z[0:64, 0:QT], onesfh[0:1, 0:64], zib[:, 0:QT],
                       False, False, tp=(0, 0))
                    mm(ps_z[64:128, 0:QT], onesfh[0:1, 0:64], zib[:, QT:2 * QT],
                       False, False, tp=(0, 64))
                    ezb = mm(ps_z[:, 0:1], zeros[:, 0:128], zeros[:, 0:1], False,
                             True, inc="pe")
                    bank["z"]["w"] = ezb
                    ezbs = dv(lambda e: e.tensor_copy(out=zbs[:, :],
                                                      in_=ps_z[:, 0:QT]),
                              waits=[ezb, st["zbs_free"]])
                    bank["z"]["r"] = ezbs
                    eo = dv(lambda e, osl=osl: e.tensor_tensor(
                        out=ost[:, :], in0=oraw[osl][:, :], in1=zbs[:, :],
                        op=ALU.mult), waits=[eor, ezbs, st["ost_free"]])
                    st["zbs_free"] = eo
                    st["oraw_free"][osl] = eo
                    st["ost_free"] = dma("gp", "go", o_scr[p][:, q0:q0 + QT],
                                         ost[:, :], waits=[eo])

                pend_flush[0] = flush
            pend_flush[0]()
            pend_flush[0] = None
            ev_kvqf = evc1

        for g in gens:
            for _ in g:
                pass

        bar_pr = P.barrier()
        ev_ob = []
        for dd in range(NC):
            ev_ob.append(dma("sy", "sob", ob[dd][:, :], o_scr[dd],
                             waits=bar_pr if dd == 0 else ()))
        ev_c1 = ev_c2 = None
        for dd in range(NC):
            e1_ = dma("sy", "swp0", xcb[0][:, :],
                      w_proj[dd * 128:(dd + 1) * 128, 0:512],
                      waits=[ev_c1] if ev_c1 else bar_pr)
            e2_ = dma("sy", "swp1", xcb[1][:, :],
                      w_proj[dd * 128:(dd + 1) * 128, 512:1024],
                      waits=[ev_c2] if ev_c2 else bar_pr)
            ev_c1 = dv(lambda e, dd=dd: e.tensor_copy(out=wpb[dd][:, 0:512],
                                                      in_=xcb[0][:, :]), waits=[e1_])
            ev_c2 = dv(lambda e, dd=dd: e.tensor_copy(out=wpb[dd][:, 512:1024],
                                                      in_=xcb[1][:, :]), waits=[e2_])
        ev_wp = ev_c2
        prb = ["s0", "s1", "s2", "s3"]
        bi = 0
        ev_gc = [None, None]
        ev_xc = [ev_c1, ev_c2]
        ev_otf = [None, None]
        ev_sg = None
        for oo in range(NC):
            ev_sfm = None
            for qt in range(NQT):
                bnk = prb[bi % 4]
                bi += 1
                evm = None
                for dd in range(NC):
                    w0 = ([ev_wp, ev_c1, ev_ob[-1], bank[bnk]["r"]]
                          if dd == 0 else ())
                    evm = mm(SBANK[bnk][:, 0:QT], wpb[dd][:, oo * 128:(oo + 1) * 128],
                             ob[dd][:, qt * QT:(qt + 1) * QT], dd == 0, dd == NC - 1,
                             waits=w0, inc="pe" if dd == NC - 1 else None)
                bank[bnk]["w"] = evm
                ev_sfm = dv(lambda e, b=bnk, oo=oo, qt=qt: e.tensor_scalar(
                    out=sfm[oo][:, qt * QT:(qt + 1) * QT], in0=SBANK[b][:, 0:QT],
                    scalar1=bpt[:, oo:oo + 1], scalar2=None, op0=ALU.add),
                    waits=[evm, e3])
                bank[bnk]["r"] = ev_sfm
            for ft in range(NFT):
                f0 = ft * 512
                slot = (oo * NFT + ft) % 2
                eg = dma("sy", "sge%d" % slot, gcb[slot][:, :],
                         g_scr[oo][:, f0:f0 + 512], waits=[ev_gc[slot]])
                ex = dma("sy", "sxe%d" % slot, xcb[slot][:, :],
                         x_fm[oo * 128:(oo + 1) * 128, f0:f0 + 512],
                         waits=[ev_xc[slot]])
                esg2 = dv(lambda e, oo=oo, f0=f0, slot=slot: e.tensor_tensor(
                    out=sgc[:, :], in0=sfm[oo][:, f0:f0 + 512], in1=gcb[slot][:, :],
                    op=ALU.mult), waits=[eg, ev_sfm, ev_sg])
                ev_gc[slot] = esg2
                eot = dv(lambda e, slot=slot: e.tensor_tensor(
                    out=otc[slot][:, :], in0=sgc[:, :], in1=xcb[slot][:, :],
                    op=ALU.add), waits=[esg2, ex, ev_otf[slot]])
                ev_sg = eot
                ev_xc[slot] = eot
                ev_otf[slot] = dma("sy", "sout%d" % slot,
                                   out_fm[oo * 128:(oo + 1) * 128, f0:f0 + 512],
                                   otc[slot][:, :], waits=[eot])

        P.emit("gp", lambda e: None,
               waits=[("sout0", P.cnt["sout0"]), ("sout1", P.cnt["sout1"])])

        block = E(nc.Block())

        @block.sync
        def _(sy):
            for f in P.cmds["sy"]:
                f(sy)

        @block.gpsimd
        def _(gp):
            for f in P.cmds["gp"]:
                f(gp)

        @block.tensor
        def _(te):
            for f in P.cmds["pe"]:
                f(te)

        @block.vector
        def _(ve):
            for f in P.cmds["dv"]:
                f(ve)

        @block.scalar
        def _(sc):
            for f in P.cmds["ac"]:
                f(sc)

    return nc


def _bc2_const():
    import ml_dtypes
    m = np.zeros((2, 128), dtype=ml_dtypes.bfloat16)
    m[0, 0:64] = 1
    m[1, 64:128] = 1
    return m


def _rope_tables(npos):
    half = 32
    inv = (np.float32(1.0) /
           np.power(np.float32(10000.0),
                    np.arange(half, dtype=np.float32) / np.float32(half)))
    t = np.arange(npos, dtype=np.float32)
    f = np.outer(t, inv).astype(np.float32)
    cos = np.cos(f).T.astype(np.float32)
    sin = np.sin(f).T.astype(np.float32)
    cos2x = np.ascontiguousarray(np.tile(cos, (4, 1)))
    sinS = np.ascontiguousarray(np.concatenate([-sin, sin, -sin, sin], axis=0))
    return cos2x, sinS


def kernel(L0_attn_x, L0_attn_w_qkv, L0_attn_q_scale, L0_attn_k_scale,
           L0_attn_w_proj, L0_attn_b_proj, L0_ffn_w_gate, L0_ffn_w_up,
           L0_ffn_w_down):
    x = np.asarray(L0_attn_x)
    B, N, C = x.shape
    NOWN = N // 2
    cos2x, sinS = _rope_tables(N)
    w_qkv = np.ascontiguousarray(L0_attn_w_qkv, dtype=np.float32)
    w_proj = np.ascontiguousarray(L0_attn_w_proj, dtype=np.float32)
    b_projt = np.ascontiguousarray(
        np.asarray(L0_attn_b_proj, dtype=np.float32).reshape(NC, 128).T)
    qs2x = np.ascontiguousarray(
        np.concatenate([L0_attn_q_scale, L0_attn_q_scale]).astype(np.float32)
        .reshape(128, 1))
    ks2x = np.ascontiguousarray(
        np.concatenate([L0_attn_k_scale, L0_attn_k_scale]).astype(np.float32)
        .reshape(128, 1))
    wgT = np.ascontiguousarray(np.asarray(L0_ffn_w_gate, dtype=np.int32).T)
    wuT = np.ascontiguousarray(np.asarray(L0_ffn_w_up, dtype=np.int32).T)
    wdT = np.ascontiguousarray(np.asarray(L0_ffn_w_down, dtype=np.int32).T)

    nc = build_program(NTOT=N, NOWN=NOWN)

    in_maps = []
    for core in range(8):
        b, h = core // 2, core % 2
        xt = np.ascontiguousarray(x[b].T.astype(np.float32))
        if h == 0:
            x_c, rc, rs = xt, cos2x, sinS
        else:
            x_c = np.ascontiguousarray(
                np.concatenate([xt[:, NOWN:], xt[:, :NOWN]], axis=1))
            rc = np.ascontiguousarray(
                np.concatenate([cos2x[:, NOWN:], cos2x[:, :NOWN]], axis=1))
            rs = np.ascontiguousarray(
                np.concatenate([sinS[:, NOWN:], sinS[:, :NOWN]], axis=1))
        in_maps.append(dict(x_fm=x_c, w_qkv=w_qkv, w_proj=w_proj,
                            b_projt=b_projt, qs2x=qs2x, ks2x=ks2x,
                            wgT=wgT, wuT=wuT, wdT=wdT, ropec=rc, ropes=rs,
                            bc2c=_bc2_const()))

    res = run_bass_kernel_spmd(nc, in_maps, core_ids=list(range(8)))
    out = np.empty((B, N, C), dtype=np.float32)
    for core in range(8):
        b, h = core // 2, core % 2
        out[b, h * NOWN:(h + 1) * NOWN, :] = np.asarray(
            res.results[core]["out_fm"]).T
    return out


# version 17
# speedup vs baseline: 1.0710x; 1.0235x over previous
import numpy as np
import concourse.bass as bass
import concourse.mybir as mybir
from concourse.bass_utils import run_bass_kernel_spmd

F32 = mybir.dt.float32
BF16 = mybir.dt.bfloat16
I32 = mybir.dt.int32
AF = mybir.ActivationFunctionType
ALU = mybir.AluOpType
AXX = mybir.AxisListType.X

D = 1024
NH = 16
HD = 64
HID = 3072
NP = 8
NC = 8
NHC = HID // 128
Q127 = float(1.0 / 127.0)

ENGS = ("pe", "dv", "ac", "sy", "gp")


class Plan:
    def __init__(self, sems):
        self.sems = sems
        self.cnt = {k: 0 for k in sems}
        self.cmds = {e: [] for e in ENGS}
        self.last_wait = {e: {} for e in ENGS}

    def emit(self, eng, fn, waits=(), inc=None, amt=None):
        merged = {}
        for w in waits:
            if w is None:
                continue
            s, v = w
            if v > merged.get(s, 0):
                merged[s] = v
        nw = []
        lw = self.last_wait[eng]
        for s, v in merged.items():
            if v <= 0 or lw.get(s, 0) >= v:
                continue
            lw[s] = v
            nw.append((s, v))
        a = None
        if inc is not None:
            a = amt if amt is not None else (16 if inc not in ("pe", "dv", "ac") else 1)
            self.cnt[inc] += a
        ev = (inc, self.cnt[inc]) if inc is not None else None
        sems = self.sems

        def closure(eng_obj, fn=fn, nw=tuple(nw), inc=inc, a=a):
            for s, v in nw:
                eng_obj.wait_ge(sems[s], v)
            r = fn(eng_obj)
            if inc is not None:
                r.then_inc(sems[inc], a)
        self.cmds[eng].append(closure)
        return ev

    def barrier(self):
        return [(s, self.cnt[s]) for s in self.sems]


def build_program(NTOT=8192, NOWN=4096, TT=512, QT=512):
    nc = bass.Bass("TRN2", target_bir_lowering=False)
    NTT = NTOT // TT
    NTTO = NOWN // TT
    NQT = NOWN // QT
    NCH = NTOT // 128
    NFT = NOWN // 512

    x_fm = nc.dram_tensor("x_fm", [D, NTOT], F32, kind="ExternalInput").ap()
    w_qkv = nc.dram_tensor("w_qkv", [D, 3 * D], F32, kind="ExternalInput").ap()
    w_proj = nc.dram_tensor("w_proj", [D, D], F32, kind="ExternalInput").ap()
    b_projt = nc.dram_tensor("b_projt", [128, NC], F32, kind="ExternalInput").ap()
    qs2x_in = nc.dram_tensor("qs2x", [128, 1], F32, kind="ExternalInput").ap()
    ks2x_in = nc.dram_tensor("ks2x", [128, 1], F32, kind="ExternalInput").ap()
    wgT = nc.dram_tensor("wgT", [D, HID], I32, kind="ExternalInput").ap()
    wuT = nc.dram_tensor("wuT", [D, HID], I32, kind="ExternalInput").ap()
    wdT = nc.dram_tensor("wdT", [HID, D], I32, kind="ExternalInput").ap()
    ropec_in = nc.dram_tensor("ropec", [128, NTOT], F32, kind="ExternalInput").ap()
    ropes_in = nc.dram_tensor("ropes", [128, NTOT], F32, kind="ExternalInput").ap()
    bc2_in = nc.dram_tensor("bc2c", [2, 128], BF16, kind="ExternalInput").ap()

    out_fm = nc.dram_tensor("out_fm", [D, NOWN], F32, kind="ExternalOutput").ap()

    wg_bf = nc.dram_tensor("wg_bf", [NHC, 128, NC * 128], BF16, kind="Internal").ap()
    wu_bf = nc.dram_tensor("wu_bf", [NHC, 128, NC * 128], BF16, kind="Internal").ap()
    wd_bf = nc.dram_tensor("wd_bf", [NC, 128, HID], BF16, kind="Internal").ap()
    xn_scr = nc.dram_tensor("xn_scr", [D, NTOT], BF16, kind="Internal").ap()
    wqkv_scr = nc.dram_tensor("wqkv_scr", [NC, 128, 3 * D], BF16, kind="Internal").ap()
    kt_scr = nc.dram_tensor("kt_scr", [NP, 128, NTOT], BF16, kind="Internal").ap()
    q_scr = nc.dram_tensor("q_scr", [NP, 128, NOWN], BF16, kind="Internal").ap()
    v_scr = nc.dram_tensor("v_scr", [NP, 128, NTOT], BF16, kind="Internal").ap()
    o_scr = nc.dram_tensor("o_scr", [NP, 128, NOWN], BF16, kind="Internal").ap()
    g_scr = nc.dram_tensor("g_scr", [NC, 128, NOWN], BF16, kind="Internal").ap()
    cflip = nc.dram_tensor("cflip", [NP, 2], F32, kind="Internal").ap()

    from contextlib import ExitStack
    with ExitStack() as es:
        E = es.enter_context
        zeros = E(nc.sbuf_tensor("zeros", [128, 512], BF16))
        onesb = E(nc.sbuf_tensor("onesb", [128, 4], BF16))
        onesf = E(nc.sbuf_tensor("onesf", [128, 128], F32))
        ones1b = E(nc.sbuf_tensor("ones1b", [128, 1], BF16))
        onesfh = E(nc.sbuf_tensor("onesfh", [128, 64], BF16))
        bones2 = E(nc.sbuf_tensor("bones2", [128, 2], BF16))
        bc2 = E(nc.sbuf_tensor("bc2", [2, 128], BF16))
        eps5 = E(nc.sbuf_tensor("eps5", [1, 1], F32))
        epsz = E(nc.sbuf_tensor("epsz", [1, 1], F32))
        bln64 = E(nc.sbuf_tensor("bln64", [1, 1], F32))
        eps6 = E(nc.sbuf_tensor("eps6", [2, 1], F32))
        onebias = E(nc.sbuf_tensor("onebias", [128, 1], F32))
        qs2x = E(nc.sbuf_tensor("qs2xt", [128, 1], F32))
        ks2x = E(nc.sbuf_tensor("ks2xt", [128, 1], F32))
        qse = E(nc.sbuf_tensor("qse", [128, 1], F32))
        bpt = E(nc.sbuf_tensor("bpt", [128, NC], F32))
        cbias = E(nc.sbuf_tensor("cbias", [128, NP], F32))
        kmax = E(nc.sbuf_tensor("kmax", [2, NP], F32))
        qmax = E(nc.sbuf_tensor("qmax", [2, NP], F32))
        crow = E(nc.sbuf_tensor("crow", [1, 2 * NP], F32))
        crow2 = E(nc.sbuf_tensor("crow2", [1, NP], F32))
        m_row = E(nc.sbuf_tensor("m_row", [1, TT], F32))
        msq = E(nc.sbuf_tensor("msq", [1, TT], F32))
        v_row = E(nc.sbuf_tensor("v_row", [1, TT], F32))
        lnv = E(nc.sbuf_tensor("lnv", [1, TT], F32))
        a_row = E(nc.sbuf_tensor("a_row", [1, TT], F32))
        b_row = E(nc.sbuf_tensor("b_row", [1, TT], F32))
        ln1 = E(nc.sbuf_tensor("ln1", [2, TT], F32))
        rstdf = E(nc.sbuf_tensor("rstdf", [2, TT], F32))
        rstdb = E(nc.sbuf_tensor("rstdb", [2, TT], BF16))
        redm = E(nc.sbuf_tensor("redm", [2, 1], F32))
        zt = E(nc.sbuf_tensor("zt", [128, QT], F32))
        zc1 = E(nc.sbuf_tensor("zc1", [1, QT], F32))
        zc2 = E(nc.sbuf_tensor("zc2", [1, QT], F32))
        zc3 = E(nc.sbuf_tensor("zc3", [1, QT], F32))
        zrow = E(nc.sbuf_tensor("zrow", [1, 2 * QT], F32))
        zln = E(nc.sbuf_tensor("zln", [1, 2 * QT], F32))
        zi = E(nc.sbuf_tensor("zi", [1, 2 * QT], F32))
        zib = E(nc.sbuf_tensor("zib", [1, 2 * QT], BF16))
        zbs = E(nc.sbuf_tensor("zbs", [128, QT], BF16))
        oraw = [E(nc.sbuf_tensor("oraw0", [128, QT], F32)),
                E(nc.sbuf_tensor("oraw1", [128, QT], F32))]

        ABIG = 82624
        big = E(nc.sbuf_tensor("big", [128, ABIG], BF16))

        class Arena:
            def __init__(self, start=0):
                self.off = start

            def a(self, cols, dt=BF16):
                n = cols if dt == BF16 else 2 * cols
                self.off = (self.off + 15) // 16 * 16
                o = self.off
                self.off += n
                assert self.off <= ABIG, f"arena overflow {self.off}"
                sl = big[:, o:o + n]
                return sl.bitcast(dt) if dt != BF16 else sl

        ar4 = Arena()
        kb = ar4.a(NTOT)
        vb = ar4.a(NTOT)
        qb = ar4.a(NOWN)
        pring = [ar4.a(2048) for _ in range(3)]
        ost = ar4.a(QT)
        assert ar4.off <= 27200, ar4.off
        ar3 = Arena(27200)
        xnf = [[ar3.a(512) for _ in range(NC)] for _ in range(2)]
        wgb = [ar3.a(NC * 128) for _ in range(2)]
        wub = [ar3.a(NC * 128) for _ in range(2)]
        hbuf = [ar3.a(512) for _ in range(NHC)]
        sgate = ar3.a(512, F32)
        sg1 = ar3.a(512, F32)
        gst = ar3.a(512)
        assert ar3.off <= 54400, ar3.off
        arp = Arena(54400)
        xnp = [[arp.a(TT) for _ in range(NC)] for _ in range(2)]
        wqp = [[arp.a(384) for _ in range(NC)] for _ in range(2)]
        ropec = [arp.a(TT, F32) for _ in range(2)]
        ropes = [arp.a(TT, F32) for _ in range(2)]
        kraw = arp.a(TT, F32)
        usc = arp.a(TT, F32)
        urot = arp.a(TT, F32)
        t1r = arp.a(TT, F32)
        tsr = arp.a(TT, F32)
        sqt = arp.a(TT)
        kqst = arp.a(TT)
        vst = arp.a(TT)
        assert arp.off <= ABIG, arp.off
        arl = Arena(0)
        xf = [[arl.a(TT, F32) for _ in range(NC)] for _ in range(2)]
        xb = arl.a(NC * TT)
        xsq = arl.a(NC * TT)
        xnt = arl.a(NC * TT)
        lnt = arl.a(TT, F32)
        assert arl.off <= 31000, arl.off
        stg = [big[:, 31008:31008 + 6144].bitcast(I32),
               big[:, 37152:37152 + 6144].bitcast(I32)]
        stgf = [big[:, 31008:31008 + 6144].bitcast(F32),
                big[:, 37152:37152 + 6144].bitcast(F32)]
        bst = [big[:, 43296:43296 + 3072], big[:, 46368:46368 + 3072]]
        ar5 = Arena(0)
        ob = [ar5.a(NOWN) for _ in range(NC)]
        wpb = [ar5.a(D) for _ in range(NC)]
        sfm = [ar5.a(NOWN) for _ in range(NC)]
        gcb = [ar5.a(512) for _ in range(2)]
        xcb = [ar5.a(512, F32) for _ in range(2)]
        sgc = ar5.a(512)
        otc = [ar5.a(512, F32) for _ in range(2)]
        assert ar5.off <= ABIG, ar5.off

        ps_s = E(nc.psum_tensor("ps_s", [128, 2048], F32))
        ps_pv = E(nc.psum_tensor("ps_pv", [128, 512], F32))
        ps_z = E(nc.psum_tensor("ps_z", [128, 512], F32))
        ps_a = E(nc.psum_tensor("ps_a", [128, 512], F32))
        ps_b = E(nc.psum_tensor("ps_b", [128, 512], F32))
        SBANK = {"s0": ps_s[:, 0:512], "s1": ps_s[:, 512:1024],
                 "s2": ps_s[:, 1024:1536], "s3": ps_s[:, 1536:2048]}

        DMA_SEMS = ("scst", "swq0", "swq1", "sxf0", "sxf1", "sxn0", "sxn1",
                    "swg0", "swg1", "swu0", "swu1", "sob", "swp0", "swp1",
                    "skv0", "skv1", "sge0", "sge1", "sxe0", "sxe1", "sxp0", "sxp1",
                    "sout0", "sout1", "gkq", "gv", "gxn", "gbst0", "gbst1",
                    "gg", "go", "gcf")
        sems = {s: E(nc.semaphore(s)) for s in ("pe", "dv", "ac") + DMA_SEMS}
        P = Plan(sems)

        def dma(q, sem, dst, src, waits=()):
            return P.emit(q, lambda e, d=dst, s=src: e.dma_start(out=d, in_=s),
                          waits=waits, inc=sem)

        def mm(out, lhsT, rhs, start, stop, tp=None, waits=(), inc=None):
            def f(e, o=out, l=lhsT, r=rhs, st=start, sp=stop, tp=tp):
                return e.matmul(o, l, r, start=st, stop=sp, tile_position=tp)
            return P.emit("pe", f, waits=waits, inc=inc)

        def dv(fn, waits=()):
            return P.emit("dv", fn, waits=waits, inc="dv")

        def ac(fn, waits=()):
            return P.emit("ac", fn, waits=waits, inc="ac")

        bank = {k: {"w": None, "r": None} for k in
                ("s0", "s1", "s2", "s3", "pv", "z", "a", "b")}

        dv(lambda e: e.memset(zeros[:, :], 0.0))
        dv(lambda e: e.memset(onesb[:, :], 1.0))
        dv(lambda e: e.memset(onesf[:, :], 1.0))
        dv(lambda e: e.memset(ones1b[:, :], 1.0))
        dv(lambda e: e.memset(onesfh[:, :], 1.0))
        dv(lambda e: e.memset(bones2[0:64, 0:1], 1.0))
        dv(lambda e: e.memset(bones2[64:128, 0:1], 0.0))
        dv(lambda e: e.memset(bones2[0:64, 1:2], 0.0))
        dv(lambda e: e.memset(bones2[64:128, 1:2], 1.0))
        pass
        dv(lambda e: e.memset(eps5[:, :], 1e-5))
        dv(lambda e: e.memset(epsz[:, :], 1e-38))
        dv(lambda e: e.memset(bln64[:, :], -44.3614195558365))
        dv(lambda e: e.memset(eps6[:, :], 1e-6))
        dv(lambda e: e.memset(onebias[:, :], 1.0))
        dv(lambda e: e.memset(kmax[:, :], 0.0))
        ev_con = dv(lambda e: e.memset(qmax[:, :], 0.0))
        dma("sy", "scst", qs2x[:, :], qs2x_in)
        dma("sy", "scst", ks2x[:, :], ks2x_in)
        dma("sy", "scst", bpt[:, :], b_projt)
        e_cst = dma("sy", "scst", bc2[:, :], bc2_in)
        e1 = e2 = e3 = e_cst
        ev_qse = dv(lambda e: e.tensor_scalar(out=qse[:, :], in0=qs2x[:, :],
                                              scalar1=0.125, scalar2=None,
                                              op0=ALU.mult), waits=[e1])

        ev_stg = [None, None]
        ev_bst0 = [None, None]
        for c in range(NC):
            sl = c % 2
            el = dma("sy", "swq%d" % sl, stgf[sl][:, 0:3 * D],
                     w_qkv[c * 128:(c + 1) * 128, :], waits=[ev_stg[sl]])
            ec = dv(lambda e, sl=sl: e.tensor_copy(
                out=bst[sl][:, 0:3072], in_=stgf[sl][:, 0:3 * D]),
                waits=[el, ev_bst0[sl]])
            ev_stg[sl] = ec
            ev_bst0[sl] = dma("gp", "gbst%d" % sl, wqkv_scr[c], bst[sl][:, 0:3072],
                              waits=[ec])
        ev_wq_all = ev_bst0[(NC - 1) % 2]

        ev_bst = list(ev_bst0)
        wlist = []
        for hh in range(NHC):
            wlist.append((D, wgT[:, hh * 128:(hh + 1) * 128], wg_bf[hh]))
            wlist.append((D, wuT[:, hh * 128:(hh + 1) * 128], wu_bf[hh]))
        for dd in range(NC):
            wlist.append((HID, wdT[:, dd * 128:(dd + 1) * 128], wd_bf[dd]))
        slot = 0
        ev_deq = {}
        ev_stg_free = [ev_stg[0], ev_stg[1]]
        for i, (rows, src, dst) in enumerate(wlist):
            ncols = rows
            it = stg[slot][:, 0:ncols]
            bt = bst[slot][:, 0:ncols]
            s3 = src.rearrange("(c p) h -> p c h", p=128)
            i3 = it.rearrange("p (c h) -> p c h", h=128)
            el = dma("sy", "swq%d" % slot, i3, s3,
                     waits=[ev_stg_free[slot], ev_wq_all])
            ec = dv(lambda e, it=it, bt=bt: e.tensor_copy(out=bt, in_=it),
                    waits=[el, ev_bst[slot]])
            ev_stg_free[slot] = ec
            ev_bst[slot] = dma("gp", "gbst%d" % slot, dst, bt, waits=[ec])
            ev_deq[i] = ev_bst[slot]
            slot ^= 1
        ev_deq_all = P.barrier()

        ev_pair = [None] * NP
        ev_xnspill = [None] * NTT
        bar_mid = None

        def prep_pairs():
            ev_xnp = [None, None]
            ev_xnp_free = [None, None]
            ev_wqp_free = [None, None]
            ev_wqp = [None, None]
            ev_kraw_free = None
            ev_sq_free = None
            ev_usc_free = None
            ev_urot_free = None
            ev_t1_free = None
            ev_tsr_free = None
            ev_kq_free = None
            ev_v_free = None
            ev_rope_free = [None, None]
            jc = 0
            for p in range(NP):
                ps_ = p % 2
                wq_w = [ev_wqp_free[ps_], ev_wq_all]
                evw = None
                for c in range(NC):
                    for j, coff in enumerate((1024, 0, 2048)):
                        evw = dma("sy", "swq%d" % ps_,
                                  wqp[ps_][c][:, j * 128:(j + 1) * 128],
                                  wqkv_scr[c][:, coff + p * 128:coff + (p + 1) * 128],
                                  waits=wq_w if (c == 0 and j == 0) else ())
                ev_wqp[ps_] = evw
                yield
                ev_maxk = None
                ev_maxq = None
                for tt in range(NTT):
                    sl = tt % 2
                    own = tt < NTTO
                    t0 = tt * TT
                    while ev_xnspill[tt] is None:
                        yield
                    wxx = [ev_xnp_free[sl], ev_xnspill[tt]]
                    evx = None
                    for c in range(NC):
                        evx = dma("sy", "sxp%d" % sl, xnp[sl][c][:, :],
                                  xn_scr[c * 128:(c + 1) * 128, t0:t0 + TT],
                                  waits=wxx if c == 0 else ())
                    er1 = dma("sy", "sxp%d" % sl, ropec[sl][:, :],
                              ropec_in[:, t0:t0 + TT],
                              waits=[ev_rope_free[sl]])
                    er2 = dma("sy", "sxp%d" % sl, ropes[sl][:, :],
                              ropes_in[:, t0:t0 + TT])
                    evx = er2
                    er1 = er2
                    ev_rope_rd = None
                    yield
                    jobs = [("k", ks2x, 0, kt_scr[p])]
                    if own:
                        jobs.append(("q", qse, 1, q_scr[p]))
                    for kind, scv, joff, dstt in jobs:
                        psk = ps_a
                        bname = "a"
                        jc += 1
                        evk = None
                        for c in range(NC):
                            w0 = ([ev_wqp[ps_], evx, bank[bname]["r"]]
                                  if c == 0 else ())
                            evk = mm(psk[:, 0:TT],
                                     wqp[ps_][c][:, joff * 128:(joff + 1) * 128],
                                     xnp[sl][c][:, :], c == 0, c == NC - 1,
                                     waits=w0, inc="pe" if c == NC - 1 else None)
                        bank[bname]["w"] = evk
                        ekraw = dv(lambda e, b=psk: e.tensor_copy(
                            out=kraw[:, :], in_=b[:, 0:TT]),
                            waits=[evk, ev_kraw_free])
                        esq = dv(lambda e: e.tensor_tensor(out=sqt[:, :],
                                                           in0=kraw[:, :],
                                                           in1=kraw[:, :],
                                                           op=ALU.mult),
                                 waits=[ekraw, ev_sq_free])
                        yield
                        evn = mm(psk[0:2, 0:TT], bones2[:, 0:2], sqt[:, :], True,
                                 True, waits=[esq, ekraw], inc="pe")
                        bank[bname]["w"] = evn
                        elr = ac(lambda e, b=psk: e.activation(
                            out=ln1[:, :], in_=b[0:2, 0:TT], func=AF.Ln,
                            bias=eps6[:, 0:1], scale=1.0 / HD), waits=[evn])
                        err = ac(lambda e: e.activation(out=rstdf[:, :],
                                                        in_=ln1[:, :],
                                                        func=AF.Exp, scale=-0.5),
                                 waits=[elr])
                        erb = dv(lambda e: e.tensor_copy(out=rstdb[:, :],
                                                         in_=rstdf[:, :]),
                                 waits=[err])
                        yield
                        evrb = mm(psk[:, 0:TT], bc2[:, :], rstdb[:, :], True, True,
                                  waits=[erb, elr], inc="pe")
                        bank[bname]["w"] = evrb
                        eu = dv(lambda e, s=scv: e.tensor_scalar(
                            out=usc[:, :], in0=kraw[:, :], scalar1=s[:, 0:1],
                            scalar2=None, op0=ALU.mult),
                            waits=[ekraw, ev_usc_free, ev_qse])
                        ev_kraw_free = eu
                        ec1 = dv(lambda e: e.tensor_copy(out=urot[0:32, :],
                                                         in_=usc[32:64, :]),
                                 waits=[eu, ev_urot_free])
                        ec2 = dv(lambda e: e.tensor_copy(out=urot[32:64, :],
                                                         in_=usc[0:32, :]),
                                 waits=[eu])
                        ec3 = dv(lambda e: e.tensor_copy(out=urot[64:96, :],
                                                         in_=usc[96:128, :]),
                                 waits=[eu])
                        ec4 = dv(lambda e: e.tensor_copy(out=urot[96:128, :],
                                                         in_=usc[64:96, :]),
                                 waits=[eu])
                        et1 = dv(lambda e, sl=sl: e.tensor_tensor(
                            out=t1r[:, :], in0=usc[:, :], in1=ropec[sl][:, :],
                            op=ALU.mult), waits=[eu, er1, ev_t1_free])
                        et2 = dv(lambda e, sl=sl: e.tensor_tensor(
                            out=urot[:, :], in0=urot[:, :], in1=ropes[sl][:, :],
                            op=ALU.mult), waits=[ec1, ec2, ec3, ec4, er2])
                        ets = dv(lambda e: e.tensor_tensor(out=tsr[:, :],
                                                           in0=t1r[:, :],
                                                           in1=urot[:, :],
                                                           op=ALU.add),
                                 waits=[et1, et2, ev_tsr_free])
                        ev_rope_rd = ets
                        ev_usc_free = ets
                        ev_urot_free = ets
                        ev_t1_free = ets
                        ekf = dv(lambda e, b=psk: e.tensor_tensor(
                            out=kqst[:, :], in0=tsr[:, :], in1=b[:, 0:TT],
                            op=ALU.mult), waits=[ets, evrb, ev_kq_free])
                        ev_tsr_free = ekf
                        esq2 = dv(lambda e: e.tensor_tensor(out=sqt[:, :],
                                                            in0=kqst[:, :],
                                                            in1=kqst[:, :],
                                                            op=ALU.mult),
                                  waits=[ekf, evn])
                        ev_sq_free = esq2
                        yield
                        evn2 = mm(psk[0:2, 0:TT], bones2[:, 0:2], sqt[:, :], True,
                                  True, waits=[esq2, ekf], inc="pe")
                        bank[bname]["w"] = evn2
                        evre = dv(lambda e, b=psk: e.reduce_max(
                            out=redm[:, 0:1], in_=b[0:2, 0:TT], axis=AXX),
                            waits=[evn2])
                        bank[bname]["r"] = evre
                        mx = kmax if kind == "k" else qmax
                        evmx = dv(lambda e, m=mx, p=p: e.tensor_tensor(
                            out=m[:, p:p + 1], in0=m[:, p:p + 1],
                            in1=redm[:, 0:1], op=ALU.max), waits=[evre])
                        if kind == "k":
                            ev_maxk = evmx
                        else:
                            ev_maxq = evmx
                        ev_kq_free = dma("gp", "gkq", dstt[:, t0:t0 + TT],
                                         kqst[:, :], waits=[ekf])
                        yield
                    psk = ps_a
                    bname = "a"
                    jc += 1
                    mm(psk[:, 0:512], zeros[:, 0:128], zeros[:, 0:512], True, False,
                       waits=[evx, ev_wqp[ps_], bank[bname]["r"]])
                    for tc in range(TT // 128):
                        for c in range(NC):
                            mm(psk[:, tc * 128:(tc + 1) * 128],
                               xnp[sl][c][:, tc * 128:(tc + 1) * 128],
                               wqp[ps_][c][:, 256:384],
                               False, False)
                    evvc = mm(psk[:, 0:1], zeros[:, 0:128], zeros[:, 0:1], False,
                              True, inc="pe")
                    bank[bname]["w"] = evvc
                    evve = dv(lambda e, b=psk: e.tensor_copy(
                        out=vst[:, :], in_=b[:, 0:TT]),
                        waits=[evvc, ev_v_free])
                    bank[bname]["r"] = evve
                    ev_v_free = dma("gp", "gv", v_scr[p][:, t0:t0 + TT],
                                    vst[:, :], waits=[evve])
                    ev_xnp_free[sl] = evvc
                    ev_rope_free[sl] = ev_rope_rd
                    yield
                ev_wqp_free[ps_] = evvc
                ebp = dv(lambda e, p=p: e.tensor_tensor(
                    out=kmax[:, p:p + 1], in0=kmax[:, p:p + 1],
                    in1=qmax[:, p:p + 1], op=ALU.mult), waits=[ev_maxk, ev_maxq])
                ef1 = dma("gp", "gcf", cflip[p:p + 1, :], kmax[0:2, p:p + 1],
                          waits=[ebp])
                ef2 = dma("gp", "gcf", crow[:, 2 * p:2 * p + 2], cflip[p:p + 1, :],
                          waits=[ef1])
                em1 = dv(lambda e, p=p: e.reduce_max(
                    out=crow2[0:1, p:p + 1], in_=crow[0:1, 2 * p:2 * p + 2],
                    axis=AXX), waits=[ef2])
                ci = crow2.bitcast(I32)
                em2a = dv(lambda e, p=p: e.tensor_scalar(
                    out=ci[0:1, p:p + 1], in0=ci[0:1, p:p + 1], scalar1=1,
                    scalar2=None, op0=ALU.arith_shift_right), waits=[em1])
                em2_ = dv(lambda e, p=p: e.tensor_scalar(
                    out=ci[0:1, p:p + 1], in0=ci[0:1, p:p + 1],
                    scalar1=0x1FC00000, scalar2=None, op0=ALU.add),
                    waits=[em2a])
                em4 = dv(lambda e, p=p: e.tensor_scalar(
                    out=crow2[0:1, p:p + 1], in0=crow2[0:1, p:p + 1],
                    scalar1=-1.04, scalar2=72.0, op0=ALU.mult, op1=ALU.add),
                    waits=[em2_])
                psk = ps_a
                bname = "a"
                jc += 1
                ecb = mm(psk[:, 0:1], onesf[0:1, 0:128], crow2[0:1, p:p + 1],
                         True, True, waits=[em4, bank[bname]["r"]], inc="pe")
                bank[bname]["w"] = ecb
                ev_cbp = dv(lambda e, b=psk, p=p: e.tensor_copy(
                    out=cbias[:, p:p + 1], in_=b[:, 0:1]), waits=[ecb])
                bank[bname]["r"] = ev_cbp
                ev_pair[p] = [ev_kq_free, ev_v_free, ev_cbp]
                yield

        gen_prep = prep_pairs()
        ev_xf = [None, None]
        ev_xf_free = [[], []]
        ev_xn_rd = None
        ev_t1r_free = None
        ev_con2 = ev_con
        for tt in range(NTT):
            sl = tt % 2
            t0 = tt * TT
            if tt % 2 == 0:
                pa_, pb_, na_, nb_ = ps_a, ps_b, "a", "b"
            else:
                pa_, pb_, na_, nb_ = SBANK["s0"], SBANK["s1"], "s0", "s1"
            wx = [e for e in ev_xf_free[sl] if e]
            for c in range(NC):
                ev_xf[sl] = dma("sy", "sxf%d" % sl, xf[sl][c][:, :],
                                x_fm[c * 128:(c + 1) * 128, t0:t0 + TT],
                                waits=wx if c == 0 else ())
            ev_xf_free[sl] = []
            evxb = None
            for c in range(NC):
                wts = [ev_xf[sl]] + ([ev_xn_rd] if (c == 0 and ev_xn_rd) else [])
                evxb = dv(lambda e, c=c, sl=sl: e.tensor_copy(
                    out=xb[:, c * TT:(c + 1) * TT], in_=xf[sl][c][:, :]), waits=wts)
            evsq = dv(lambda e: e.tensor_tensor(out=xsq[:, :], in0=xb[:, :],
                                                in1=xb[:, :], op=ALU.mult),
                      waits=[evxb])
            evA = evB = None
            for c in range(NC):
                wa = ([evxb, bank[na_]["r"]]) if c == 0 else ()
                wb = ([evsq, bank[nb_]["r"]]) if c == 0 else ()
                evA = mm(pa_[0:1, 0:TT], ones1b[:, 0:1], xb[:, c * TT:(c + 1) * TT],
                         c == 0, c == NC - 1, waits=wa,
                         inc="pe" if c == NC - 1 else None)
                evB = mm(pb_[0:1, 0:TT], ones1b[:, 0:1], xsq[:, c * TT:(c + 1) * TT],
                         c == 0, c == NC - 1, waits=wb,
                         inc="pe" if c == NC - 1 else None)
            bank[na_]["w"] = evA
            bank[nb_]["w"] = evB
            em = dv(lambda e, pa_=pa_: e.tensor_scalar(out=m_row[:, :],
                                              in0=pa_[0:1, 0:TT],
                                              scalar1=1.0 / D, scalar2=None,
                                              op0=ALU.mult), waits=[evA])
            em2 = dv(lambda e: e.tensor_tensor(out=msq[:, :], in0=m_row[:, :],
                                               in1=m_row[:, :], op=ALU.mult),
                     waits=[em])
            evr = dv(lambda e, pb_=pb_: e.scalar_tensor_tensor(out=v_row[:, :],
                                                      in0=pb_[0:1, 0:TT],
                                                      scalar=1.0 / D, in1=msq[:, :],
                                                      op0=ALU.mult,
                                                      op1=ALU.subtract),
                     waits=[evB, em2])
            bank[na_]["r"] = em
            bank[nb_]["r"] = evr
            el1 = ac(lambda e: e.activation(out=lnv[:, :], in_=v_row[:, :],
                                            func=AF.Ln, bias=eps5[:, 0:1],
                                            scale=1.0), waits=[evr, ev_con2])
            ea = ac(lambda e: e.activation(out=a_row[:, :], in_=lnv[:, :],
                                           func=AF.Exp, scale=-0.5), waits=[el1])
            eb = dv(lambda e: e.scalar_tensor_tensor(out=b_row[:, :], in0=m_row[:, :],
                                                     scalar=-1.0, in1=a_row[:, :],
                                                     op0=ALU.mult, op1=ALU.mult),
                    waits=[ea])
            eba = mm(pa_[:, 0:TT], onesf[0:1, 0:128], a_row[:, :], True, True,
                     waits=[ea, bank[na_]["r"]], inc="pe")
            ebb = mm(pb_[:, 0:TT], onesf[0:1, 0:128], b_row[:, :], True, True,
                     waits=[eb, bank[nb_]["r"]], inc="pe")
            bank[na_]["w"] = eba
            bank[nb_]["w"] = ebb
            exn = None
            ev_sp = None
            for c in range(NC):
                wts = [eba, ev_t1r_free] if c == 0 else [eba, exn]
                et = dv(lambda e, c=c, sl=sl, pa_=pa_: e.tensor_tensor(
                    out=lnt[:, :], in0=xf[sl][c][:, :], in1=pa_[:, 0:TT],
                    op=ALU.mult),
                    waits=wts + ([ev_xn_rd] if (c == 0 and ev_xn_rd) else []))
                exn = dv(lambda e, c=c, pb_=pb_: e.tensor_tensor(
                    out=xnt[:, c * TT:(c + 1) * TT], in0=lnt[:, :],
                    in1=pb_[:, 0:TT], op=ALU.add), waits=[ebb, et])
                ev_sp = dma("gp", "gxn", xn_scr[c * 128:(c + 1) * 128, t0:t0 + TT],
                            xnt[:, c * TT:(c + 1) * TT], waits=[exn])
            ev_t1r_free = exn
            bank[na_]["r"] = exn
            bank[nb_]["r"] = exn
            ev_xn_rd = ev_sp
            ev_xf_free[sl].append(exn)
            ev_xnspill[tt] = ev_sp
            if tt >= 2:
                for _ in range(26):
                    next(gen_prep)
        bar_mid = P.barrier()

        TQ = float(Q127 / 2.0)

        def ffn_units():
            ev_xnf = [None, None]
            ev_xnf_free = [list(bar_mid), list(bar_mid)]
            ev_wg_free = [None, None]
            ev_wu_free = [None, None]
            ev_h = [None] * NHC
            ev_sg_free = None
            ev_sg1_free = None
            ev_gst_free = None
            for ft in range(NFT):
                f0 = ft * 512
                sl = ft % 2
                evs = None
                for c in range(NC):
                    w0 = ()
                    if c == 0 and ev_xnf_free[sl]:
                        w0 = (ev_xnf_free[sl] if isinstance(ev_xnf_free[sl], list)
                              else [ev_xnf_free[sl]])
                    evs = dma("sy", "sxn%d" % sl, xnf[sl][c][:, :],
                              xn_scr[c * 128:(c + 1) * 128, f0:f0 + 512], waits=w0)
                ev_xnf[sl] = evs
                yield
                for hh in range(NHC):
                    ws = hh % 2
                    eg = dma("sy", "swg%d" % ws, wgb[ws][:, :], wg_bf[hh],
                             waits=[ev_wg_free[ws], ev_deq[2 * hh]])
                    eu_ = dma("sy", "swu%d" % ws, wub[ws][:, :], wu_bf[hh],
                              waits=[ev_wu_free[ws], ev_deq[2 * hh + 1]])
                    yield
                    evg = evu = None
                    for c in range(NC):
                        wg_ = [eg, ev_xnf[sl], bank["b"]["r"]] if c == 0 else ()
                        evg = mm(ps_b[:, :], wgb[ws][:, c * 128:(c + 1) * 128],
                                 xnf[sl][c][:, :], c == 0, c == NC - 1, waits=wg_,
                                 inc="pe" if c == NC - 1 else None)
                    bank["b"]["w"] = evg
                    ev_wg_free[ws] = evg
                    ee0 = dv(lambda e: e.tensor_scalar(
                        out=sg1[:, :], in0=ps_b[:, :], scalar1=-Q127, scalar2=43.0,
                        op0=ALU.mult, op1=ALU.min), waits=[evg, ev_sg1_free])
                    ee1 = ac(lambda e: e.activation(out=sgate[:, :], in_=sg1[:, :],
                                                    func=AF.Exp),
                             waits=[ee0, ev_sg_free])
                    ee2 = ac(lambda e: e.activation(out=sg1[:, :], in_=sgate[:, :],
                                                    func=AF.Ln, bias=onebias[:, 0:1],
                                                    scale=1.0),
                             waits=[ee1])
                    ee3 = ac(lambda e: e.activation(out=sgate[:, :], in_=sg1[:, :],
                                                    func=AF.Exp, scale=-1.0),
                             waits=[ee2])
                    et0 = dv(lambda e: e.scalar_tensor_tensor(
                        out=sg1[:, :], in0=ps_b[:, :], scalar=Q127 * Q127,
                        in1=sgate[:, :], op0=ALU.mult, op1=ALU.mult),
                        waits=[ee3, evg])
                    ev_sg1_free = et0
                    bank["b"]["r"] = et0
                    yield
                    for c in range(NC):
                        wu2 = [eu_, ev_xnf[sl], et0] if c == 0 else ()
                        evu = mm(ps_b[:, :], wub[ws][:, c * 128:(c + 1) * 128],
                                 xnf[sl][c][:, :], c == 0, c == NC - 1, waits=wu2,
                                 inc="pe" if c == NC - 1 else None)
                    bank["b"]["w"] = evu
                    ev_wu_free[ws] = evu
                    evh = dv(lambda e, hh=hh: e.tensor_tensor(
                        out=hbuf[hh][:, :], in0=ps_b[:, :], in1=sg1[:, :],
                        op=ALU.mult), waits=[et0, evu, ev_h[hh]])
                    ev_sg_free = evh
                    ev_sg1_free = evh
                    ev_h[hh] = evh
                    bank["b"]["r"] = evh
                    yield
                ev_wd_free = [ev_wg_free[0], ev_wg_free[1], ev_wu_free[0]]
                evd_last = None
                for dd in range(NC):
                    psd = ps_b
                    bname = "b"
                    wdl = []
                    for seg in range(3):
                        buf = (wgb[0], wgb[1], wub[0])[seg]
                        wsem = ("swg0", "swg1", "swu0")[seg]
                        wdl.append((buf, dma("sy", wsem, buf[:, :],
                                             wd_bf[dd][:, seg * 1024:(seg + 1) * 1024],
                                             waits=[ev_wd_free[seg],
                                                    ev_deq[2 * NHC + dd]])))
                    yield
                    evd = None
                    for hh in range(NHC):
                        seg, off = divmod(hh, 8)
                        w0 = ([wdl[0][1], wdl[1][1], wdl[2][1], ev_h[NHC - 1],
                               bank[bname]["r"]] if hh == 0 else ())
                        evd = mm(psd[:, :], wdl[seg][0][:, off * 128:(off + 1) * 128],
                                 hbuf[hh][:, :], hh == 0, hh == NHC - 1, waits=w0,
                                 inc="pe" if hh == NHC - 1 else None)
                    ev_wd_free = [evd, evd, evd]
                    bank[bname]["w"] = evd
                    evd_last = evd
                    ef0 = dv(lambda e, psd=psd: e.tensor_scalar(
                        out=sg1[:, :], in0=psd[:, :], scalar1=-Q127, scalar2=43.0,
                        op0=ALU.mult, op1=ALU.min), waits=[evd, ev_sg1_free])
                    ef1 = ac(lambda e: e.activation(
                        out=sgate[:, :], in_=sg1[:, :], func=AF.Exp),
                        waits=[ef0, ev_sg_free])
                    ef2 = ac(lambda e: e.activation(out=sg1[:, :], in_=sgate[:, :],
                                                    func=AF.Ln, bias=onebias[:, 0:1],
                                                    scale=1.0),
                             waits=[ef1])
                    es2 = ac(lambda e: e.activation(out=sgate[:, :], in_=sg1[:, :],
                                                    func=AF.Exp, scale=-1.0),
                             waits=[ef2])
                    ev_sg1_free = es2
                    evgb = dv(lambda e, psd=psd: e.scalar_tensor_tensor(
                        out=gst[:, :], in0=psd[:, :], scalar=Q127,
                        in1=sgate[:, :], op0=ALU.mult, op1=ALU.mult),
                        waits=[es2, ev_gst_free])
                    ev_sg_free = evgb
                    ev_sg1_free = evgb
                    bank[bname]["r"] = evgb
                    ev_gst_free = dma("gp", "gg", g_scr[dd][:, f0:f0 + 512],
                                      gst[:, :], waits=[evgb])
                    yield

                for hh in range(NHC):
                    ev_h[hh] = evd_last
                ev_xnf_free[sl] = evd_last
                ev_wg_free = [evd_last, evd_last]
                ev_wu_free = [evd_last, evd_last]

        gen_ffn = ffn_units()
        while ev_pair[0] is None:
            next(gen_prep)
        npump = [0]
        gens = [gen_prep, gen_ffn]

        def pump():
            npump[0] += 1
            r = npump[0] % 4
            if r == 1:
                g, h = gens[0], gens[1]
            elif r == 3:
                g, h = gens[1], gens[0]
            else:
                return
            try:
                next(g)
            except StopIteration:
                try:
                    next(h)
                except StopIteration:
                    pass
        ev_kvqf = None
        ev_pfree = [None, None, None]
        ev_ost_free = None
        ev_zt_free = None
        ev_zbs_free = None
        ev_oraw_free = [None, None]
        st = {"zbs_free": None, "ost_free": None, "oraw_free": [None, None],
              "zt_free": None}

        def pr6(i):
            j = i % 6
            return pring[j // 2][:, (j % 2) * 1024:((j % 2) + 1) * 1024]

        for p in range(NP):
            sl = p % 2
            while ev_pair[p] is None:
                next(gen_prep)
            wld = ev_pair[p] + ([ev_kvqf] if ev_kvqf else [])
            ek = dma("sy", "skv%d" % sl, kb[:, 0:NTOT], kt_scr[p], waits=wld)
            eq = dma("sy", "skv%d" % sl, qb[:, 0:NOWN], q_scr[p])
            evv = dma("sy", "skv%d" % sl, vb[:, 0:NTOT], v_scr[p])
            ek = eq = evv
            ev_sexp = [None, None]
            pfree = [[] for _ in range(6)]
            pend_flush = [None]
            for qt in range(NQT):
                q0 = qt * QT
                mm(ps_pv[:, 0:QT], zeros[:, 0:128], zeros[:, 0:QT], True, False,
                   waits=[ek, bank["pv"]["r"]])
                ev_exp = [None] * NCH

                def emit_pv(c):
                    slot = pr6(c)
                    mm(ps_pv[0:64, 0:QT], vb[:, c * 128:c * 128 + 64],
                       slot[:, 0:QT], False, False, tp=(0, 0), waits=[ev_exp[c]])
                    ev = mm(ps_pv[64:128, 0:QT], vb[:, c * 128 + 64:(c + 1) * 128],
                            slot[:, 512:512 + QT], False, False, tp=(0, 64),
                            inc="pe")
                    pfree[c % 6].append(ev)

                def emit_z(c0):
                    sa, sb_ = pr6(c0), pr6(c0 + 1)
                    mm(ps_z[0:1, 0:QT], onesb[:, 0:1], sa[:, 0:QT], False, False,
                       tp=(0, 0), waits=[ev_exp[c0 + 1]])
                    mm(ps_z[32:33, 0:QT], onesb[:, 1:2], sa[:, 512:512 + QT],
                       False, False, tp=(0, 32))
                    mm(ps_z[64:65, 0:QT], onesb[:, 2:3], sb_[:, 0:QT],
                       False, False, tp=(0, 64))
                    ev = mm(ps_z[96:97, 0:QT], onesb[:, 3:4], sb_[:, 512:512 + QT],
                            False, False, tp=(0, 96), inc="pe")
                    pfree[c0 % 6].append(ev)
                    pfree[(c0 + 1) % 6].append(ev)

                for c in range(NCH):
                    hf = c % 2
                    so = hf * 1024
                    mm(ps_s[:, so:so + QT], kb[0:64, c * 128:(c + 1) * 128],
                       qb[0:64, q0:q0 + QT], True, True, tp=(0, 0),
                       waits=[ek, ev_sexp[hf]])
                    ev_s = mm(ps_s[:, so + 512:so + 512 + QT],
                              kb[64:128, c * 128:(c + 1) * 128],
                              qb[64:128, q0:q0 + QT], True, True, tp=(64, 0),
                              inc="pe")
                    ev_exp[c] = ac(lambda e, c=c, so=so, p=p: e.activation(
                        out=pr6(c), in_=ps_s[:, so:so + 1024], func=AF.Exp,
                        bias=cbias[:, p:p + 1], scale=1.0),
                        waits=[ev_s] + pfree[c % 6])
                    pfree[c % 6] = []
                    ev_sexp[hf] = ev_exp[c]
                    if c == 2:
                        if pend_flush[0] is not None:
                            pend_flush[0]()
                            pend_flush[0] = None
                        mm(ps_z[:, 0:QT], zeros[:, 0:128], zeros[:, 0:QT], True,
                           False, waits=[bank["z"]["r"]])
                    if c >= 2:
                        emit_pv(c - 2)
                    if c >= 3 and c % 2 == 1:
                        emit_z(c - 3)
                    pump()
                emit_pv(NCH - 2)
                emit_pv(NCH - 1)
                emit_z(NCH - 2)
                evc1 = mm(ps_pv[:, 0:1], zeros[:, 0:128], zeros[:, 0:1], False,
                          True, inc="pe")
                evc2 = mm(ps_z[:, 0:1], zeros[:, 0:128], zeros[:, 0:1], False,
                          True, inc="pe")
                bank["pv"]["w"] = evc1
                bank["z"]["w"] = evc2
                osl = qt % 2
                eor = dv(lambda e, osl=osl: e.tensor_copy(out=oraw[osl][:, :],
                                                          in_=ps_pv[:, 0:QT]),
                         waits=[evc1, st["oraw_free"][osl]])
                bank["pv"]["r"] = eor
                ez0 = dv(lambda e: e.tensor_copy(out=zt[:, :], in_=ps_z[:, 0:QT]),
                         waits=[evc2, st["zt_free"]])
                bank["z"]["r"] = ez0
                ec1 = dv(lambda e: e.tensor_copy(out=zc1[:, :], in_=zt[64:65, :]),
                         waits=[ez0])
                ec2 = dv(lambda e: e.tensor_copy(out=zc2[:, :], in_=zt[96:97, :]),
                         waits=[ez0])
                ec3 = dv(lambda e: e.tensor_copy(out=zc3[:, :], in_=zt[32:33, :]),
                         waits=[ez0])
                ea1 = dv(lambda e: e.tensor_tensor(out=zrow[:, 0:QT],
                                                   in0=zt[0:1, :], in1=zc1[:, :],
                                                   op=ALU.add), waits=[ec1])
                ea2 = dv(lambda e: e.tensor_tensor(out=zrow[:, QT:2 * QT],
                                                   in0=zc3[:, :], in1=zc2[:, :],
                                                   op=ALU.add), waits=[ec2, ec3])
                st["zt_free"] = ea2
                eg_ = dv(lambda e: e.tensor_scalar(out=zln[:, :], in0=zrow[:, :],
                                                   scalar1=1e-30, scalar2=None,
                                                   op0=ALU.add), waits=[ea1, ea2])
                ei = dv(lambda e: e.reciprocal(out=zi[:, :], in_=zln[:, :]),
                        waits=[eg_])
                eib = dv(lambda e: e.tensor_copy(out=zib[:, :], in_=zi[:, :]),
                         waits=[ei])

                def flush(p=p, qt=qt, q0=q0, eib=eib, osl=osl, eor=eor):
                    mm(ps_z[:, 0:QT], zeros[:, 0:128], zeros[:, 0:QT], True, False,
                       waits=[eib, bank["z"]["r"]])
                    mm(ps_z[0:64, 0:QT], onesfh[0:1, 0:64], zib[:, 0:QT],
                       False, False, tp=(0, 0))
                    mm(ps_z[64:128, 0:QT], onesfh[0:1, 0:64], zib[:, QT:2 * QT],
                       False, False, tp=(0, 64))
                    ezb = mm(ps_z[:, 0:1], zeros[:, 0:128], zeros[:, 0:1], False,
                             True, inc="pe")
                    bank["z"]["w"] = ezb
                    ezbs = dv(lambda e: e.tensor_copy(out=zbs[:, :],
                                                      in_=ps_z[:, 0:QT]),
                              waits=[ezb, st["zbs_free"]])
                    bank["z"]["r"] = ezbs
                    eo = dv(lambda e, osl=osl: e.tensor_tensor(
                        out=ost[:, :], in0=oraw[osl][:, :], in1=zbs[:, :],
                        op=ALU.mult), waits=[eor, ezbs, st["ost_free"]])
                    st["zbs_free"] = eo
                    st["oraw_free"][osl] = eo
                    st["ost_free"] = dma("gp", "go", o_scr[p][:, q0:q0 + QT],
                                         ost[:, :], waits=[eo])

                pend_flush[0] = flush
            pend_flush[0]()
            pend_flush[0] = None
            ev_kvqf = evc1

        for g in gens:
            for _ in g:
                pass

        bar_pr = P.barrier()
        ev_ob = []
        for dd in range(NC):
            ev_ob.append(dma("sy", "sob", ob[dd][:, :], o_scr[dd],
                             waits=bar_pr if dd == 0 else ()))
        ev_c1 = ev_c2 = None
        for dd in range(NC):
            e1_ = dma("sy", "swp0", xcb[0][:, :],
                      w_proj[dd * 128:(dd + 1) * 128, 0:512],
                      waits=[ev_c1] if ev_c1 else bar_pr)
            e2_ = dma("sy", "swp1", xcb[1][:, :],
                      w_proj[dd * 128:(dd + 1) * 128, 512:1024],
                      waits=[ev_c2] if ev_c2 else bar_pr)
            ev_c1 = dv(lambda e, dd=dd: e.tensor_copy(out=wpb[dd][:, 0:512],
                                                      in_=xcb[0][:, :]), waits=[e1_])
            ev_c2 = dv(lambda e, dd=dd: e.tensor_copy(out=wpb[dd][:, 512:1024],
                                                      in_=xcb[1][:, :]), waits=[e2_])
        ev_wp = ev_c2
        prb = ["s0", "s1", "s2", "s3"]
        bi = 0
        ev_gc = [None, None]
        ev_xc = [ev_c1, ev_c2]
        ev_otf = [None, None]
        ev_sg = None
        for oo in range(NC):
            ev_sfm = None
            for qt in range(NQT):
                bnk = prb[bi % 4]
                bi += 1
                evm = None
                for dd in range(NC):
                    w0 = ([ev_wp, ev_c1, ev_ob[-1], bank[bnk]["r"]]
                          if dd == 0 else ())
                    evm = mm(SBANK[bnk][:, 0:QT], wpb[dd][:, oo * 128:(oo + 1) * 128],
                             ob[dd][:, qt * QT:(qt + 1) * QT], dd == 0, dd == NC - 1,
                             waits=w0, inc="pe" if dd == NC - 1 else None)
                bank[bnk]["w"] = evm
                ev_sfm = dv(lambda e, b=bnk, oo=oo, qt=qt: e.tensor_scalar(
                    out=sfm[oo][:, qt * QT:(qt + 1) * QT], in0=SBANK[b][:, 0:QT],
                    scalar1=bpt[:, oo:oo + 1], scalar2=None, op0=ALU.add),
                    waits=[evm, e3])
                bank[bnk]["r"] = ev_sfm
            for ft in range(NFT):
                f0 = ft * 512
                slot = (oo * NFT + ft) % 2
                eg = dma("sy", "sge%d" % slot, gcb[slot][:, :],
                         g_scr[oo][:, f0:f0 + 512], waits=[ev_gc[slot]])
                ex = dma("sy", "sxe%d" % slot, xcb[slot][:, :],
                         x_fm[oo * 128:(oo + 1) * 128, f0:f0 + 512],
                         waits=[ev_xc[slot]])
                esg2 = dv(lambda e, oo=oo, f0=f0, slot=slot: e.tensor_tensor(
                    out=sgc[:, :], in0=sfm[oo][:, f0:f0 + 512], in1=gcb[slot][:, :],
                    op=ALU.mult), waits=[eg, ev_sfm, ev_sg])
                ev_gc[slot] = esg2
                eot = dv(lambda e, slot=slot: e.tensor_tensor(
                    out=otc[slot][:, :], in0=sgc[:, :], in1=xcb[slot][:, :],
                    op=ALU.add), waits=[esg2, ex, ev_otf[slot]])
                ev_sg = eot
                ev_xc[slot] = eot
                ev_otf[slot] = dma("sy", "sout%d" % slot,
                                   out_fm[oo * 128:(oo + 1) * 128, f0:f0 + 512],
                                   otc[slot][:, :], waits=[eot])

        P.emit("gp", lambda e: None,
               waits=[("sout0", P.cnt["sout0"]), ("sout1", P.cnt["sout1"])])

        block = E(nc.Block())

        @block.sync
        def _(sy):
            for f in P.cmds["sy"]:
                f(sy)

        @block.gpsimd
        def _(gp):
            for f in P.cmds["gp"]:
                f(gp)

        @block.tensor
        def _(te):
            for f in P.cmds["pe"]:
                f(te)

        @block.vector
        def _(ve):
            for f in P.cmds["dv"]:
                f(ve)

        @block.scalar
        def _(sc):
            for f in P.cmds["ac"]:
                f(sc)

    return nc


def _bc2_const():
    import ml_dtypes
    m = np.zeros((2, 128), dtype=ml_dtypes.bfloat16)
    m[0, 0:64] = 1
    m[1, 64:128] = 1
    return m


def _rope_tables(npos):
    half = 32
    inv = (np.float32(1.0) /
           np.power(np.float32(10000.0),
                    np.arange(half, dtype=np.float32) / np.float32(half)))
    t = np.arange(npos, dtype=np.float32)
    f = np.outer(t, inv).astype(np.float32)
    cos = np.cos(f).T.astype(np.float32)
    sin = np.sin(f).T.astype(np.float32)
    cos2x = np.ascontiguousarray(np.tile(cos, (4, 1)))
    sinS = np.ascontiguousarray(np.concatenate([-sin, sin, -sin, sin], axis=0))
    return cos2x, sinS


def kernel(L0_attn_x, L0_attn_w_qkv, L0_attn_q_scale, L0_attn_k_scale,
           L0_attn_w_proj, L0_attn_b_proj, L0_ffn_w_gate, L0_ffn_w_up,
           L0_ffn_w_down):
    x = np.asarray(L0_attn_x)
    B, N, C = x.shape
    NOWN = N // 2
    cos2x, sinS = _rope_tables(N)
    w_qkv = np.ascontiguousarray(L0_attn_w_qkv, dtype=np.float32)
    w_proj = np.ascontiguousarray(L0_attn_w_proj, dtype=np.float32)
    b_projt = np.ascontiguousarray(
        np.asarray(L0_attn_b_proj, dtype=np.float32).reshape(NC, 128).T)
    qs2x = np.ascontiguousarray(
        np.concatenate([L0_attn_q_scale, L0_attn_q_scale]).astype(np.float32)
        .reshape(128, 1))
    ks2x = np.ascontiguousarray(
        np.concatenate([L0_attn_k_scale, L0_attn_k_scale]).astype(np.float32)
        .reshape(128, 1))
    wgT = np.ascontiguousarray(np.asarray(L0_ffn_w_gate, dtype=np.int32).T)
    wuT = np.ascontiguousarray(np.asarray(L0_ffn_w_up, dtype=np.int32).T)
    wdT = np.ascontiguousarray(np.asarray(L0_ffn_w_down, dtype=np.int32).T)

    nc = build_program(NTOT=N, NOWN=NOWN)

    in_maps = []
    for core in range(8):
        b, h = core // 2, core % 2
        xt = np.ascontiguousarray(x[b].T.astype(np.float32))
        if h == 0:
            x_c, rc, rs = xt, cos2x, sinS
        else:
            x_c = np.ascontiguousarray(
                np.concatenate([xt[:, NOWN:], xt[:, :NOWN]], axis=1))
            rc = np.ascontiguousarray(
                np.concatenate([cos2x[:, NOWN:], cos2x[:, :NOWN]], axis=1))
            rs = np.ascontiguousarray(
                np.concatenate([sinS[:, NOWN:], sinS[:, :NOWN]], axis=1))
        in_maps.append(dict(x_fm=x_c, w_qkv=w_qkv, w_proj=w_proj,
                            b_projt=b_projt, qs2x=qs2x, ks2x=ks2x,
                            wgT=wgT, wuT=wuT, wdT=wdT, ropec=rc, ropes=rs,
                            bc2c=_bc2_const()))

    res = run_bass_kernel_spmd(nc, in_maps, core_ids=list(range(8)))
    out = np.empty((B, N, C), dtype=np.float32)
    for core in range(8):
        b, h = core // 2, core % 2
        out[b, h * NOWN:(h + 1) * NOWN, :] = np.asarray(
            res.results[core]["out_fm"]).T
    return out
